# version 20
# baseline (speedup 1.0000x reference)
import numpy as np
from contextlib import ExitStack

import concourse.bass as bass
import concourse.mybir as mybir
from concourse.bass_utils import run_bass_kernel_spmd

F32 = mybir.dt.float32
BF16 = mybir.dt.bfloat16
AF = mybir.ActivationFunctionType
ALU = mybir.AluOpType
AXL = mybir.AxisListType

B, S, D = 2, 8192, 1024
NCORES = 8
TLOC = 2048
NCH = D // 128
LN_EPS = 1e-5
NHEADS = 16
HDIM = 64
FMOE = 4096
NEXP = 8
GROUPS = [[0, 1, 2, 3], [4, 5, 6, 7]]
ALL8 = [list(range(8))]


class Sem:
    __slots__ = ("h", "hid", "base", "n")
    def __init__(self, handle, hid, base):
        self.h = handle
        self.hid = hid
        self.base = base
        self.n = 0
    def inc(self, instr, k=1):
        instr.then_inc(self.h, k)
        self.n += k
        return self.n


class Ctx:
    def __init__(self):
        self.nc = bass.Bass("TRN2", target_bir_lowering=False, debug=False)
        self.es = ExitStack()
        self._ns = 0
        self._waited = {}

    SEM_POOL = 72

    def sem(self, name=None):
        slot = self._ns % self.SEM_POOL
        self._ns += 1
        if not hasattr(self, "_pool"):
            self._pool = {}
            self._last = {}
        if slot not in self._pool:
            self._pool[slot] = self.es.enter_context(
                self.nc.semaphore(f"sp{slot}"))
            base = 0
        else:
            prev = self._last[slot]
            base = prev.base + prev.n
        s = Sem(self._pool[slot], slot, base)
        self._last[slot] = s
        return s

    def sbuf(self, name, shape, dtype):
        return self.es.enter_context(self.nc.sbuf_tensor(name, shape, dtype))

    def psum(self, name, shape, dtype=F32):
        return self.es.enter_context(self.nc.psum_tensor(name, shape, dtype))

    def dram(self, name, shape, dtype, **kw):
        return self.nc.dram_tensor(name, shape, dtype, **kw)

    def block(self):
        return self.nc.Block()

    def wait(self, eng, sem, val=None):
        if val is None:
            val = sem.n
        if val <= 0:
            return
        a = sem.base + val
        key = (eng.engine, sem.hid)
        if self._waited.get(key, -1) >= a:
            return
        eng.wait_ge(sem.h, a)
        self._waited[key] = a

    def arena_init(self, words=24576, ps_words=4096):
        self.AR = self.sbuf("ARENA", [128, words], F32)
        self.PS = self.psum("PSARENA", [128, ps_words], F32)

    def at(self, off_kb, cols, part=128):
        w0 = off_kb * 256
        return self.AR[0:part, w0:w0 + cols]

    def atb(self, off_kb, cols):
        w0 = off_kb * 256
        return self.AR[:, w0:w0 + cols // 2].bitcast(BF16)

    def ps(self, bank, cols=512, part=128):
        return self.PS[0:part, bank * 512:bank * 512 + cols]

    def close(self):
        self.es.close()
        return self.nc


class NS:
    pass



def ph_load_x(ctx, x_dram, x_sb, dsem):
    with ctx.block() as blk:
        @blk.sync
        def _(y):
            for c in range(NCH):
                dsem.inc(y.dma_start(out=x_sb[:, c * TLOC:(c + 1) * TLOC],
                                     in_=x_dram.ap()[c * 128:(c + 1) * 128, :]), 16)


def ph_store_x(ctx, x_sb, xo_dram, deps):
    dsem = ctx.sem()
    with ctx.block() as blk:
        @blk.sync
        def _(y):
            for s in deps:
                ctx.wait(y, s)
            for c in range(NCH):
                dsem.inc(y.dma_start(out=xo_dram.ap()[c * 128:(c + 1) * 128, :],
                                     in_=x_sb[:, c * TLOC:(c + 1) * TLOC]), 16)
            ctx.wait(y, dsem)
    return dsem



def alloc_ln_scratch(ctx, nt=TLOC):
    sc = NS()
    sc.sq_bf = ctx.atb(88, NCH * 512)
    rows = ctx.sbuf("ln_rows", [33, nt], F32)
    sc.murow = rows[0:1, 0:nt]
    sc.rowA = rows[32:33, 0:nt]
    sc.rowB = ctx.AR[32:33, 84 * 256:84 * 256 + nt]
    sc.ones_col_bf = ctx.sbuf("ones_col_bf", [128, 1], BF16)
    ones_rows = ctx.sbuf("ones_rows", [65, 128], F32)
    sc.ones_rows = ones_rows
    sc.ones_row_f = ones_rows[0:1, :]
    sc.ones_row32 = ones_rows[32:33, :]
    sc.tmp512 = ctx.at(86, 512)
    return sc


def ph_init_consts(ctx, sc):
    g = ctx.sem()
    with ctx.block() as blk:
        @blk.gpsimd
        def _(gg):
            gg.memset(sc.ones_col_bf[:], 1.0)
            g.inc(gg.memset(sc.ones_rows[:], 1.0))
    return g


def _ln_stats(ctx, blk, sc, src_getter, deps, src_is_bf, nt=TLOC):
    v = ctx.sem(); s = ctx.sem(); t = ctx.sem()
    ntb = nt // 512
    ps_sum = ctx.ps(0)
    ps_sq = ctx.ps(1)
    for tb in range(ntb):
        sl = slice(tb * 512, (tb + 1) * 512)
        if not src_is_bf:
            pass
        @blk.scalar
        def _(ss, tb=tb):
            for d in deps:
                ctx.wait(ss, d)
            if tb > 0:
                ctx.wait(ss, t, tb)
            for c in range(NCH):
                ins = ss.activation(sc.sq_bf[:, c * 512:(c + 1) * 512],
                                    src_getter(c, tb), AF.Square)
            s.inc(ins)
        @blk.tensor
        def _(tt, tb=tb):
            for d in deps:
                ctx.wait(tt, d)
            ctx.wait(tt, s, tb + 1)
            if tb > 0:
                ctx.wait(tt, v, 2 * tb)
            for c in range(NCH):
                tt.matmul(ps_sum[0:1, :], sc.ones_col_bf[:], _bf_src(ctx, sc, src_getter, c, tb),
                          start=(c == 0), stop=(c == NCH - 1))
            for c in range(NCH):
                ins = tt.matmul(ps_sq[0:1, :], sc.ones_col_bf[:],
                                sc.sq_bf[:, c * 512:(c + 1) * 512],
                                start=(c == 0), stop=(c == NCH - 1))
            t.inc(ins)
        @blk.vector
        def _(vv, tb=tb, sl=sl):
            ctx.wait(vv, t)
            v.inc(vv.tensor_scalar(sc.murow[:, sl], ps_sum[0:1, :],
                                   1.0 / D, 0.0, op0=ALU.mult, op1=ALU.add))
            v.inc(vv.tensor_scalar(sc.rowA[:, sl], ps_sq[0:1, :],
                                   1.0 / D, LN_EPS, op0=ALU.mult, op1=ALU.add))
    @blk.vector
    def _(vv):
        ctx.wait(vv, v)
        m = v.inc(vv.tensor_tensor(sc.rowB[:, :nt], sc.murow[:, :nt],
                                   sc.murow[:, :nt], op=ALU.mult))
        ctx.wait(vv, v, m)
        v.inc(vv.tensor_tensor(sc.rowA[:, :nt], sc.rowA[:, :nt],
                               sc.rowB[:, :nt], op=ALU.subtract))
    @blk.scalar
    def _(ss):
        ctx.wait(ss, v)
        m = s.inc(ss.activation(sc.rowB[:, :nt], sc.rowA[:, :nt], AF.Ln))
        ctx.wait(ss, s, m)
        s.inc(ss.activation(sc.rowA[:, :nt], sc.rowB[:, :nt], AF.Exp, scale=-0.5))
    return v, s, t


def _bf_src(ctx, sc, src_getter, c, tb):
    ap = src_getter(c, tb)
    return ap


def ph_ln(ctx, sc, x_sb, xn_bf, deps=(), nt=TLOC):
    ntb = nt // 512
    fin = ctx.sem()
    with ctx.block() as blk:
        @blk.vector
        def _(vv):
            for d in deps:
                ctx.wait(vv, d)
            vv.tensor_copy(xn_bf[:, :NCH * nt], x_sb[:, :NCH * nt])
    with ctx.block() as blk:
        v, s, t = _ln_stats(ctx, blk, sc,
                            lambda c, tb: xn_bf[:, c * nt + tb * 512:
                                                c * nt + (tb + 1) * 512],
                            deps=(), src_is_bf=True, nt=nt)
        bc1 = ctx.ps(2); bc2 = ctx.ps(3)
        for tb in range(ntb):
            sl = slice(tb * 512, (tb + 1) * 512)
            @blk.tensor
            def _(tt, sl=sl):
                ctx.wait(tt, s)
                ctx.wait(tt, v)
                tt.matmul(bc1[:], sc.ones_row_f[:], sc.murow[:, sl],
                          start=True, stop=True)
                t.inc(tt.matmul(bc2[:], sc.ones_row32[:], sc.rowA[:, sl],
                                start=True, stop=True))
            @blk.vector
            def _(vv, sl=sl, tb=tb):
                ctx.wait(vv, t)
                for c in range(NCH):
                    xsl = slice(c * nt + tb * 512, c * nt + (tb + 1) * 512)
                    ctx.wait(vv, v)
                    m = v.inc(vv.tensor_tensor(sc.tmp512[:], x_sb[:, xsl],
                                               bc1[:], op=ALU.subtract))
                    ctx.wait(vv, v, m)
                    m = v.inc(vv.tensor_tensor(xn_bf[:, xsl], sc.tmp512[:],
                                               bc2[:], op=ALU.mult))
        @blk.vector
        def _(vv):
            fin.inc(vv.tensor_copy(sc.tmp512[0:1, 0:1], sc.tmp512[0:1, 0:1]))
    return fin


def ph_ln_affine_residual(ctx, sc, u_bf, x_sb, gb_col, deps=(), nt=TLOC):
    ntb = nt // 512
    fin = ctx.sem()
    with ctx.block() as blk:
        v, s, t = _ln_stats(ctx, blk, sc,
                            lambda c, tb: u_bf[:, c * nt + tb * 512:
                                               c * nt + (tb + 1) * 512],
                            deps=deps, src_is_bf=True, nt=nt)
        bc1 = ctx.ps(2); bc2 = ctx.ps(3)
        for tb in range(ntb):
            sl = slice(tb * 512, (tb + 1) * 512)
            @blk.tensor
            def _(tt, sl=sl):
                ctx.wait(tt, s)
                ctx.wait(tt, v)
                tt.matmul(bc1[:], sc.ones_row_f[:], sc.murow[:, sl],
                          start=True, stop=True)
                t.inc(tt.matmul(bc2[:], sc.ones_row32[:], sc.rowA[:, sl],
                                start=True, stop=True))
            @blk.vector
            def _(vv, sl=sl, tb=tb):
                ctx.wait(vv, t)
                for c in range(NCH):
                    xsl = slice(c * nt + tb * 512, c * nt + (tb + 1) * 512)
                    ctx.wait(vv, v)
                    m = v.inc(vv.tensor_tensor(sc.tmp512[:], u_bf[:, xsl],
                                               bc1[:], op=ALU.subtract))
                    ctx.wait(vv, v, m)
                    m = v.inc(vv.tensor_tensor(sc.tmp512[:], sc.tmp512[:],
                                               bc2[:], op=ALU.mult))
                    ctx.wait(vv, v, m)
                    m = v.inc(vv.tensor_scalar(sc.tmp512[:], sc.tmp512[:],
                                               gb_col[:, 3 * c + 1:3 * c + 2],
                                               gb_col[:, 3 * c + 2:3 * c + 3],
                                               op0=ALU.mult, op1=ALU.add))
                    ctx.wait(vv, v, m)
                    v.inc(vv.tensor_tensor(x_sb[:, xsl], x_sb[:, xsl],
                                           sc.tmp512[:], op=ALU.add))
        @blk.vector
        def _(vv):
            fin.inc(vv.tensor_copy(sc.tmp512[0:1, 0:1], sc.tmp512[0:1, 0:1]))
    return fin



def load_weight_cm(ctx, eng, dram_t, sb, sem, row0=0, nch=8, dout=1024):
    for c in range(nch):
        sem.inc(eng.dma_start(
            out=sb[:, c * dout:(c + 1) * dout],
            in_=dram_t.ap()[row0 + c * 128:row0 + (c + 1) * 128, :]), 16)



def ph_l0_attn(ctx, lnsc, x_sb, xn_bf, ln_done):
    kexp_dram = ctx.dram("l0_kexp", [D, TLOC], BF16)
    qn_dram = ctx.dram("l0_qn", [TLOC, D], BF16)
    ar_in = ctx.dram("l0_ar_in", [D, 1026], F32)
    ar_out = ctx.dram("l0_ar_out", [D, 1026], F32)
    ck_bounce = ctx.dram("l0_ckb", [1, 1024], F32)

    wk_sb = ctx.atb(0, 8 * 1024)
    wq_sb = ctx.atb(0, 8 * 1024)
    ktd = ctx.atb(0, 16 * 1024)
    qn_dt = ctx.atb(0, 8 * TLOC)
    wv_sb = ctx.atb(16, 8 * 1024)
    wo_sb = ctx.atb(0, 8 * 1024)
    vte = ctx.atb(32, 16 * 1024)
    a_et = ctx.atb(32, 8 * TLOC)
    krot = ctx.atb(64, 2 * 2048)
    stg = ctx.at(64, 2048)
    qrot = ctx.atb(72, 2 * 1024)
    crot = ctx.at(72, 2048)
    wksq = ctx.atb(80, 1024)
    ckrow = ctx.at(82, 1024, part=1)
    c_bf = ctx.atb(80, 8 * 1024)

    negck = ctx.sbuf("l0_negck", [128, 8], F32)
    zacc = ctx.sbuf("l0_zacc", [128, 8], F32)
    zinv = ctx.sbuf("l0_zinv", [128, 8], F32)
    bias3 = ctx.sbuf("l0_bias3", [128, 24], F32)
    qsc = ctx.sbuf("l0_qsc", [128, 8], F32)
    u_bf = xn_bf

    big0 = ctx.PS[:, 0:2048]
    big1 = ctx.PS[:, 2048:4096]
    Q = [ctx.PS[:, 0:512], ctx.PS[:, 512:1024],
         ctx.PS[:, 1024:1536], ctx.PS[:, 1536:2048]]

    dg = ctx.sem(); dy = ctx.sem()

    with ctx.block() as blk:
        @blk.gpsimd
        def _(g):
            load_weight_cm(ctx, g, ctx.wk, wk_sb, dg)
            load_weight_cm(ctx, g, ctx.wv, wv_sb, dg)
            for c in range(8):
                for j in range(3):
                    dg.inc(g.dma_start(
                        out=bias3[:, 3 * c + j:3 * c + j + 1],
                        in_=ctx.bo_gb.ap()[c * 128:(c + 1) * 128, j:j + 1],
                        allow_slow_non_contiguous=True), 16)
    v0 = ctx.sem(); s0 = ctx.sem(); t0 = ctx.sem()
    with ctx.block() as blk:
        @blk.scalar
        def _(ss):
            ctx.wait(ss, dg, 16 * 8)
            for c in range(8):
                if c > 0:
                    ctx.wait(ss, t0, c)
                s0.inc(ss.activation(wksq[:], wk_sb[:, c * 1024:(c + 1) * 1024],
                                     AF.Square))
        @blk.tensor
        def _(tt):
            for c in range(8):
                ctx.wait(tt, s0, c + 1)
                tt.matmul(ctx.PS[0:1, 0:512], lnsc.ones_col_bf[:],
                          wksq[:, 0:512], start=(c == 0), stop=(c == 7))
                ins = tt.matmul(ctx.PS[0:1, 512:1024], lnsc.ones_col_bf[:],
                                wksq[:, 512:1024], start=(c == 0), stop=(c == 7))
                t0.inc(ins)
        @blk.scalar
        def _(ss):
            ctx.wait(ss, t0, 8)
            s0.inc(ss.activation(ckrow[0:1, :], ctx.PS[0:1, 0:1024], AF.Sqrt,
                                 scale=1024.0))
        @blk.vector
        def _(vv):
            ctx.wait(vv, s0)
            v0.inc(vv.tensor_scalar(ckrow[0:1, :], ckrow[0:1, :], -1.0, 0.0,
                                    op0=ALU.mult, op1=ALU.add))
        @blk.sync
        def _(y):
            ctx.wait(y, v0)
            dy.inc(y.dma_start(out=ck_bounce.ap(), in_=ckrow[0:1, :]), 16)
            ctx.wait(y, dy)
            for c in range(8):
                dy.inc(y.dma_start(out=negck[:, c:c + 1],
                                   in_=ck_bounce.ap()[0:1, c * 128:(c + 1) * 128]), 16)

    tk = ctx.sem(); sk = ctx.sem(); dk = ctx.sem()
    with ctx.block() as blk:
        @blk.tensor
        def _(tt):
            ctx.wait(tt, ln_done)
            ctx.wait(tt, dg)
            ctx.wait(tt, dy)
            for m in range(8):
                if m >= 2:
                    ctx.wait(tt, sk, m - 1)
                ps = big0 if m % 2 == 0 else big1
                for n in range(4):
                    for c in range(8):
                        ins = tt.matmul(
                            ps[:, n * 512:(n + 1) * 512],
                            wk_sb[:, c * 1024 + m * 128: c * 1024 + m * 128 + 128],
                            xn_bf[:, c * TLOC + n * 512: c * TLOC + (n + 1) * 512],
                            start=(c == 0), stop=(c == 7))
                tk.inc(ins)
        @blk.scalar
        def _(ss):
            for m in range(8):
                ctx.wait(ss, tk, m + 1)
                if m >= 2:
                    ctx.wait(ss, dk, 16 * (m - 1))
                ps = big0 if m % 2 == 0 else big1
                sk.inc(ss.activation(krot[:, (m % 2) * 2048:(m % 2 + 1) * 2048],
                                     ps, AF.Exp, bias=negck[:, m:m + 1],
                                     accum_out=zacc[:, m:m + 1]))
        @blk.sync
        def _(y):
            for m in range(8):
                ctx.wait(y, sk, m + 1)
                dk.inc(y.dma_start(out=kexp_dram.ap()[m * 128:(m + 1) * 128, :],
                                   in_=krot[:, (m % 2) * 2048:(m % 2 + 1) * 2048]), 16)

    tv = ctx.sem(); vv_ = ctx.sem(); dgq = ctx.sem()
    with ctx.block() as blk:
        @blk.gpsimd
        def _(g):
            load_weight_cm(ctx, g, ctx.wq, wq_sb, dgq)
        @blk.tensor
        def _(tt):
            for i in range(16):
                if i >= 2:
                    ctx.wait(tt, vv_, i - 1)
                ps = big0 if i % 2 == 0 else big1
                for h in range(2):
                    for c in range(8):
                        ins = tt.matmul(
                            ps[:, h * 512:(h + 1) * 512],
                            xn_bf[:, c * TLOC + i * 128: c * TLOC + (i + 1) * 128],
                            wv_sb[:, c * 1024 + h * 512: c * 1024 + (h + 1) * 512],
                            start=(c == 0), stop=(c == 7))
                tv.inc(ins)
        @blk.vector
        def _(vv):
            for i in range(16):
                ctx.wait(vv, tv, i + 1)
                ps = big0 if i % 2 == 0 else big1
                vv_.inc(vv.tensor_copy(vte[:, i * 1024:(i + 1) * 1024],
                                       ps[:, 0:1024]))

    tq = ctx.sem(); vq = ctx.sem(); sq = ctx.sem(); vq2 = ctx.sem(); dq = ctx.sem()
    with ctx.block() as blk:
        @blk.tensor
        def _(tt):
            ctx.wait(tt, dgq)
            for i in range(16):
                if i >= 2:
                    ctx.wait(tt, sq, i - 1)
                ps = big0 if i % 2 == 0 else big1
                for h in range(2):
                    for c in range(8):
                        ins = tt.matmul(
                            ps[:, h * 512:(h + 1) * 512],
                            xn_bf[:, c * TLOC + i * 128: c * TLOC + (i + 1) * 128],
                            wq_sb[:, c * 1024 + h * 512: c * 1024 + (h + 1) * 512],
                            start=(c == 0), stop=(c == 7))
                tq.inc(ins)
        @blk.scalar
        def _(ss):
            for i in range(16):
                p = i % 2
                ctx.wait(ss, vq, i + 1)
                if i >= 2:
                    ctx.wait(ss, vq2, 2 * (i - 1))
                ps = big0 if p == 0 else big1
                sq.inc(ss.activation(qrot[:, p * 1024:(p + 1) * 1024],
                                     ps[:, 0:1024], AF.Exp,
                                     bias=qsc[:, 4 * p:4 * p + 1],
                                     accum_out=qsc[:, 4 * p + 1:4 * p + 2]))
        @blk.vector
        def _(vv):
            def recip_scale(j):
                p = j % 2
                ctx.wait(vv, sq, j + 1)
                m = vq2.inc(vv.reciprocal(qsc[:, 4 * p + 2:4 * p + 3],
                                          qsc[:, 4 * p + 1:4 * p + 2]))
                ctx.wait(vv, vq2, m)
                if j >= 2:
                    ctx.wait(vv, dq, 16 * (j - 1))
                vq2.inc(vv.tensor_scalar(qrot[:, p * 1024:(p + 1) * 1024],
                                         qrot[:, p * 1024:(p + 1) * 1024],
                                         qsc[:, 4 * p + 2:4 * p + 3], 1.0 / 32.0,
                                         op0=ALU.mult, op1=ALU.mult))
            for i in range(16):
                p = i % 2
                ctx.wait(vv, tq, i + 1)
                if i >= 2:
                    ctx.wait(vv, sq, i - 1)
                ps = big0 if p == 0 else big1
                vq.inc(vv.tensor_reduce(qsc[:, 4 * p:4 * p + 1], ps[:, 0:1024],
                                        axis=AXL.X, op=ALU.max, negate=True))
                if i >= 1:
                    recip_scale(i - 1)
            recip_scale(15)
        @blk.sync
        def _(y):
            for i in range(16):
                ctx.wait(y, vq2, 2 * i + 2)
                dq.inc(y.dma_start(out=qn_dram.ap()[i * 128:(i + 1) * 128, :],
                                   in_=qrot[:, (i % 2) * 1024:(i % 2 + 1) * 1024]), 16)

    dktd = ctx.sem(); tc_ = ctx.sem(); vc = ctx.sem(); dc = ctx.sem()
    with ctx.block() as blk:
        @blk.scalar
        def _(ss):
            ctx.wait(ss, dk)
            for i in range(16):
                dktd.inc(ss.dma_start(out=ktd[:, i * 1024:(i + 1) * 1024],
                                      in_=kexp_dram.ap()[:, i * 128:(i + 1) * 128],
                                      transpose=True), 16)
        @blk.tensor
        def _(tt):
            ctx.wait(tt, dktd)
            for m in range(8):
                for n in range(2):
                    it = m * 2 + n
                    if it >= 4:
                        ctx.wait(tt, vc, it - 3)
                    for i in range(16):
                        ins = tt.matmul(
                            Q[it % 4],
                            ktd[:, i * 1024 + m * 128: i * 1024 + m * 128 + 128],
                            vte[:, i * 1024 + n * 512: i * 1024 + (n + 1) * 512],
                            start=(i == 0), stop=(i == 15))
                    tc_.inc(ins)
        @blk.vector
        def _(vv):
            for it in range(16):
                ctx.wait(vv, tc_, it + 1)
                if it >= 4:
                    ctx.wait(vv, dc, 16 * (it - 3))
                vc.inc(vv.tensor_copy(stg[:, (it % 4) * 512:(it % 4 + 1) * 512],
                                      Q[it % 4]))
        @blk.sync
        def _(y):
            for it in range(16):
                m, n = divmod(it, 2)
                ctx.wait(y, vc, it + 1)
                dc.inc(y.dma_start(
                    out=ar_in.ap()[m * 128:(m + 1) * 128, n * 512:(n + 1) * 512],
                    in_=stg[:, (it % 4) * 512:(it % 4 + 1) * 512]), 16)
            ctx.wait(y, sk)
            for m in range(8):
                dc.inc(y.dma_start(out=ar_in.ap()[m * 128:(m + 1) * 128, 1024:1025],
                                   in_=zacc[:, m:m + 1],
                                   allow_slow_non_contiguous=True), 16)

    ccs = ctx.sem()
    with ctx.block() as blk:
        @blk.gpsimd
        def _(g):
            ctx.wait(g, dc)
            ccs.inc(g.collective_compute(
                "AllReduce", ALU.add, replica_groups=GROUPS,
                ins=[ar_in.ap()], outs=[ar_out.ap()]))

    dr = ctx.sem(); vr = ctx.sem(); dqn = ctx.sem()
    with ctx.block() as blk:
        @blk.sync
        def _(y):
            ctx.wait(y, ccs)
            for m in range(8):
                dr.inc(y.dma_start(out=zinv[:, m:m + 1],
                                   in_=ar_out.ap()[m * 128:(m + 1) * 128, 1024:1025],
                                   allow_slow_non_contiguous=True), 16)
            for m in range(8):
                if m >= 2:
                    ctx.wait(y, vr, m)
                dr.inc(y.dma_start(out=crot[:, (m % 2) * 1024:(m % 2 + 1) * 1024],
                                   in_=ar_out.ap()[m * 128:(m + 1) * 128, 0:1024]), 16)
        @blk.vector
        def _(vv):
            ctx.wait(vv, dr, 16 * 8)
            vr.inc(vv.reciprocal(zinv[:], zinv[:]))
            for m in range(8):
                ctx.wait(vv, dr, 16 * (9 + m))
                vr.inc(vv.tensor_scalar(c_bf[:, m * 1024:(m + 1) * 1024],
                                        crot[:, (m % 2) * 1024:(m % 2 + 1) * 1024],
                                        zinv[:, m:m + 1], 1.0,
                                        op0=ALU.mult, op1=ALU.mult))
        @blk.scalar
        def _(ss):
            ctx.wait(ss, dq)
            for c in range(8):
                dqn.inc(ss.dma_start(out=qn_dt[:, c * TLOC:(c + 1) * TLOC],
                                     in_=qn_dram.ap()[:, c * 128:(c + 1) * 128],
                                     transpose=True), 16)

    ta = ctx.sem(); va = ctx.sem()
    with ctx.block() as blk:
        @blk.tensor
        def _(tt):
            ctx.wait(tt, dqn)
            ctx.wait(tt, vr)
            for m in range(8):
                for n in range(4):
                    it = m * 4 + n
                    if it >= 4:
                        ctx.wait(tt, va, it - 3)
                    for c in range(8):
                        ins = tt.matmul(
                            Q[it % 4],
                            c_bf[:, c * 1024 + m * 128: c * 1024 + m * 128 + 128],
                            qn_dt[:, c * TLOC + n * 512: c * TLOC + (n + 1) * 512],
                            start=(c == 0), stop=(c == 7))
                    ta.inc(ins)
        @blk.vector
        def _(vv):
            for it in range(32):
                m, n = divmod(it, 4)
                ctx.wait(vv, ta, it + 1)
                va.inc(vv.tensor_copy(
                    a_et[:, m * TLOC + n * 512: m * TLOC + (n + 1) * 512],
                    Q[it % 4]))

    dgo = ctx.sem(); to = ctx.sem(); vo = ctx.sem()
    with ctx.block() as blk:
        @blk.gpsimd
        def _(g):
            load_weight_cm(ctx, g, ctx.wo, wo_sb, dgo)
    with ctx.block() as blk:
        @blk.tensor
        def _(tt):
            ctx.wait(tt, dgo)
            for n in range(4):
                for m in range(8):
                    it = n * 8 + m
                    if it >= 4:
                        ctx.wait(tt, vo, 2 * (it - 3))
                    for c in range(8):
                        ins = tt.matmul(
                            Q[it % 4],
                            wo_sb[:, c * 1024 + m * 128: c * 1024 + m * 128 + 128],
                            a_et[:, c * TLOC + n * 512: c * TLOC + (n + 1) * 512],
                            start=(c == 0), stop=(c == 7))
                    to.inc(ins)
        @blk.vector
        def _(vv):
            for n in range(4):
                for m in range(8):
                    it = n * 8 + m
                    ctx.wait(vv, to, it + 1)
                    xsl = slice(m * TLOC + n * 512, m * TLOC + (n + 1) * 512)
                    mv = vo.inc(vv.scalar_tensor_tensor(
                        stg[:, 0:512], Q[it % 4], bias3[:, 3 * m:3 * m + 1],
                        u_bf[:, xsl], op0=ALU.add, op1=ALU.add))
                    ctx.wait(vv, vo, mv)
                    vo.inc(vv.tensor_copy(u_bf[:, xsl], stg[:, 0:512]))
    return ph_ln_affine_residual(ctx, lnsc, u_bf, x_sb, bias3, deps=(vo,))



def ph_l0_ffn(ctx, lnsc, x_sb, xn_bf, xn_ready):
    wacc_r = ctx.at(0, 2048)
    wacc_i = ctx.at(8, 2048)
    rep_a = ctx.at(16, 2048)
    rep_b = ctx.at(24, 2048)
    rep_t = ctx.at(32, 2048)
    wr_bf = ctx.atb(40, 2048)
    wi_bf = ctx.atb(44, 2048)
    nwi_bf = ctx.atb(48, 2048)
    bias_col = ctx.sbuf("ffn_bias_col", [128, 8], F32)
    ps_a = ctx.ps(0); ps_b = ctx.ps(1)

    dg = ctx.sem(); v = ctx.sem(); t = ctx.sem(); ev = ctx.sem()
    for (arep, brep, wacc) in ((ctx.arep_r, ctx.brep_r, wacc_r),
                               (ctx.arep_i, ctx.brep_i, wacc_i)):
        for cc in range(4):
            with ctx.block() as blk:
                @blk.gpsimd
                def _(gg, arep=arep, brep=brep, cc=cc):
                    ctx.wait(gg, v)
                    for k in range(4):
                        dg.inc(gg.dma_start(
                            out=rep_a[:, k * 512:(k + 1) * 512],
                            in_=arep.ap()[cc, k * 128:(k + 1) * 128, :]), 16)
                        dg.inc(gg.dma_start(
                            out=rep_b[:, k * 512:(k + 1) * 512],
                            in_=brep.ap()[cc, k * 128:(k + 1) * 128, :]), 16)
                @blk.vector
                def _(vv, cc=cc, wacc=wacc):
                    ctx.wait(vv, dg)
                    if cc == 0:
                        v.inc(vv.tensor_tensor(wacc[:], rep_a[:], rep_b[:],
                                               op=ALU.mult))
                    else:
                        m = v.inc(vv.tensor_tensor(rep_t[:], rep_a[:],
                                                   rep_b[:], op=ALU.mult))
                        ctx.wait(vv, v, m)
                        v.inc(vv.tensor_tensor(wacc[:], wacc[:], rep_t[:],
                                               op=ALU.add))
    with ctx.block() as blk:
        @blk.gpsimd
        def _(gg):
            for c in range(8):
                dg.inc(gg.dma_start(out=bias_col[:, c:c + 1],
                                    in_=ctx.bias_ri.ap()[c * 128:(c + 1) * 128, :]), 16)
        @blk.vector
        def _(vv):
            ctx.wait(vv, v)
            vv.tensor_copy(wr_bf[:], wacc_r[:])
            vv.tensor_copy(wi_bf[:], wacc_i[:])
            v.inc(vv.tensor_scalar(nwi_bf[:], wacc_i[:], -1.0, 0.0,
                                   op0=ALU.mult, op1=ALU.add))
    with ctx.block() as blk:
        @blk.tensor
        def _(tt):
            ctx.wait(tt, v)
            ctx.wait(tt, dg)
            ctx.wait(tt, xn_ready)
            for it in range(32):
                m, n = divmod(it, 4)
                real = m < 4
                fcols = (m % 4) * 128
                ps = ps_a if it % 2 == 0 else ps_b
                if it >= 2:
                    ctx.wait(tt, ev, it - 1)
                first = wr_bf if real else wi_bf
                second = nwi_bf if real else wr_bf
                for c in range(4):
                    tt.matmul(ps, first[:, c * 512 + fcols:c * 512 + fcols + 128],
                              xn_bf[:, c * TLOC + n * 512: c * TLOC + (n + 1) * 512],
                              start=(c == 0), stop=False)
                for c in range(4):
                    ins = tt.matmul(
                        ps, second[:, c * 512 + fcols:c * 512 + fcols + 128],
                        xn_bf[:, (4 + c) * TLOC + n * 512: (4 + c) * TLOC + (n + 1) * 512],
                        start=False, stop=(c == 3))
                t.inc(ins)
        @blk.vector
        def _(vv):
            for it in range(32):
                m, n = divmod(it, 4)
                ps = ps_a if it % 2 == 0 else ps_b
                ctx.wait(vv, t, it + 1)
                xsl = slice(m * TLOC + n * 512, m * TLOC + (n + 1) * 512)
                ev.inc(vv.scalar_tensor_tensor(x_sb[:, xsl], ps,
                                               bias_col[:, m:m + 1],
                                               x_sb[:, xsl],
                                               op0=ALU.add, op1=ALU.add))
    return ev




def ph_l1_attn(ctx, lnsc, x_sb, xn_bf, ln_done):
    from concourse.aot_env import allow_ici_all_to_all

    RSH = 386
    a2a_in = ctx.dram("c_a2a_in", [8 * RSH, TLOC], BF16)
    a2a_out = ctx.dram("c_a2a_out", [8 * RSH, TLOC], BF16)
    ao_dram = [ctx.dram(f"c_ao{b_}", [128, 8192], BF16) for b_ in range(2)]
    rs_in = ctx.dram("c_rs_in", [8 * 1024, TLOC], BF16)
    rs_out = ctx.dram("c_rs_out", [1024, TLOC], BF16)
    nb_bounce = ctx.dram("c_nbb", [1, 2], F32)
    a2a_flat = a2a_in.ap().rearrange("r c -> (r c)")
    a2o_flat = a2a_out.ap().rearrange("r c -> (r c)")

    q_bh = ctx.AR[:, 0:4096].bitcast(BF16)
    k_bh = ctx.AR[:, 4096:8192].bitcast(BF16)
    v_bh = ctx.AR[:, 8192:12352].bitcast(BF16)
    pring4 = [ctx.AR[:, 12352 + s * 512:12352 + (s + 1) * 512].bitcast(BF16)
             for s in range(4)]
    masks = ctx.AR[:, 14400:15424].bitcast(BF16)
    bcsb = ctx.AR[0:64, 15424:15936]
    dscr = ctx.AR[0:1, 15936:16448]
    aorot = [ctx.AR[0:64, 16448 + s * 256:16448 + (s + 1) * 256].bitcast(BF16)
             for s in range(2)]
    wpart = ctx.AR[:, 12352:16448].bitcast(BF16)
    rotq = [ctx.AR[:, 16448 + s * 1024:16448 + (s + 1) * 1024].bitcast(BF16)
            for s in range(2)]
    rotv = [ctx.AR[:, 18496 + s * 520:18496 + (s + 1) * 520].bitcast(BF16)
            for s in range(2)]
    wo_sb = ctx.AR[:, 0:512].bitcast(BF16)
    stg7 = ctx.AR[:, 512:1024].bitcast(BF16)
    aold = [ctx.AR[:, 1536 + s * 256:1536 + (s + 1) * 256].bitcast(BF16)
            for s in range(2)]
    rsld = [ctx.AR[:, 2048 + s * 2048:2048 + (s + 1) * 2048] for s in range(2)]
    negc_bh = ctx.sbuf("c_negc", [128, 2], F32)
    nqm = ctx.sbuf("c_nqm", [2, 4], F32)
    ncrow = ctx.sbuf("c_ncrow", [1, 2], F32)
    if not hasattr(ctx, "c_halfones"):
        ctx.c_halfones = ctx.sbuf("c_halfones", [128, 2], BF16)

    dq_ship = dk_ship = None
    for wname in ("q", "k"):
        dgw = ctx.sem(); tw = ctx.sem(); vw = ctx.sem(); dww = ctx.sem()
        wdram = ctx.wqT if wname == "q" else ctx.wkT
        roff = 0 if wname == "q" else 128
        with ctx.block() as blk:
            @blk.gpsimd
            def _(g, wdram=wdram, wname=wname):
                if wname == "q":
                    g.memset(ctx.c_halfones[:], 0.0)
                    g.memset(ctx.c_halfones[0:64, 0:1], 1.0)
                    g.memset(ctx.c_halfones[64:128, 1:2], 1.0)
                load_weight_cm(ctx, g, wdram, wpart, dgw)
            @blk.tensor
            def _(tt):
                ctx.wait(tt, ln_done)
                ctx.wait(tt, dgw)
                for m in range(8):
                    if m >= 2:
                        ctx.wait(tt, vw, m - 1)
                    ps = ctx.PS[:, 0:2048] if m % 2 == 0 else ctx.PS[:, 2048:4096]
                    for n in range(4):
                        for c in range(8):
                            ins = tt.matmul(
                                ps[:, n * 512:(n + 1) * 512],
                                wpart[:, c * 1024 + m * 128: c * 1024 + m * 128 + 128],
                                xn_bf[:, c * TLOC + n * 512: c * TLOC + (n + 1) * 512],
                                start=(c == 0), stop=(c == 7))
                    tw.inc(ins)
            @blk.vector
            def _(vv):
                for m in range(8):
                    ctx.wait(vv, tw, m + 1)
                    if m >= 2:
                        ctx.wait(vv, dww, 16 * (m - 1))
                    ps = ctx.PS[:, 0:2048] if m % 2 == 0 else ctx.PS[:, 2048:4096]
                    vw.inc(vv.tensor_copy(rotq[m % 2][:], ps))
            @blk.sync
            def _(y, roff=roff):
                for m in range(8):
                    ctx.wait(y, vw, m + 1)
                    dww.inc(y.dma_start(
                        out=a2a_in.ap()[RSH * m + roff:RSH * m + roff + 128, :],
                        in_=rotq[m % 2][:]), 16)
        if wname == "q":
            dq_ship = dww
        else:
            dk_ship = dww
    dgv = ctx.sem(); tv = ctx.sem(); vv_ = ctx.sem(); dvv = ctx.sem(); gmz = ctx.sem()
    with ctx.block() as blk:
        @blk.gpsimd
        def _(g):
            load_weight_cm(ctx, g, ctx.wvT, wpart, dgv)
            g.memset(rotv[0][:].rearrange("p (h w) -> p h w", w=65)[:, :, 64:65], 1.0)
            gmz.inc(g.memset(
                rotv[1][:].rearrange("p (h w) -> p h w", w=65)[:, :, 64:65], 1.0))
        @blk.tensor
        def _(tt):
            ctx.wait(tt, dgv)
            ctx.wait(tt, gmz)
            for i in range(16):
                if i >= 2:
                    ctx.wait(tt, vv_, i - 1)
                ps = ctx.PS[:, 0:2048] if i % 2 == 0 else ctx.PS[:, 2048:4096]
                for h in range(2):
                    for c in range(8):
                        ins = tt.matmul(
                            ps[:, h * 512:(h + 1) * 512],
                            xn_bf[:, c * TLOC + i * 128: c * TLOC + (i + 1) * 128],
                            wpart[:, c * 1024 + h * 512: c * 1024 + (h + 1) * 512],
                            start=(c == 0), stop=(c == 7))
                tv.inc(ins)
        @blk.vector
        def _(vv):
            for i in range(16):
                ctx.wait(vv, tv, i + 1)
                if i >= 2:
                    ctx.wait(vv, dvv, 16 * 8 * (i - 1))
                ps = ctx.PS[:, 0:2048] if i % 2 == 0 else ctx.PS[:, 2048:4096]
                vv_.inc(vv.tensor_copy(
                    rotv[i % 2][:].rearrange("p (h w) -> p h w", w=65)[:, :, 0:64],
                    ps[:, 0:1024].rearrange("p (h w) -> p h w", w=64)))
        @blk.sync
        def _(y):
            for i in range(16):
                ctx.wait(y, vv_, i + 1)
                for jj in range(8):
                    off = (RSH * jj + 256) * TLOC + (i * 128) * 130
                    dvv.inc(y.dma_start(
                        out=a2a_flat[off:off + 128 * 130].rearrange(
                            "(p w) -> p w", w=130),
                        in_=rotv[i % 2][:, jj * 130:(jj + 1) * 130]), 16)

    ccs = ctx.sem(); dmk = ctx.sem()
    with ctx.block() as blk:
        @blk.gpsimd
        def _(g):
            ctx.wait(g, dq_ship); ctx.wait(g, dk_ship); ctx.wait(g, dvv)
            with allow_ici_all_to_all():
                ccs.inc(g.collective_compute(
                    "AllToAll", ALU.bypass, replica_groups=ALL8,
                    ins=[a2a_in.ap()], outs=[a2a_out.ap()]))
            for t4 in range(4):
                dmk.inc(g.dma_start(out=masks[:, t4 * 512:(t4 + 1) * 512],
                                    in_=ctx.cmask.ap()[t4, :, :]), 16)

    def rk(b_, g):
        return 4 * b_ + (g if g < 4 else 7 - g), (0 if g < 4 else 1)

    dao_list = []
    for b_ in range(2):
        dl = ctx.sem()
        with ctx.block() as blk:
            @blk.sync
            def _(y, b_=b_):
                ctx.wait(y, ccs)
                for g in range(8):
                    rg, sg = rk(b_, g)
                    dl.inc(y.dma_start(
                        out=q_bh[:, g * 1024:(g + 1) * 1024],
                        in_=a2a_out.ap()[RSH * rg:RSH * rg + 128,
                                         1024 * sg:1024 * sg + 1024]), 16)
                    dl.inc(y.dma_start(
                        out=k_bh[:, g * 1024:(g + 1) * 1024],
                        in_=a2a_out.ap()[RSH * rg + 128:RSH * rg + 256,
                                         1024 * sg:1024 * sg + 1024]), 16)
                    off = (RSH * rg + 256) * TLOC + sg * 1024 * 130
                    dl.inc(y.dma_start(
                        out=v_bh[:, g * 1040:(g + 1) * 1040].rearrange(
                            "p (i w) -> p i w", w=130),
                        in_=a2o_flat[off:off + 1024 * 130].rearrange(
                            "(i p w) -> p i w", p=128, w=130)), 16)

        sn = ctx.sem(); tn = ctx.sem(); vn = ctx.sem(); dn = ctx.sem(); tb2 = ctx.sem()
        with ctx.block() as blk:
            @blk.scalar
            def _(ss):
                ctx.wait(ss, dl)
                for w2 in range(4):
                    src = q_bh if w2 < 2 else k_bh
                    half = w2 % 2
                    if w2 > 0:
                        ctx.wait(ss, tn, 8 * w2)
                    sn.inc(ss.activation(lnsc.sq_bf[:, 0:4096],
                                         src[:, half * 4096:(half + 1) * 4096],
                                         AF.Square))
            @blk.tensor
            def _(tt):
                vn_after = [1, 3, 4, 6]
                for w2 in range(4):
                    ctx.wait(tt, sn, w2 + 1)
                    if w2 > 0:
                        ctx.wait(tt, vn, vn_after[w2 - 1])
                    for n in range(8):
                        ins = tt.matmul(
                            ctx.PS[0:2, n * 512:(n + 1) * 512],
                            ctx.c_halfones[:], lnsc.sq_bf[:, n * 512:(n + 1) * 512],
                            start=True, stop=True)
                        tn.inc(ins)
            @blk.vector
            def _(vv):
                cnt = 0
                for w2 in range(4):
                    col, half = w2 // 2, w2 % 2
                    ctx.wait(vv, tn, 8 * (w2 + 1))
                    if half == 0:
                        m = vn.inc(vv.tensor_reduce(nqm[0:2, col:col + 1],
                                                    ctx.PS[0:2, 0:4096],
                                                    axis=AXL.X, op=ALU.max))
                        ctx.wait(vv, vn, m)
                    else:
                        m = vn.inc(vv.tensor_reduce(nqm[0:2, 2 + col:3 + col],
                                                    ctx.PS[0:2, 0:4096],
                                                    axis=AXL.X, op=ALU.max))
                        ctx.wait(vv, vn, m)
                        m = vn.inc(vv.tensor_tensor(nqm[0:2, col:col + 1],
                                                    nqm[0:2, col:col + 1],
                                                    nqm[0:2, 2 + col:3 + col],
                                                    op=ALU.max))
                        ctx.wait(vv, vn, m)
            @blk.scalar
            def _(ss):
                ctx.wait(ss, vn)
                m = sn.inc(ss.activation(nqm[0:2, 2:3], nqm[0:2, 0:1], AF.Sqrt))
                ctx.wait(ss, sn, m)
                sn.inc(ss.activation(nqm[0:2, 3:4], nqm[0:2, 1:2], AF.Sqrt))
            @blk.vector
            def _(vv):
                ctx.wait(vv, sn)
                m = vn.inc(vv.tensor_tensor(nqm[0:2, 0:1], nqm[0:2, 2:3],
                                            nqm[0:2, 3:4], op=ALU.mult))
                ctx.wait(vv, vn, m)
                vn.inc(vv.tensor_scalar(nqm[0:2, 1:2], nqm[0:2, 0:1],
                                        -0.126, 0.0, op0=ALU.mult, op1=ALU.add))
            @blk.sync
            def _(y):
                ctx.wait(y, vn)
                dn.inc(y.dma_start(out=nb_bounce.ap(), in_=nqm[0:2, 1:2],
                                   allow_slow_non_contiguous=True), 16)
                ctx.wait(y, dn)
                dn.inc(y.dma_start(out=ncrow[:], in_=nb_bounce.ap()), 16)
            @blk.tensor
            def _(tt):
                ctx.wait(tt, dn)
                for h in range(2):
                    tb2.inc(tt.matmul(
                        ctx.PS[:, 2048 + h * 512:2048 + h * 512 + 1],
                        lnsc.ones_row_f[:], ncrow[0:1, h:h + 1],
                        start=True, stop=True))
            @blk.vector
            def _(vv):
                ctx.wait(vv, tb2)
                for h in range(2):
                    vn.inc(vv.tensor_copy(
                        negc_bh[:, h:h + 1],
                        ctx.PS[:, 2048 + h * 512:2048 + h * 512 + 1]))

        tl = ctx.sem(); se = ctx.sem(); vm = ctx.sem(); vf = ctx.sem(); dao_b = ctx.sem()

        def maskops2(bp2):
            ops = []
            for sc in range(2):
                off = bp2 * 128 - sc * 512
                if off <= -128:
                    continue
                ops.append((sc, "zero" if off >= 512 else off // 128))
            return ops

        seq = []
        for h in range(2):
            for qq in range(8):
                for g in range(qq + 1):
                    for bp2 in range(8):
                        seq.append((h, qq, g, bp2))
        n_j = len(seq)
        is_first = [g == 0 and bp2 == 0 for (h, qq, g, bp2) in seq]
        is_last = [g == qq and bp2 == 7 for (h, qq, g, bp2) in seq]

        vmM = [0] * n_j
        _vmc = 0
        for j, (h, qq, g, bp2) in enumerate(seq):
            if g == qq:
                _vmc += len(maskops2(bp2))
            vmM[j] = _vmc
        tlL = [0] * n_j; tlPV = [0] * n_j
        tlB = {}; vfR = {}; vfE = {}; vfA = {}
        fin_f = [None] * n_j
        _tl = 2; tlL[0] = 1; tlL[1] = 2
        _f = 0
        for j in range(n_j):
            if j + 2 < n_j:
                _tl += 1; tlL[j + 2] = _tl
            _tl += 1; tlPV[j] = _tl
            if is_last[j]:
                fin_f[j] = (_f, _f + 1)
                for sc in range(2):
                    _tl += 1; tlB[_f + sc] = _tl
                _f += 2
        _vf = 0
        for j in range(n_j):
            if is_last[j]:
                for f in fin_f[j]:
                    _vf += 1; vfR[f] = _vf
                    _vf += 1; vfE[f] = _vf
                    _vf += 1; vfA[f] = _vf
        nfin = _f

        with ctx.block() as blk:
            @blk.tensor
            def _(tt, b_=b_):
                ctx.wait(tt, vn)
                ctx.wait(tt, dmk)
                prev_f = [None]

                def emit_L(m):
                    h, qq, g, bp2 = seq[m]
                    if m >= 2:
                        ctx.wait(tt, se, m - 1)
                    for sc in range(2):
                        ins = tt.matmul(
                            ctx.PS[:, (m % 2) * 1024 + sc * 512:
                                   (m % 2) * 1024 + sc * 512 + 512],
                            k_bh[64 * h:64 * h + 64,
                                 g * 1024 + bp2 * 128:g * 1024 + (bp2 + 1) * 128],
                            q_bh[64 * h:64 * h + 64,
                                 qq * 1024 + sc * 512:qq * 1024 + sc * 512 + 512],
                            start=True, stop=True, skip_group_check=True)
                    tl.inc(ins)
                    assert tl.n == tlL[m]

                def emit_PV(j):
                    h, qq, g, bp2 = seq[j]
                    ctx.wait(tt, se, j + 1)
                    if g == qq:
                        ctx.wait(tt, vm, vmM[j])
                    if is_first[j] and prev_f[0] is not None:
                        ctx.wait(tt, vf, vfA[prev_f[0]])
                    for sc in range(2):
                        ins = tt.matmul(
                            ctx.PS[0:65, (4 + sc) * 512:(4 + sc) * 512 + 512],
                            v_bh[:, (g * 8 + bp2) * 130 + 65 * h:
                                 (g * 8 + bp2) * 130 + 65 * h + 65],
                            pring4[j % 4][:, sc * 512:(sc + 1) * 512],
                            start=(g == 0 and bp2 == 0),
                            stop=(g == qq and bp2 == 7),
                            skip_group_check=True)
                    tl.inc(ins)
                    assert tl.n == tlPV[j]

                emit_L(0); emit_L(1)
                for j in range(n_j):
                    if j + 2 < n_j:
                        emit_L(j + 2)
                    emit_PV(j)
                    if is_last[j]:
                        for sc in range(2):
                            f = fin_f[j][sc]
                            ctx.wait(tt, vf, vfR[f])
                            ins = tt.matmul(ctx.PS[0:64, 3072:3584],
                                            lnsc.ones_row_f[0:1, 0:64],
                                            dscr[0:1, :], start=True, stop=True,
                                            skip_group_check=True)
                            tl.inc(ins)
                            assert tl.n == tlB[f]
                        prev_f[0] = fin_f[j][1]
            @blk.scalar
            def _(ss):
                for j in range(n_j):
                    h, qq, g, bp2 = seq[j]
                    ctx.wait(ss, tl, tlL[j])
                    se.inc(ss.activation(pring4[j % 4][:],
                                         ctx.PS[:, (j % 2) * 1024:
                                                (j % 2) * 1024 + 1024],
                                         AF.Exp, bias=negc_bh[:, h:h + 1],
                                         scale=0.125))
            @blk.vector
            def _(vv, b_=b_):
                for j in range(n_j):
                    h, qq, g, bp2 = seq[j]
                    if g == qq:
                        ctx.wait(vv, se, j + 1)
                        for (sc, kind) in maskops2(bp2):
                            sl2 = slice(sc * 512, (sc + 1) * 512)
                            if kind == "zero":
                                m = vm.inc(vv.memset(pring4[j % 4][:, sl2], 0.0))
                            else:
                                m = vm.inc(vv.tensor_tensor(
                                    pring4[j % 4][:, sl2], pring4[j % 4][:, sl2],
                                    masks[:, kind * 512:(kind + 1) * 512],
                                    op=ALU.mult))
                            ctx.wait(vv, vm, m)
                        assert vm.n == vmM[j]
                    if is_last[j]:
                        for sc in range(2):
                            f = fin_f[j][sc]
                            ctx.wait(vv, tl, tlPV[j])
                            if f >= 1:
                                ctx.wait(vv, tl, tlB[f - 1])
                            m = vf.inc(vv.reciprocal(
                                dscr[0:1, :],
                                ctx.PS[64:65, (4 + sc) * 512:(4 + sc) * 512 + 512]))
                            assert vf.n == vfR[f]
                            ctx.wait(vv, tl, tlB[f])
                            if f >= 2:
                                ctx.wait(vv, dao_b, 16 * (f - 1))
                            m = vf.inc(vv.tensor_copy(bcsb[:], ctx.PS[0:64, 3072:3584]))
                            assert vf.n == vfE[f]
                            ctx.wait(vv, vf, m)
                            vf.inc(vv.tensor_tensor(
                                aorot[f % 2][:],
                                ctx.PS[0:64, (4 + sc) * 512:(4 + sc) * 512 + 512],
                                bcsb[:], op=ALU.mult))
                            assert vf.n == vfA[f]
            @blk.sync
            def _(y, b_=b_):
                for j in range(n_j):
                    if not is_last[j]:
                        continue
                    h, qq, g, bp2 = seq[j]
                    for sc in range(2):
                        f = fin_f[j][sc]
                        ctx.wait(y, vf, vfA[f])
                        dao_b.inc(y.dma_start(
                            out=ao_dram[b_].ap()[64 * h:64 * h + 64,
                                                 qq * 1024 + sc * 512:
                                                 qq * 1024 + sc * 512 + 512],
                            in_=aorot[f % 2][:]), 16)
        dao_list.append(dao_b)

    dgo = ctx.sem(); dld = ctx.sem(); t7 = ctx.sem(); v7 = ctx.sem(); d7 = ctx.sem()
    with ctx.block() as blk:
        @blk.gpsimd
        def _(g):
            load_weight_cm(ctx, g, ctx.woT_pair, wo_sb, dgo, nch=1)
    with ctx.block() as blk:
        @blk.scalar
        def _(ss):
            ctx.wait(ss, dgo)
            for dab in dao_list:
                ctx.wait(ss, dab)
            for b_ in range(2):
                for n in range(16):
                    i = b_ * 16 + n
                    if i >= 2:
                        ctx.wait(ss, t7, 8 * (i - 1))
                    dld.inc(ss.dma_start(out=aold[i % 2][:],
                                         in_=ao_dram[b_].ap()[:, n * 512:(n + 1) * 512]), 16)
        @blk.tensor
        def _(tt):
            for b_ in range(2):
                for n in range(16):
                    i = b_ * 16 + n
                    ctx.wait(tt, dld, 16 * (i + 1))
                    if i >= 1:
                        ctx.wait(tt, v7, 8 * i)
                    for m in range(8):
                        if m >= 4:
                            ctx.wait(tt, v7, 8 * i + m - 3)
                        ins = tt.matmul(
                            ctx.PS[:, (m % 4) * 512:(m % 4 + 1) * 512],
                            wo_sb[:, m * 128:(m + 1) * 128],
                            aold[i % 2][:], start=True, stop=True,
                            skip_group_check=True)
                        t7.inc(ins)
        @blk.vector
        def _(vv):
            for b_ in range(2):
                for n in range(16):
                    i = b_ * 16 + n
                    for m in range(8):
                        ctx.wait(vv, t7, 8 * i + m + 1)
                        if 8 * i + m >= 2:
                            ctx.wait(vv, d7, 16 * (8 * i + m - 1))
                        v7.inc(vv.tensor_copy(
                            stg7[:, (m % 2) * 512:(m % 2 + 1) * 512],
                            ctx.PS[:, (m % 4) * 512:(m % 4 + 1) * 512]))
        @blk.sync
        def _(y):
            for b_ in range(2):
                for n in range(16):
                    i = b_ * 16 + n
                    g = n // 2
                    rg, sg = rk(b_, g)
                    for m in range(8):
                        ctx.wait(y, v7, 8 * i + m + 1)
                        d7.inc(y.dma_start(
                            out=rs_in.ap()[1024 * rg + m * 128:1024 * rg + (m + 1) * 128,
                                           1024 * sg + (n % 2) * 512:
                                           1024 * sg + (n % 2) * 512 + 512],
                            in_=stg7[:, (m % 2) * 512:(m % 2 + 1) * 512]), 16)
    ccs2 = ctx.sem()
    with ctx.block() as blk:
        @blk.gpsimd
        def _(g):
            ctx.wait(g, d7)
            ccs2.inc(g.collective_compute(
                "ReduceScatter", ALU.add, replica_groups=ALL8,
                ins=[rs_in.ap()], outs=[rs_out.ap()]))
    dr8 = ctx.sem(); v8 = ctx.sem()
    with ctx.block() as blk:
        @blk.gpsimd
        def _(g):
            ctx.wait(g, ccs2)
            for m in range(8):
                if m >= 2:
                    ctx.wait(g, v8, m - 1)
                dr8.inc(g.dma_start(out=rsld[m % 2][:],
                                    in_=rs_out.ap()[m * 128:(m + 1) * 128, :]), 16)
        @blk.vector
        def _(vv):
            for m in range(8):
                ctx.wait(vv, dr8, 16 * (m + 1))
                v8.inc(vv.tensor_tensor(x_sb[:, m * TLOC:(m + 1) * TLOC],
                                        x_sb[:, m * TLOC:(m + 1) * TLOC],
                                        rsld[m % 2][:], op=ALU.add))
    return v8

def ph_l1_moe(ctx, lnsc, x_sb, xn_bf, ln_done):
    from concourse.aot_env import allow_ici_all_to_all

    agx_in = ctx.dram("d_agx_in", [D, TLOC], BF16)
    agx_out = ctx.dram("d_agx_out", [8 * D, TLOC], BF16)
    a2g_in = ctx.dram("d_a2g_in", [8, TLOC], F32)
    a2g_out = ctx.dram("d_a2g_out", [8, TLOC], F32)
    rs_in_c = [[ctx.dram(f"d_rs_in{pp}_{tc}", [8 * 1024, 512], BF16)
                for tc in range(4)] for pp in range(4)]
    rs_out_c = [[ctx.dram(f"d_rs_out{pp}_{tc}", [1024, 512], BF16)
                 for tc in range(4)] for pp in range(4)]

    w1q = ctx.AR[:, 0:4096].bitcast(BF16)
    w2q = ctx.AR[:, 4096:8192].bitcast(BF16)
    h1g = ctx.AR[:, 8192:10240].bitcast(BF16)
    xnt = [ctx.AR[:, 10240 + s * 2048:10240 + (s + 1) * 2048].bitcast(BF16)
           for s in range(2)]
    grow = [ctx.AR[0:1, 14336 + s * 512:14336 + (s + 1) * 512] for s in range(2)]
    gbc_sb = ctx.AR[:, 15360:15872]
    stg = [ctx.AR[:, 15872 + s * 256:15872 + (s + 1) * 256].bitcast(BF16)
           for s in range(2)]
    g_fm = ctx.AR[0:8, 16896:18944]
    gtm = ctx.AR[:, 18944:18952]
    gsc = ctx.AR[:, 18952:18984]
    ident_f = ctx.AR[:, 19072:19200]
    rsld = [ctx.AR[:, 19200 + s * 2048:19200 + (s + 1) * 2048] for s in range(2)]
    b1col = ctx.sbuf("d_b1col", [128, 32], F32)
    b2sb = ctx.sbuf("d_b2sb", [1, 1024], BF16)
    ones512 = ctx.sbuf("d_ones512", [1, 512], BF16)
    rwT_sb = ctx.sbuf("d_rwT", [128, 64], BF16)

    dsx = ctx.sem(); dgs = ctx.sem()
    with ctx.block() as blk:
        @blk.sync
        def _(y):
            ctx.wait(y, ln_done)
            for c in range(NCH):
                dsx.inc(y.dma_start(out=agx_in.ap()[c * 128:(c + 1) * 128, :],
                                    in_=xn_bf[:, c * TLOC:(c + 1) * TLOC]), 16)
        @blk.gpsimd
        def _(g):
            g.memset(ones512[:], 1.0)
            for c in range(8):
                dgs.inc(g.dma_start(out=rwT_sb[:, c * 8:(c + 1) * 8],
                                    in_=ctx.rwT.ap()[c * 128:(c + 1) * 128, :]), 16)
            for c in range(32):
                dgs.inc(g.dma_start(out=b1col[:, c:c + 1],
                                    in_=ctx.b1row.ap()[0:1, c * 128:(c + 1) * 128]), 16)
            dgs.inc(g.dma_start(out=b2sb[:], in_=ctx.b2row.ap()), 16)
            dgs.inc(g.dma_start(out=ident_f[:], in_=ctx.ident.ap()), 16)

    tr = ctx.sem(); vr = ctx.sem(); sr = ctx.sem(); dgg = ctx.sem()
    VPI = 12
    with ctx.block() as blk:
        @blk.tensor
        def _(tt):
            ctx.wait(tt, ln_done)
            ctx.wait(tt, dgs)
            for i in range(16):
                if i >= 2:
                    ctx.wait(tt, vr, VPI * (i - 1))
                    ctx.wait(tt, sr, i - 1)
                pcol = (i % 2) * 512
                for c in range(8):
                    ins = tt.matmul(
                        ctx.PS[:, pcol:pcol + 8],
                        xn_bf[:, c * TLOC + i * 128: c * TLOC + (i + 1) * 128],
                        rwT_sb[:, c * 8:(c + 1) * 8],
                        start=(c == 0), stop=(c == 7))
                tr.inc(ins)
                if i >= 1:
                    ctx.wait(tt, vr, VPI * (i - 1) + 12)
                    tr.inc(tt.transpose(ctx.PS[0:8, 1024:1152], gtm[:, 0:8],
                                        ident_f[:]))
            ctx.wait(tt, vr, VPI * 16)
            tr.inc(tt.transpose(ctx.PS[0:8, 1024:1152], gtm[:, 0:8], ident_f[:]))
        @blk.vector
        def _(vv):
            for i in range(16):
                pcol = (i % 2) * 512
                ps = ctx.PS[:, pcol:pcol + 8]
                ctx.wait(vv, tr, 2 * i + 1 if i >= 1 else 1)
                m = vr.inc(vv.tensor_reduce(gsc[:, 16:17], ps, axis=AXL.X,
                                            op=ALU.max, negate=True))
                ctx.wait(vv, vr, m)
                m = vr.inc(vv.tensor_scalar(gsc[:, 8:16], ps, gsc[:, 16:17], 0.0,
                                            op0=ALU.add, op1=ALU.add))
                ctx.wait(vv, vr, m)
                m = vr.inc(vv.tensor_scalar(gsc[:, 24:32], gsc[:, 8:16], 0.0,
                                            -1e30, op0=ALU.is_ge, op1=ALU.mult))
                ctx.wait(vv, vr, m)
                m = vr.inc(vv.tensor_tensor(gsc[:, 24:32], gsc[:, 8:16],
                                            gsc[:, 24:32], op=ALU.add))
                ctx.wait(vv, vr, m)
                m = vr.inc(vv.tensor_reduce(gsc[:, 17:18], gsc[:, 24:32],
                                            axis=AXL.X, op=ALU.max, negate=True))
                ctx.wait(vv, vr, m)
                m = vr.inc(vv.tensor_scalar(gsc[:, 24:32], gsc[:, 8:16],
                                            gsc[:, 17:18], 1.0,
                                            op0=ALU.add, op1=ALU.mult))
                ctx.wait(vv, vr, m)
                m = vr.inc(vv.tensor_scalar(gsc[:, 8:16], gsc[:, 24:32], 0.0, 1.0,
                                            op0=ALU.is_ge, op1=ALU.mult))
                ctx.wait(vv, vr, m)
                ctx.wait(vv, sr, i + 1)
                m = vr.inc(vv.tensor_tensor(gsc[:, 24:32], gsc[:, 0:8],
                                            gsc[:, 8:16], op=ALU.mult))
                ctx.wait(vv, vr, m)
                m = vr.inc(vv.tensor_reduce(gsc[:, 18:19], gsc[:, 24:32],
                                            axis=AXL.X, op=ALU.add))
                ctx.wait(vv, vr, m)
                m = vr.inc(vv.reciprocal(gsc[:, 19:20], gsc[:, 18:19]))
                ctx.wait(vv, vr, m)
                if i >= 1:
                    ctx.wait(vv, tr, 2 * i + 1)
                m = vr.inc(vv.tensor_scalar(gtm[:, 0:8], gsc[:, 24:32],
                                            gsc[:, 19:20], 1.0,
                                            op0=ALU.mult, op1=ALU.mult))
                ctx.wait(vv, vr, m)
                if i >= 1:
                    m = vr.inc(vv.tensor_copy(g_fm[:, (i - 1) * 128:i * 128],
                                              ctx.PS[0:8, 1024:1152]))
                else:
                    m = vr.inc(vv.tensor_copy(gsc[0:1, 0:1], gsc[0:1, 0:1]))
                ctx.wait(vv, vr, m)
            ctx.wait(vv, tr, 32)
            vr.inc(vv.tensor_copy(g_fm[:, 15 * 128:16 * 128], ctx.PS[0:8, 1024:1152]))
        @blk.scalar
        def _(ss):
            for i in range(16):
                pcol = (i % 2) * 512
                ctx.wait(ss, vr, VPI * i + 1)
                sr.inc(ss.activation(gsc[:, 0:8], ctx.PS[:, pcol:pcol + 8],
                                     AF.Exp, bias=gsc[:, 16:17]))
        @blk.sync
        def _(y):
            ctx.wait(y, vr)
            dgg.inc(y.dma_start(out=a2g_in.ap(), in_=g_fm[:]), 16)

    ccx = ctx.sem()
    with ctx.block() as blk:
        @blk.gpsimd
        def _(g):
            ctx.wait(g, dsx)
            ccx.inc(g.collective_compute(
                "AllGather", ALU.bypass, replica_groups=ALL8,
                ins=[agx_in.ap()], outs=[agx_out.ap()]))
            ctx.wait(g, dgg)
            with allow_ici_all_to_all():
                ccx.inc(g.collective_compute(
                    "AllToAll", ALU.bypass, replica_groups=ALL8,
                    ins=[a2g_in.ap()], outs=[a2g_out.ap()]))

    ccr_s = [[ctx.sem() for tc in range(4)] for pp in range(4)]
    vAdd = ctx.sem(); dld9 = ctx.sem()

    def rb_loads(g, Pp, tc):
        ctx.wait(g, ccr_s[Pp][tc], 1)
        for mi in range(8):
            gl = (4 * Pp + tc) * 8 + mi
            if gl >= 2:
                ctx.wait(g, vAdd, gl - 1)
            dld9.inc(g.dma_start(out=rsld[gl % 2][:, 0:512],
                                 in_=rs_out_c[Pp][tc].ap()[mi * 128:(mi + 1) * 128, :]), 16)

    def rb_adds(vv, Pp, tc):
        for mi in range(8):
            gl = (4 * Pp + tc) * 8 + mi
            ctx.wait(vv, dld9, 16 * (gl + 1))
            vAdd.inc(vv.tensor_tensor(
                x_sb[:, mi * TLOC + 512 * tc:mi * TLOC + 512 * (tc + 1)],
                x_sb[:, mi * TLOC + 512 * tc:mi * TLOC + 512 * (tc + 1)],
                rsld[gl % 2][:, 0:512], op=ALU.add))

    dout_list = []
    for P in range(4):
        dwp = ctx.sem()
        dxl = ctx.sem(); tm = ctx.sem(); sg = ctx.sem(); vg = ctx.sem()
        dout_cs = [[ctx.sem() for _ in range(2)] for _ in range(4)]
        dout_list.append(dout_cs)
        with ctx.block() as blk:
            @blk.gpsimd
            def _(g, P=P):
                for m in range(8):
                    for c in range(8):
                        dwp.inc(g.dma_start(
                            out=w1q[:, c * 1024 + m * 128:c * 1024 + (m + 1) * 128],
                            in_=ctx.moe_w1t.ap()[c * 128:(c + 1) * 128,
                                                 1024 * P + m * 128:
                                                 1024 * P + (m + 1) * 128]), 16)
                for k in range(8):
                    dwp.inc(g.dma_start(
                        out=w2q[:, k * 1024:(k + 1) * 1024],
                        in_=ctx.moe_w2t.ap()[1024 * P + k * 128:
                                             1024 * P + (k + 1) * 128, :]), 16)
                if P >= 1:
                    for k in range(2):
                        ctx.wait(g, dout_list[P - 1][3][k], 16 * 32)
                    ccr_s[P - 1][3].inc(g.collective_compute(
                        "ReduceScatter", ALU.add, replica_groups=ALL8,
                        ins=[rs_in_c[P - 1][3].ap()],
                        outs=[rs_out_c[P - 1][3].ap()]))
                for tc in range(3):
                    if P >= 1:
                        rb_loads(g, P - 1, tc)
                    for k in range(2):
                        ctx.wait(g, dout_cs[tc][k], 16 * 32)
                    ccr_s[P][tc].inc(g.collective_compute(
                        "ReduceScatter", ALU.add, replica_groups=ALL8,
                        ins=[rs_in_c[P][tc].ap()],
                        outs=[rs_out_c[P][tc].ap()]))
                if P >= 1:
                    rb_loads(g, P - 1, 3)

            def emit_stores(y, it, P):
                tc, rnk = divmod(it, 8)
                for dm in range(8):
                    ctx.wait(y, vg, 9 * it + 2 + dm)
                    dout_cs[tc][dm % 2].inc(y.dma_start(
                        out=rs_in_c[P][tc].ap()[1024 * rnk + dm * 128:
                                                1024 * rnk + (dm + 1) * 128, :],
                        in_=stg[dm % 2][:]), 16)

            @blk.sync
            def _(y, P=P):
                ctx.wait(y, ccx)
                for it in range(32):
                    tc, rnk = divmod(it, 8)
                    if it >= 2:
                        ctx.wait(y, tm, max(0, 17 * it - 25))
                    for c in range(8):
                        dxl.inc(y.dma_start(
                            out=xnt[it % 2][:, c * 512:(c + 1) * 512],
                            in_=agx_out.ap()[D * rnk + c * 128:D * rnk + (c + 1) * 128,
                                             512 * tc:512 * (tc + 1)]), 16)
                    dxl.inc(y.dma_start(
                        out=grow[it % 2][:],
                        in_=a2g_out.ap()[rnk:rnk + 1, 512 * tc:512 * (tc + 1)]), 16)
                    if it >= 1:
                        emit_stores(y, it - 1, P)
                emit_stores(y, 31, P)
            @blk.tensor
            def _(tt, P=P):
                for it in range(32):
                    ctx.wait(tt, dxl, 16 * 9 * (it + 1))
                    if it >= 1:
                        ctx.wait(tt, vg, 9 * it - 8)
                    ins = tt.matmul(ctx.PS[:, 2048:2560], lnsc.ones_row_f[:],
                                    grow[it % 2][:], start=True, stop=True)
                    tm.inc(ins)
                    for m in range(8):
                        ctx.wait(tt, dwp, 16 * 8 * (m + 1))
                        if m >= 2:
                            ctx.wait(tt, sg, 8 * it + m - 1)
                        for c in range(8):
                            ins = tt.matmul(
                                ctx.PS[:, (m % 2) * 512:(m % 2) * 512 + 512],
                                w1q[:, c * 1024 + m * 128: c * 1024 + m * 128 + 128],
                                xnt[it % 2][:, c * 512:(c + 1) * 512],
                                start=(c == 0), stop=(c == 7))
                        tm.inc(ins)
                    ctx.wait(tt, sg, 8 * (it + 1))
                    ctx.wait(tt, dwp, 16 * 72)
                    for dm in range(8):
                        if dm >= 2:
                            ctx.wait(tt, vg, 9 * it + dm)
                        elif it >= 1:
                            ctx.wait(tt, vg, 9 * it + dm - 1)
                        for k in range(8):
                            ins = tt.matmul(
                                ctx.PS[:, 1024 + (dm % 2) * 512:1024 + (dm % 2) * 512 + 512],
                                w2q[:, k * 1024 + dm * 128: k * 1024 + dm * 128 + 128],
                                h1g[:, k * 512:(k + 1) * 512],
                                start=(k == 0), stop=(k == 7 and P != 0))
                        if P == 0:
                            ins = tt.matmul(
                                ctx.PS[:, 1024 + (dm % 2) * 512:1024 + (dm % 2) * 512 + 512],
                                b2sb[0:1, dm * 128:(dm + 1) * 128],
                                ones512[:], start=False, stop=True)
                        tm.inc(ins)
            @blk.scalar
            def _(ss, P=P):
                for it in range(32):
                    for m in range(8):
                        ctx.wait(ss, tm, 17 * it + 2 + m)
                        sg.inc(ss.activation(h1g[:, m * 512:(m + 1) * 512],
                                             ctx.PS[:, (m % 2) * 512:(m % 2) * 512 + 512],
                                             AF.Gelu, bias=b1col[:, 8 * P + m:8 * P + m + 1]))
            @blk.vector
            def _(vv, P=P):
                for it in range(32):
                    if P >= 1 and it % 8 == 0 and it >= 8:
                        rb_adds(vv, P - 1, it // 8 - 1)
                    ctx.wait(vv, tm, 17 * it + 1)
                    m = vg.inc(vv.tensor_copy(gbc_sb[:], ctx.PS[:, 2048:2560]))
                    ctx.wait(vv, vg, m)
                    for dm in range(8):
                        ctx.wait(vv, tm, 17 * it + 10 + dm)
                        if it * 8 + dm >= 2:
                            e2 = it * 8 + dm - 2
                            ctx.wait(vv, dout_cs[e2 // 64][dm % 2],
                                     16 * ((e2 % 64) // 2 + 1))
                        vg.inc(vv.tensor_tensor(
                            stg[dm % 2][:],
                            ctx.PS[:, 1024 + (dm % 2) * 512:1024 + (dm % 2) * 512 + 512],
                            gbc_sb[:], op=ALU.mult))
                if P >= 1:
                    rb_adds(vv, P - 1, 3)

    with ctx.block() as blk:
        @blk.gpsimd
        def _(g):
            for k in range(2):
                ctx.wait(g, dout_list[3][3][k], 16 * 32)
            ccr_s[3][3].inc(g.collective_compute(
                "ReduceScatter", ALU.add, replica_groups=ALL8,
                ins=[rs_in_c[3][3].ap()], outs=[rs_out_c[3][3].ap()]))
            for tc in range(4):
                rb_loads(g, 3, tc)
        @blk.vector
        def _(vv):
            for tc in range(4):
                rb_adds(vv, 3, tc)
    return vAdd

def shard_x(x):
    out = []
    for r in range(NCORES):
        b, j = r // 4, r % 4
        tok = np.concatenate([x[b, j * 1024:(j + 1) * 1024],
                              x[b, (7 - j) * 1024:(8 - j) * 1024]], axis=0)
        out.append(np.ascontiguousarray(tok.T))
    return out


def unshard_x(parts):
    x = np.empty((B, S, D), np.float32)
    for r in range(NCORES):
        b, j = r // 4, r % 4
        tok = parts[r].T
        x[b, j * 1024:(j + 1) * 1024] = tok[:1024]
        x[b, (7 - j) * 1024:(8 - j) * 1024] = tok[1024:]
    return x


def stage_ffn_weights(inp):
    G1R, G2R = np.asarray(inp["L0_ffn_G1_R"]), np.asarray(inp["L0_ffn_G2_R"])
    G1I, G2I = np.asarray(inp["L0_ffn_G1_I"]), np.asarray(inp["L0_ffn_G2_I"])

    def arep(G1):
        A = np.transpose(np.asarray(G1)[0], (1, 2, 0))
        return np.ascontiguousarray(
            np.broadcast_to(A[:, :, None, :, None],
                            (4, 32, 16, 32, 16)).reshape(4, 512, 512))

    def brep(G2):
        Bm = np.transpose(np.asarray(G2)[:, :, 0, :], (0, 2, 1))
        return np.ascontiguousarray(
            np.broadcast_to(Bm[:, None, :, None, :],
                            (4, 32, 16, 32, 16)).reshape(4, 512, 512))

    br = np.asarray(inp["L0_ffn_bias_real"]).reshape(512, 1)
    bi = np.asarray(inp["L0_ffn_bias_imag"]).reshape(512, 1)
    bias_ri = np.ascontiguousarray(
        np.concatenate([br, bi], 0).astype(np.float32))
    return (arep(G1R).astype(np.float32), brep(G2R).astype(np.float32),
            arep(G1I).astype(np.float32), brep(G2I).astype(np.float32), bias_ri)


def stage_l0_attn_weights(inp):
    wq = np.ascontiguousarray(np.asarray(inp["L0_attn_Wq"], np.float32))
    wk = np.ascontiguousarray(np.asarray(inp["L0_attn_Wk"], np.float32))
    wv = np.ascontiguousarray(np.asarray(inp["L0_attn_Wv"], np.float32))
    wo = np.ascontiguousarray(np.asarray(inp["L0_attn_Wo"], np.float32))
    bo_gb = np.ascontiguousarray(np.stack(
        [np.asarray(inp["L0_attn_bo"], np.float32),
         np.asarray(inp["L0_attn_gamma"], np.float32),
         np.asarray(inp["L0_attn_beta"], np.float32)], axis=1))
    return wq, wk, wv, wo, bo_gb



def build_stage_a():
    ctx = Ctx()
    nc = ctx.nc
    x_in = nc.dram_tensor("x_in", [D, TLOC], F32, kind="ExternalInput")
    x_out = nc.dram_tensor("x_out", [D, TLOC], F32, kind="ExternalOutput")
    ctx.wq = nc.dram_tensor("wq", [D, D], F32, kind="ExternalInput")
    ctx.wk = nc.dram_tensor("wk", [D, D], F32, kind="ExternalInput")
    ctx.wv = nc.dram_tensor("wv", [D, D], F32, kind="ExternalInput")
    ctx.wo = nc.dram_tensor("wo", [D, D], F32, kind="ExternalInput")
    ctx.bo_gb = nc.dram_tensor("bo_gb", [D, 3], F32, kind="ExternalInput")

    x_sb = ctx.sbuf("x_sb", [128, NCH * TLOC], F32)
    xn_bf = ctx.sbuf("xn_bf", [128, NCH * TLOC], BF16)
    ctx.arena_init()
    lnsc = alloc_ln_scratch(ctx)

    dsem = ctx.sem()
    ph_load_x(ctx, x_in, x_sb, dsem)
    csem = ph_init_consts(ctx, lnsc)
    ln_done = ph_ln(ctx, lnsc, x_sb, xn_bf, deps=(dsem, csem))
    fin = ph_l0_attn(ctx, lnsc, x_sb, xn_bf, ln_done)
    ph_store_x(ctx, x_sb, x_out, (fin,))
    ctx.close()
    return ctx


def run_stage_a(x_parts, inp):
    wq, wk, wv, wo, bo_gb = stage_l0_attn_weights(inp)
    ctx = build_stage_a()
    in_maps = [{"x_in": x_parts[r], "wq": wq, "wk": wk, "wv": wv, "wo": wo,
                "bo_gb": bo_gb} for r in range(NCORES)]
    res = run_bass_kernel_spmd(ctx.nc, in_maps, core_ids=list(range(NCORES)))
    return [res.results[r]["x_out"] for r in range(NCORES)]


def build_stage_b():
    ctx = Ctx()
    nc = ctx.nc
    x_in = nc.dram_tensor("x_in", [D, TLOC], F32, kind="ExternalInput")
    x_out = nc.dram_tensor("x_out", [D, TLOC], F32, kind="ExternalOutput")
    ctx.arep_r = nc.dram_tensor("arep_r", [4, 512, 512], F32, kind="ExternalInput")
    ctx.brep_r = nc.dram_tensor("brep_r", [4, 512, 512], F32, kind="ExternalInput")
    ctx.arep_i = nc.dram_tensor("arep_i", [4, 512, 512], F32, kind="ExternalInput")
    ctx.brep_i = nc.dram_tensor("brep_i", [4, 512, 512], F32, kind="ExternalInput")
    ctx.bias_ri = nc.dram_tensor("bias_ri", [D, 1], F32, kind="ExternalInput")

    x_sb = ctx.sbuf("x_sb", [128, NCH * TLOC], F32)
    xn_bf = ctx.sbuf("xn_bf", [128, NCH * TLOC], BF16)
    ctx.arena_init()
    lnsc = alloc_ln_scratch(ctx)

    dsem = ctx.sem()
    ph_load_x(ctx, x_in, x_sb, dsem)
    csem = ph_init_consts(ctx, lnsc)
    ln_done = ph_ln(ctx, lnsc, x_sb, xn_bf, deps=(dsem, csem))
    ev = ph_l0_ffn(ctx, lnsc, x_sb, xn_bf, ln_done)
    ph_store_x(ctx, x_sb, x_out, (ev,))
    ctx.close()
    return ctx


def run_stage_b(x_parts, inp):
    ar, br_, ai, bi_, bias = stage_ffn_weights(inp)
    ctx = build_stage_b()
    in_maps = [{"x_in": x_parts[r], "arep_r": ar, "brep_r": br_,
                "arep_i": ai, "brep_i": bi_, "bias_ri": bias}
               for r in range(NCORES)]
    res = run_bass_kernel_spmd(ctx.nc, in_maps, core_ids=list(range(NCORES)))
    return [res.results[r]["x_out"] for r in range(NCORES)]




def stage_l1_attn_weights(inp, r):
    wqkv = np.asarray(inp["L1_attn_w_qkv"], np.float32)
    wout = np.asarray(inp["L1_attn_w_out"], np.float32)
    wqT = np.ascontiguousarray(wqkv[0:1024].T)
    wkT = np.ascontiguousarray(wqkv[1024:2048].T)
    wvT = np.ascontiguousarray(wqkv[2048:3072].T)
    woT = np.ascontiguousarray(wout.T)
    woT_pair = np.ascontiguousarray(woT[128 * r:128 * (r + 1), :])
    return wqT, wkT, wvT, woT_pair


def make_cmask():
    m = np.zeros((4, 128, 512), np.float32)
    for o in range(4):
        for p in range(128):
            m[o, p, o * 128 + p:] = 1.0
    return m


def build_stage_c():
    ctx = Ctx()
    nc = ctx.nc
    x_in = nc.dram_tensor("x_in", [D, TLOC], F32, kind="ExternalInput")
    x_out = nc.dram_tensor("x_out", [D, TLOC], F32, kind="ExternalOutput")
    ctx.wqT = nc.dram_tensor("wqT", [D, D], F32, kind="ExternalInput")
    ctx.wkT = nc.dram_tensor("wkT", [D, D], F32, kind="ExternalInput")
    ctx.wvT = nc.dram_tensor("wvT", [D, D], F32, kind="ExternalInput")
    ctx.woT_pair = nc.dram_tensor("woT_pair", [128, D], F32, kind="ExternalInput")
    ctx.cmask = nc.dram_tensor("cmask", [4, 128, 512], F32, kind="ExternalInput")

    x_sb = ctx.sbuf("x_sb", [128, NCH * TLOC], F32)
    xn_bf = ctx.sbuf("xn_bf", [128, NCH * TLOC], BF16)
    ctx.arena_init()
    lnsc = alloc_ln_scratch(ctx)

    dsem = ctx.sem()
    ph_load_x(ctx, x_in, x_sb, dsem)
    csem = ph_init_consts(ctx, lnsc)
    ln_done = ph_ln(ctx, lnsc, x_sb, xn_bf, deps=(dsem, csem))
    fin = ph_l1_attn(ctx, lnsc, x_sb, xn_bf, ln_done)
    ph_store_x(ctx, x_sb, x_out, (fin,))
    ctx.close()
    return ctx


def run_stage_c(x_parts, inp):
    cm = make_cmask()
    ctx = build_stage_c()
    in_maps = []
    for r in range(NCORES):
        wqT, wkT, wvT, woT_pair = stage_l1_attn_weights(inp, r)
        in_maps.append({"x_in": x_parts[r], "wqT": wqT, "wkT": wkT,
                        "wvT": wvT, "woT_pair": woT_pair, "cmask": cm})
    res = run_bass_kernel_spmd(ctx.nc, in_maps, core_ids=list(range(NCORES)))
    return [res.results[r]["x_out"] for r in range(NCORES)]




def stage_moe_weights(inp, r):
    w1 = np.asarray(inp["L1_ffn_w1"], np.float32)[r]
    w2 = np.asarray(inp["L1_ffn_w2"], np.float32)[r]
    moe_w1t = np.ascontiguousarray(w1.T)
    moe_w2t = np.ascontiguousarray(w2.T)
    b1row = np.ascontiguousarray(np.asarray(inp["L1_ffn_b1"], np.float32)[r].reshape(1, 4096))
    b2row = np.ascontiguousarray(np.asarray(inp["L1_ffn_b2"], np.float32)[r].reshape(1, 1024))
    rwT = np.ascontiguousarray(np.asarray(inp["L1_ffn_router_w"], np.float32).T)
    return moe_w1t, moe_w2t, b1row, b2row, rwT


def build_stage_d():
    ctx = Ctx()
    nc = ctx.nc
    x_in = nc.dram_tensor("x_in", [D, TLOC], F32, kind="ExternalInput")
    x_out = nc.dram_tensor("x_out", [D, TLOC], F32, kind="ExternalOutput")
    ctx.moe_w1t = nc.dram_tensor("moe_w1t", [D, FMOE], F32, kind="ExternalInput")
    ctx.moe_w2t = nc.dram_tensor("moe_w2t", [FMOE, D], F32, kind="ExternalInput")
    ctx.b1row = nc.dram_tensor("b1row", [1, FMOE], F32, kind="ExternalInput")
    ctx.b2row = nc.dram_tensor("b2row", [1, D], F32, kind="ExternalInput")
    ctx.rwT = nc.dram_tensor("rwT", [D, 8], F32, kind="ExternalInput")
    ctx.ident = nc.dram_tensor("ident", [128, 128], F32, kind="ExternalInput")

    x_sb = ctx.sbuf("x_sb", [128, NCH * TLOC], F32)
    xn_bf = ctx.sbuf("xn_bf", [128, NCH * TLOC], BF16)
    ctx.arena_init()
    lnsc = alloc_ln_scratch(ctx)

    dsem = ctx.sem()
    ph_load_x(ctx, x_in, x_sb, dsem)
    csem = ph_init_consts(ctx, lnsc)
    ln_done = ph_ln(ctx, lnsc, x_sb, xn_bf, deps=(dsem, csem))
    fin = ph_l1_moe(ctx, lnsc, x_sb, xn_bf, ln_done)
    ph_store_x(ctx, x_sb, x_out, (fin,))
    ctx.close()
    return ctx


def run_stage_d(x_parts, inp):
    ctx = build_stage_d()
    ident = np.eye(128, dtype=np.float32)
    in_maps = []
    for r in range(NCORES):
        w1t, w2t, b1r, b2r, rwT = stage_moe_weights(inp, r)
        in_maps.append({"x_in": x_parts[r], "moe_w1t": w1t, "moe_w2t": w2t,
                        "b1row": b1r, "b2row": b2r, "rwT": rwT, "ident": ident})
    res = run_bass_kernel_spmd(ctx.nc, in_maps, core_ids=list(range(NCORES)))
    return [res.results[r]["x_out"] for r in range(NCORES)]




def build_fused():
    ctx = Ctx()
    nc = ctx.nc
    x_in = nc.dram_tensor("x_in", [D, TLOC], F32, kind="ExternalInput")
    x_out = nc.dram_tensor("x_out", [D, TLOC], F32, kind="ExternalOutput")
    ctx.wq = nc.dram_tensor("wq", [D, D], F32, kind="ExternalInput")
    ctx.wk = nc.dram_tensor("wk", [D, D], F32, kind="ExternalInput")
    ctx.wv = nc.dram_tensor("wv", [D, D], F32, kind="ExternalInput")
    ctx.wo = nc.dram_tensor("wo", [D, D], F32, kind="ExternalInput")
    ctx.bo_gb = nc.dram_tensor("bo_gb", [D, 3], F32, kind="ExternalInput")
    ctx.arep_r = nc.dram_tensor("arep_r", [4, 512, 512], F32, kind="ExternalInput")
    ctx.brep_r = nc.dram_tensor("brep_r", [4, 512, 512], F32, kind="ExternalInput")
    ctx.arep_i = nc.dram_tensor("arep_i", [4, 512, 512], F32, kind="ExternalInput")
    ctx.brep_i = nc.dram_tensor("brep_i", [4, 512, 512], F32, kind="ExternalInput")
    ctx.bias_ri = nc.dram_tensor("bias_ri", [D, 1], F32, kind="ExternalInput")
    ctx.wqT = nc.dram_tensor("wqT", [D, D], F32, kind="ExternalInput")
    ctx.wkT = nc.dram_tensor("wkT", [D, D], F32, kind="ExternalInput")
    ctx.wvT = nc.dram_tensor("wvT", [D, D], F32, kind="ExternalInput")
    ctx.woT_pair = nc.dram_tensor("woT_pair", [128, D], F32, kind="ExternalInput")
    ctx.cmask = nc.dram_tensor("cmask", [4, 128, 512], F32, kind="ExternalInput")
    ctx.moe_w1t = nc.dram_tensor("moe_w1t", [D, FMOE], F32, kind="ExternalInput")
    ctx.moe_w2t = nc.dram_tensor("moe_w2t", [FMOE, D], F32, kind="ExternalInput")
    ctx.b1row = nc.dram_tensor("b1row", [1, FMOE], F32, kind="ExternalInput")
    ctx.b2row = nc.dram_tensor("b2row", [1, D], F32, kind="ExternalInput")
    ctx.rwT = nc.dram_tensor("rwT", [D, 8], F32, kind="ExternalInput")
    ctx.ident = nc.dram_tensor("ident", [128, 128], F32, kind="ExternalInput")

    x_sb = ctx.sbuf("x_sb", [128, NCH * TLOC], F32)
    xn_bf = ctx.sbuf("xn_bf", [128, NCH * TLOC], BF16)
    ctx.arena_init()
    lnsc = alloc_ln_scratch(ctx)

    dsem = ctx.sem()
    ph_load_x(ctx, x_in, x_sb, dsem)
    csem = ph_init_consts(ctx, lnsc)
    ln0 = ph_ln(ctx, lnsc, x_sb, xn_bf, deps=(dsem, csem))
    finA = ph_l0_attn(ctx, lnsc, x_sb, xn_bf, ln0)
    ln1 = ph_ln(ctx, lnsc, x_sb, xn_bf, deps=(finA,))
    finB = ph_l0_ffn(ctx, lnsc, x_sb, xn_bf, ln1)
    ln2 = ph_ln(ctx, lnsc, x_sb, xn_bf, deps=(finB,))
    finC = ph_l1_attn(ctx, lnsc, x_sb, xn_bf, ln2)
    ln3 = ph_ln(ctx, lnsc, x_sb, xn_bf, deps=(finC,))
    finD = ph_l1_moe(ctx, lnsc, x_sb, xn_bf, ln3)
    ph_store_x(ctx, x_sb, x_out, (finD,))
    ctx.close()
    return ctx


def kernel(**inputs):
    parts = shard_x(np.asarray(inputs["L0_attn_x"], np.float32))
    wq, wk, wv, wo, bo_gb = stage_l0_attn_weights(inputs)
    ar, br_, ai, bi_, bias = stage_ffn_weights(inputs)
    cm = make_cmask()
    ident = np.eye(128, dtype=np.float32)
    ctx = build_fused()
    in_maps = []
    for r in range(NCORES):
        wqT, wkT, wvT, woT_pair = stage_l1_attn_weights(inputs, r)
        w1t, w2t, b1r, b2r, rwT = stage_moe_weights(inputs, r)
        in_maps.append({
            "x_in": parts[r], "wq": wq, "wk": wk, "wv": wv, "wo": wo,
            "bo_gb": bo_gb, "arep_r": ar, "brep_r": br_, "arep_i": ai,
            "brep_i": bi_, "bias_ri": bias, "wqT": wqT, "wkT": wkT,
            "wvT": wvT, "woT_pair": woT_pair, "cmask": cm,
            "moe_w1t": w1t, "moe_w2t": w2t, "b1row": b1r, "b2row": b2r,
            "rwT": rwT, "ident": ident})
    res = run_bass_kernel_spmd(ctx.nc, in_maps, core_ids=list(range(NCORES)))
    return unshard_x([res.results[r]["x_out"] for r in range(NCORES)])


_OLD = '''unused'''



# version 22
# speedup vs baseline: 1.0257x; 1.0257x over previous
import numpy as np
from contextlib import ExitStack

import concourse.bass as bass
import concourse.mybir as mybir
from concourse.bass_utils import run_bass_kernel_spmd

F32 = mybir.dt.float32
BF16 = mybir.dt.bfloat16
AF = mybir.ActivationFunctionType
ALU = mybir.AluOpType
AXL = mybir.AxisListType

B, S, D = 2, 8192, 1024
NCORES = 8
TLOC = 2048
NCH = D // 128
LN_EPS = 1e-5
NHEADS = 16
HDIM = 64
FMOE = 4096
NEXP = 8
GROUPS = [[0, 1, 2, 3], [4, 5, 6, 7]]
ALL8 = [list(range(8))]


class Sem:
    __slots__ = ("h", "hid", "base", "n")
    def __init__(self, handle, hid, base):
        self.h = handle
        self.hid = hid
        self.base = base
        self.n = 0
    def inc(self, instr, k=1):
        instr.then_inc(self.h, k)
        self.n += k
        return self.n


class Ctx:
    def __init__(self):
        self.nc = bass.Bass("TRN2", target_bir_lowering=False, debug=False)
        self.es = ExitStack()
        self._ns = 0
        self._waited = {}

    SEM_POOL = 72

    def sem(self, name=None):
        slot = self._ns % self.SEM_POOL
        self._ns += 1
        if not hasattr(self, "_pool"):
            self._pool = {}
            self._last = {}
        if slot not in self._pool:
            self._pool[slot] = self.es.enter_context(
                self.nc.semaphore(f"sp{slot}"))
            base = 0
        else:
            prev = self._last[slot]
            base = prev.base + prev.n
        s = Sem(self._pool[slot], slot, base)
        self._last[slot] = s
        return s

    def sbuf(self, name, shape, dtype):
        return self.es.enter_context(self.nc.sbuf_tensor(name, shape, dtype))

    def psum(self, name, shape, dtype=F32):
        return self.es.enter_context(self.nc.psum_tensor(name, shape, dtype))

    def dram(self, name, shape, dtype, **kw):
        return self.nc.dram_tensor(name, shape, dtype, **kw)

    def block(self):
        return self.nc.Block()

    def wait(self, eng, sem, val=None):
        if val is None:
            val = sem.n
        if val <= 0:
            return
        a = sem.base + val
        key = (eng.engine, sem.hid)
        if self._waited.get(key, -1) >= a:
            return
        eng.wait_ge(sem.h, a)
        self._waited[key] = a

    def arena_init(self, words=24576, ps_words=4096):
        self.AR = self.sbuf("ARENA", [128, words], F32)
        self.PS = self.psum("PSARENA", [128, ps_words], F32)

    def at(self, off_kb, cols, part=128):
        w0 = off_kb * 256
        return self.AR[0:part, w0:w0 + cols]

    def atb(self, off_kb, cols):
        w0 = off_kb * 256
        return self.AR[:, w0:w0 + cols // 2].bitcast(BF16)

    def ps(self, bank, cols=512, part=128):
        return self.PS[0:part, bank * 512:bank * 512 + cols]

    def close(self):
        self.es.close()
        return self.nc


class NS:
    pass



def ph_load_x(ctx, x_dram, x_sb, dsem):
    with ctx.block() as blk:
        @blk.sync
        def _(y):
            for c in range(NCH):
                dsem.inc(y.dma_start(out=x_sb[:, c * TLOC:(c + 1) * TLOC],
                                     in_=x_dram.ap()[c * 128:(c + 1) * 128, :]), 16)


def ph_store_x(ctx, x_sb, xo_dram, deps):
    dsem = ctx.sem()
    with ctx.block() as blk:
        @blk.sync
        def _(y):
            for s in deps:
                ctx.wait(y, s)
            for c in range(NCH):
                dsem.inc(y.dma_start(out=xo_dram.ap()[c * 128:(c + 1) * 128, :],
                                     in_=x_sb[:, c * TLOC:(c + 1) * TLOC]), 16)
            ctx.wait(y, dsem)
    return dsem



def alloc_ln_scratch(ctx, nt=TLOC):
    sc = NS()
    sc.sq_bf = ctx.atb(88, NCH * 512)
    rows = ctx.sbuf("ln_rows", [33, nt], F32)
    sc.murow = rows[0:1, 0:nt]
    sc.rowA = rows[32:33, 0:nt]
    sc.rowB = ctx.AR[32:33, 84 * 256:84 * 256 + nt]
    sc.ones_col_bf = ctx.sbuf("ones_col_bf", [128, 1], BF16)
    ones_rows = ctx.sbuf("ones_rows", [65, 128], F32)
    sc.ones_rows = ones_rows
    sc.ones_row_f = ones_rows[0:1, :]
    sc.ones_row32 = ones_rows[32:33, :]
    sc.tmp512 = ctx.at(86, 512)
    return sc


def ph_init_consts(ctx, sc):
    g = ctx.sem()
    with ctx.block() as blk:
        @blk.gpsimd
        def _(gg):
            gg.memset(sc.ones_col_bf[:], 1.0)
            g.inc(gg.memset(sc.ones_rows[:], 1.0))
    return g


def _ln_stats(ctx, blk, sc, src_getter, deps, src_is_bf, nt=TLOC):
    v = ctx.sem(); s = ctx.sem(); t = ctx.sem()
    ntb = nt // 512
    ps_sum = ctx.ps(0)
    ps_sq = ctx.ps(1)
    for tb in range(ntb):
        sl = slice(tb * 512, (tb + 1) * 512)
        if not src_is_bf:
            pass
        @blk.scalar
        def _(ss, tb=tb):
            for d in deps:
                ctx.wait(ss, d)
            if tb > 0:
                ctx.wait(ss, t, tb)
            for c in range(NCH):
                ins = ss.activation(sc.sq_bf[:, c * 512:(c + 1) * 512],
                                    src_getter(c, tb), AF.Square)
            s.inc(ins)
        @blk.tensor
        def _(tt, tb=tb):
            for d in deps:
                ctx.wait(tt, d)
            ctx.wait(tt, s, tb + 1)
            if tb > 0:
                ctx.wait(tt, v, 2 * tb)
            for c in range(NCH):
                tt.matmul(ps_sum[0:1, :], sc.ones_col_bf[:], _bf_src(ctx, sc, src_getter, c, tb),
                          start=(c == 0), stop=(c == NCH - 1))
            for c in range(NCH):
                ins = tt.matmul(ps_sq[0:1, :], sc.ones_col_bf[:],
                                sc.sq_bf[:, c * 512:(c + 1) * 512],
                                start=(c == 0), stop=(c == NCH - 1))
            t.inc(ins)
        @blk.vector
        def _(vv, tb=tb, sl=sl):
            ctx.wait(vv, t)
            v.inc(vv.tensor_scalar(sc.murow[:, sl], ps_sum[0:1, :],
                                   1.0 / D, 0.0, op0=ALU.mult, op1=ALU.add))
            v.inc(vv.tensor_scalar(sc.rowA[:, sl], ps_sq[0:1, :],
                                   1.0 / D, LN_EPS, op0=ALU.mult, op1=ALU.add))
    @blk.vector
    def _(vv):
        ctx.wait(vv, v)
        m = v.inc(vv.tensor_tensor(sc.rowB[:, :nt], sc.murow[:, :nt],
                                   sc.murow[:, :nt], op=ALU.mult))
        ctx.wait(vv, v, m)
        v.inc(vv.tensor_tensor(sc.rowA[:, :nt], sc.rowA[:, :nt],
                               sc.rowB[:, :nt], op=ALU.subtract))
    @blk.scalar
    def _(ss):
        ctx.wait(ss, v)
        m = s.inc(ss.activation(sc.rowB[:, :nt], sc.rowA[:, :nt], AF.Ln))
        ctx.wait(ss, s, m)
        s.inc(ss.activation(sc.rowA[:, :nt], sc.rowB[:, :nt], AF.Exp, scale=-0.5))
    return v, s, t


def _bf_src(ctx, sc, src_getter, c, tb):
    ap = src_getter(c, tb)
    return ap


def ph_ln(ctx, sc, x_sb, xn_bf, deps=(), nt=TLOC):
    ntb = nt // 512
    fin = ctx.sem()
    with ctx.block() as blk:
        @blk.vector
        def _(vv):
            for d in deps:
                ctx.wait(vv, d)
            vv.tensor_copy(xn_bf[:, :NCH * nt], x_sb[:, :NCH * nt])
    with ctx.block() as blk:
        v, s, t = _ln_stats(ctx, blk, sc,
                            lambda c, tb: xn_bf[:, c * nt + tb * 512:
                                                c * nt + (tb + 1) * 512],
                            deps=(), src_is_bf=True, nt=nt)
        bc1 = ctx.ps(2); bc2 = ctx.ps(3)
        for tb in range(ntb):
            sl = slice(tb * 512, (tb + 1) * 512)
            @blk.tensor
            def _(tt, sl=sl):
                ctx.wait(tt, s)
                ctx.wait(tt, v)
                tt.matmul(bc1[:], sc.ones_row_f[:], sc.murow[:, sl],
                          start=True, stop=True)
                t.inc(tt.matmul(bc2[:], sc.ones_row32[:], sc.rowA[:, sl],
                                start=True, stop=True))
            @blk.vector
            def _(vv, sl=sl, tb=tb):
                ctx.wait(vv, t)
                for c in range(NCH):
                    xsl = slice(c * nt + tb * 512, c * nt + (tb + 1) * 512)
                    ctx.wait(vv, v)
                    m = v.inc(vv.tensor_tensor(sc.tmp512[:], x_sb[:, xsl],
                                               bc1[:], op=ALU.subtract))
                    ctx.wait(vv, v, m)
                    m = v.inc(vv.tensor_tensor(xn_bf[:, xsl], sc.tmp512[:],
                                               bc2[:], op=ALU.mult))
        @blk.vector
        def _(vv):
            fin.inc(vv.tensor_copy(sc.tmp512[0:1, 0:1], sc.tmp512[0:1, 0:1]))
    return fin


def ph_ln_affine_residual(ctx, sc, u_bf, x_sb, gb_col, deps=(), nt=TLOC):
    ntb = nt // 512
    fin = ctx.sem()
    with ctx.block() as blk:
        v, s, t = _ln_stats(ctx, blk, sc,
                            lambda c, tb: u_bf[:, c * nt + tb * 512:
                                               c * nt + (tb + 1) * 512],
                            deps=deps, src_is_bf=True, nt=nt)
        bc1 = ctx.ps(2); bc2 = ctx.ps(3)
        for tb in range(ntb):
            sl = slice(tb * 512, (tb + 1) * 512)
            @blk.tensor
            def _(tt, sl=sl):
                ctx.wait(tt, s)
                ctx.wait(tt, v)
                tt.matmul(bc1[:], sc.ones_row_f[:], sc.murow[:, sl],
                          start=True, stop=True)
                t.inc(tt.matmul(bc2[:], sc.ones_row32[:], sc.rowA[:, sl],
                                start=True, stop=True))
            @blk.vector
            def _(vv, sl=sl, tb=tb):
                ctx.wait(vv, t)
                for c in range(NCH):
                    xsl = slice(c * nt + tb * 512, c * nt + (tb + 1) * 512)
                    ctx.wait(vv, v)
                    m = v.inc(vv.tensor_tensor(sc.tmp512[:], u_bf[:, xsl],
                                               bc1[:], op=ALU.subtract))
                    ctx.wait(vv, v, m)
                    m = v.inc(vv.tensor_tensor(sc.tmp512[:], sc.tmp512[:],
                                               bc2[:], op=ALU.mult))
                    ctx.wait(vv, v, m)
                    m = v.inc(vv.tensor_scalar(sc.tmp512[:], sc.tmp512[:],
                                               gb_col[:, 3 * c + 1:3 * c + 2],
                                               gb_col[:, 3 * c + 2:3 * c + 3],
                                               op0=ALU.mult, op1=ALU.add))
                    ctx.wait(vv, v, m)
                    v.inc(vv.tensor_tensor(x_sb[:, xsl], x_sb[:, xsl],
                                           sc.tmp512[:], op=ALU.add))
        @blk.vector
        def _(vv):
            fin.inc(vv.tensor_copy(sc.tmp512[0:1, 0:1], sc.tmp512[0:1, 0:1]))
    return fin



def load_weight_cm(ctx, eng, dram_t, sb, sem, row0=0, nch=8, dout=1024):
    for c in range(nch):
        sem.inc(eng.dma_start(
            out=sb[:, c * dout:(c + 1) * dout],
            in_=dram_t.ap()[row0 + c * 128:row0 + (c + 1) * 128, :]), 16)



def ph_l0_attn(ctx, lnsc, x_sb, xn_bf, ln_done):
    kexp_dram = ctx.dram("l0_kexp", [D, TLOC], BF16)
    qn_dram = ctx.dram("l0_qn", [TLOC, D], BF16)
    ar_in = ctx.dram("l0_ar_in", [D, 1026], F32)
    ar_out = ctx.dram("l0_ar_out", [D, 1026], F32)
    ck_bounce = ctx.dram("l0_ckb", [1, 1024], F32)

    wk_sb = ctx.atb(0, 8 * 1024)
    wq_sb = ctx.atb(0, 8 * 1024)
    ktd = ctx.atb(0, 16 * 1024)
    qn_dt = ctx.atb(0, 8 * TLOC)
    wv_sb = ctx.atb(16, 8 * 1024)
    wo_sb = ctx.atb(0, 8 * 1024)
    vte = ctx.atb(32, 16 * 1024)
    a_et = ctx.atb(32, 8 * TLOC)
    krot = ctx.atb(64, 2 * 2048)
    stg = ctx.at(64, 2048)
    qrot = ctx.atb(72, 2 * 1024)
    crot = ctx.at(72, 2048)
    wksq = ctx.atb(80, 1024)
    ckrow = ctx.at(82, 1024, part=1)
    c_bf = ctx.atb(80, 8 * 1024)

    negck = ctx.sbuf("l0_negck", [128, 8], F32)
    zacc = ctx.sbuf("l0_zacc", [128, 8], F32)
    zinv = ctx.sbuf("l0_zinv", [128, 8], F32)
    bias3 = ctx.sbuf("l0_bias3", [128, 24], F32)
    qsc = ctx.sbuf("l0_qsc", [128, 8], F32)
    u_bf = xn_bf

    big0 = ctx.PS[:, 0:2048]
    big1 = ctx.PS[:, 2048:4096]
    Q = [ctx.PS[:, 0:512], ctx.PS[:, 512:1024],
         ctx.PS[:, 1024:1536], ctx.PS[:, 1536:2048]]

    dg = ctx.sem(); dy = ctx.sem()

    with ctx.block() as blk:
        @blk.gpsimd
        def _(g):
            load_weight_cm(ctx, g, ctx.wk, wk_sb, dg)
            load_weight_cm(ctx, g, ctx.wv, wv_sb, dg)
            for c in range(8):
                for j in range(3):
                    dg.inc(g.dma_start(
                        out=bias3[:, 3 * c + j:3 * c + j + 1],
                        in_=ctx.bo_gb.ap()[c * 128:(c + 1) * 128, j:j + 1],
                        allow_slow_non_contiguous=True), 16)
    v0 = ctx.sem(); s0 = ctx.sem(); t0 = ctx.sem()
    with ctx.block() as blk:
        @blk.scalar
        def _(ss):
            ctx.wait(ss, dg, 16 * 8)
            for c in range(8):
                if c > 0:
                    ctx.wait(ss, t0, c)
                s0.inc(ss.activation(wksq[:], wk_sb[:, c * 1024:(c + 1) * 1024],
                                     AF.Square))
        @blk.tensor
        def _(tt):
            for c in range(8):
                ctx.wait(tt, s0, c + 1)
                tt.matmul(ctx.PS[0:1, 0:512], lnsc.ones_col_bf[:],
                          wksq[:, 0:512], start=(c == 0), stop=(c == 7))
                ins = tt.matmul(ctx.PS[0:1, 512:1024], lnsc.ones_col_bf[:],
                                wksq[:, 512:1024], start=(c == 0), stop=(c == 7))
                t0.inc(ins)
        @blk.scalar
        def _(ss):
            ctx.wait(ss, t0, 8)
            s0.inc(ss.activation(ckrow[0:1, :], ctx.PS[0:1, 0:1024], AF.Sqrt,
                                 scale=1024.0))
        @blk.vector
        def _(vv):
            ctx.wait(vv, s0)
            v0.inc(vv.tensor_scalar(ckrow[0:1, :], ckrow[0:1, :], -1.0, 0.0,
                                    op0=ALU.mult, op1=ALU.add))
        @blk.sync
        def _(y):
            ctx.wait(y, v0)
            dy.inc(y.dma_start(out=ck_bounce.ap(), in_=ckrow[0:1, :]), 16)
            ctx.wait(y, dy)
            for c in range(8):
                dy.inc(y.dma_start(out=negck[:, c:c + 1],
                                   in_=ck_bounce.ap()[0:1, c * 128:(c + 1) * 128]), 16)

    tk = ctx.sem(); sk = ctx.sem(); dk = ctx.sem()
    with ctx.block() as blk:
        @blk.tensor
        def _(tt):
            ctx.wait(tt, ln_done)
            ctx.wait(tt, dg)
            ctx.wait(tt, dy)
            for m in range(8):
                if m >= 2:
                    ctx.wait(tt, sk, m - 1)
                ps = big0 if m % 2 == 0 else big1
                for n in range(4):
                    for c in range(8):
                        ins = tt.matmul(
                            ps[:, n * 512:(n + 1) * 512],
                            wk_sb[:, c * 1024 + m * 128: c * 1024 + m * 128 + 128],
                            xn_bf[:, c * TLOC + n * 512: c * TLOC + (n + 1) * 512],
                            start=(c == 0), stop=(c == 7))
                tk.inc(ins)
        @blk.scalar
        def _(ss):
            for m in range(8):
                ctx.wait(ss, tk, m + 1)
                if m >= 2:
                    ctx.wait(ss, dk, 16 * (m - 1))
                ps = big0 if m % 2 == 0 else big1
                sk.inc(ss.activation(krot[:, (m % 2) * 2048:(m % 2 + 1) * 2048],
                                     ps, AF.Exp, bias=negck[:, m:m + 1],
                                     accum_out=zacc[:, m:m + 1]))
        @blk.sync
        def _(y):
            for m in range(8):
                ctx.wait(y, sk, m + 1)
                dk.inc(y.dma_start(out=kexp_dram.ap()[m * 128:(m + 1) * 128, :],
                                   in_=krot[:, (m % 2) * 2048:(m % 2 + 1) * 2048]), 16)

    tv = ctx.sem(); vv_ = ctx.sem(); dgq = ctx.sem()
    with ctx.block() as blk:
        @blk.gpsimd
        def _(g):
            load_weight_cm(ctx, g, ctx.wq, wq_sb, dgq)
        @blk.tensor
        def _(tt):
            for i in range(16):
                if i >= 2:
                    ctx.wait(tt, vv_, i - 1)
                ps = big0 if i % 2 == 0 else big1
                for h in range(2):
                    for c in range(8):
                        ins = tt.matmul(
                            ps[:, h * 512:(h + 1) * 512],
                            xn_bf[:, c * TLOC + i * 128: c * TLOC + (i + 1) * 128],
                            wv_sb[:, c * 1024 + h * 512: c * 1024 + (h + 1) * 512],
                            start=(c == 0), stop=(c == 7))
                tv.inc(ins)
        @blk.vector
        def _(vv):
            for i in range(16):
                ctx.wait(vv, tv, i + 1)
                ps = big0 if i % 2 == 0 else big1
                vv_.inc(vv.tensor_copy(vte[:, i * 1024:(i + 1) * 1024],
                                       ps[:, 0:1024]))

    tq = ctx.sem(); vq = ctx.sem(); sq = ctx.sem(); vq2 = ctx.sem(); dq = ctx.sem()
    with ctx.block() as blk:
        @blk.tensor
        def _(tt):
            ctx.wait(tt, dgq)
            for i in range(16):
                if i >= 2:
                    ctx.wait(tt, sq, i - 1)
                ps = big0 if i % 2 == 0 else big1
                for h in range(2):
                    for c in range(8):
                        ins = tt.matmul(
                            ps[:, h * 512:(h + 1) * 512],
                            xn_bf[:, c * TLOC + i * 128: c * TLOC + (i + 1) * 128],
                            wq_sb[:, c * 1024 + h * 512: c * 1024 + (h + 1) * 512],
                            start=(c == 0), stop=(c == 7))
                tq.inc(ins)
        @blk.scalar
        def _(ss):
            for i in range(16):
                p = i % 2
                ctx.wait(ss, vq, i + 1)
                if i >= 2:
                    ctx.wait(ss, vq2, 2 * (i - 1))
                ps = big0 if p == 0 else big1
                sq.inc(ss.activation(qrot[:, p * 1024:(p + 1) * 1024],
                                     ps[:, 0:1024], AF.Exp,
                                     bias=qsc[:, 4 * p:4 * p + 1],
                                     accum_out=qsc[:, 4 * p + 1:4 * p + 2]))
        @blk.vector
        def _(vv):
            def recip_scale(j):
                p = j % 2
                ctx.wait(vv, sq, j + 1)
                m = vq2.inc(vv.reciprocal(qsc[:, 4 * p + 2:4 * p + 3],
                                          qsc[:, 4 * p + 1:4 * p + 2]))
                ctx.wait(vv, vq2, m)
                if j >= 2:
                    ctx.wait(vv, dq, 16 * (j - 1))
                vq2.inc(vv.tensor_scalar(qrot[:, p * 1024:(p + 1) * 1024],
                                         qrot[:, p * 1024:(p + 1) * 1024],
                                         qsc[:, 4 * p + 2:4 * p + 3], 1.0 / 32.0,
                                         op0=ALU.mult, op1=ALU.mult))
            for i in range(16):
                p = i % 2
                ctx.wait(vv, tq, i + 1)
                if i >= 2:
                    ctx.wait(vv, sq, i - 1)
                ps = big0 if p == 0 else big1
                vq.inc(vv.tensor_reduce(qsc[:, 4 * p:4 * p + 1], ps[:, 0:1024],
                                        axis=AXL.X, op=ALU.max, negate=True))
                if i >= 1:
                    recip_scale(i - 1)
            recip_scale(15)
        @blk.sync
        def _(y):
            for i in range(16):
                ctx.wait(y, vq2, 2 * i + 2)
                dq.inc(y.dma_start(out=qn_dram.ap()[i * 128:(i + 1) * 128, :],
                                   in_=qrot[:, (i % 2) * 1024:(i % 2 + 1) * 1024]), 16)

    dktd = ctx.sem(); tc_ = ctx.sem(); vc = ctx.sem(); dc = ctx.sem()
    with ctx.block() as blk:
        @blk.scalar
        def _(ss):
            ctx.wait(ss, dk)
            for i in range(16):
                dktd.inc(ss.dma_start(out=ktd[:, i * 1024:(i + 1) * 1024],
                                      in_=kexp_dram.ap()[:, i * 128:(i + 1) * 128],
                                      transpose=True), 16)
        @blk.tensor
        def _(tt):
            ctx.wait(tt, dktd)
            for m in range(8):
                for n in range(2):
                    it = m * 2 + n
                    if it >= 4:
                        ctx.wait(tt, vc, it - 3)
                    for i in range(16):
                        ins = tt.matmul(
                            Q[it % 4],
                            ktd[:, i * 1024 + m * 128: i * 1024 + m * 128 + 128],
                            vte[:, i * 1024 + n * 512: i * 1024 + (n + 1) * 512],
                            start=(i == 0), stop=(i == 15))
                    tc_.inc(ins)
        @blk.vector
        def _(vv):
            for it in range(16):
                ctx.wait(vv, tc_, it + 1)
                if it >= 4:
                    ctx.wait(vv, dc, 16 * (it - 3))
                vc.inc(vv.tensor_copy(stg[:, (it % 4) * 512:(it % 4 + 1) * 512],
                                      Q[it % 4]))
        @blk.sync
        def _(y):
            for it in range(16):
                m, n = divmod(it, 2)
                ctx.wait(y, vc, it + 1)
                dc.inc(y.dma_start(
                    out=ar_in.ap()[m * 128:(m + 1) * 128, n * 512:(n + 1) * 512],
                    in_=stg[:, (it % 4) * 512:(it % 4 + 1) * 512]), 16)
            ctx.wait(y, sk)
            for m in range(8):
                dc.inc(y.dma_start(out=ar_in.ap()[m * 128:(m + 1) * 128, 1024:1025],
                                   in_=zacc[:, m:m + 1],
                                   allow_slow_non_contiguous=True), 16)

    ccs = ctx.sem()
    with ctx.block() as blk:
        @blk.gpsimd
        def _(g):
            ctx.wait(g, dc)
            ccs.inc(g.collective_compute(
                "AllReduce", ALU.add, replica_groups=GROUPS,
                ins=[ar_in.ap()], outs=[ar_out.ap()]))

    dr = ctx.sem(); vr = ctx.sem(); dqn = ctx.sem()
    with ctx.block() as blk:
        @blk.sync
        def _(y):
            ctx.wait(y, ccs)
            for m in range(8):
                dr.inc(y.dma_start(out=zinv[:, m:m + 1],
                                   in_=ar_out.ap()[m * 128:(m + 1) * 128, 1024:1025],
                                   allow_slow_non_contiguous=True), 16)
            for m in range(8):
                if m >= 2:
                    ctx.wait(y, vr, m)
                dr.inc(y.dma_start(out=crot[:, (m % 2) * 1024:(m % 2 + 1) * 1024],
                                   in_=ar_out.ap()[m * 128:(m + 1) * 128, 0:1024]), 16)
        @blk.vector
        def _(vv):
            ctx.wait(vv, dr, 16 * 8)
            vr.inc(vv.reciprocal(zinv[:], zinv[:]))
            for m in range(8):
                ctx.wait(vv, dr, 16 * (9 + m))
                vr.inc(vv.tensor_scalar(c_bf[:, m * 1024:(m + 1) * 1024],
                                        crot[:, (m % 2) * 1024:(m % 2 + 1) * 1024],
                                        zinv[:, m:m + 1], 1.0,
                                        op0=ALU.mult, op1=ALU.mult))
        @blk.scalar
        def _(ss):
            ctx.wait(ss, dq)
            for c in range(8):
                dqn.inc(ss.dma_start(out=qn_dt[:, c * TLOC:(c + 1) * TLOC],
                                     in_=qn_dram.ap()[:, c * 128:(c + 1) * 128],
                                     transpose=True), 16)

    ta = ctx.sem(); va = ctx.sem()
    with ctx.block() as blk:
        @blk.tensor
        def _(tt):
            ctx.wait(tt, dqn)
            ctx.wait(tt, vr)
            for m in range(8):
                for n in range(4):
                    it = m * 4 + n
                    if it >= 4:
                        ctx.wait(tt, va, it - 3)
                    for c in range(8):
                        ins = tt.matmul(
                            Q[it % 4],
                            c_bf[:, c * 1024 + m * 128: c * 1024 + m * 128 + 128],
                            qn_dt[:, c * TLOC + n * 512: c * TLOC + (n + 1) * 512],
                            start=(c == 0), stop=(c == 7))
                    ta.inc(ins)
        @blk.vector
        def _(vv):
            for it in range(32):
                m, n = divmod(it, 4)
                ctx.wait(vv, ta, it + 1)
                va.inc(vv.tensor_copy(
                    a_et[:, m * TLOC + n * 512: m * TLOC + (n + 1) * 512],
                    Q[it % 4]))

    dgo = ctx.sem(); to = ctx.sem(); vo = ctx.sem()
    with ctx.block() as blk:
        @blk.gpsimd
        def _(g):
            load_weight_cm(ctx, g, ctx.wo, wo_sb, dgo)
    with ctx.block() as blk:
        @blk.tensor
        def _(tt):
            ctx.wait(tt, dgo)
            for n in range(4):
                for m in range(8):
                    it = n * 8 + m
                    if it >= 4:
                        ctx.wait(tt, vo, 2 * (it - 3))
                    for c in range(8):
                        ins = tt.matmul(
                            Q[it % 4],
                            wo_sb[:, c * 1024 + m * 128: c * 1024 + m * 128 + 128],
                            a_et[:, c * TLOC + n * 512: c * TLOC + (n + 1) * 512],
                            start=(c == 0), stop=(c == 7))
                    to.inc(ins)
        @blk.vector
        def _(vv):
            for n in range(4):
                for m in range(8):
                    it = n * 8 + m
                    ctx.wait(vv, to, it + 1)
                    xsl = slice(m * TLOC + n * 512, m * TLOC + (n + 1) * 512)
                    mv = vo.inc(vv.scalar_tensor_tensor(
                        stg[:, 0:512], Q[it % 4], bias3[:, 3 * m:3 * m + 1],
                        u_bf[:, xsl], op0=ALU.add, op1=ALU.add))
                    ctx.wait(vv, vo, mv)
                    vo.inc(vv.tensor_copy(u_bf[:, xsl], stg[:, 0:512]))
    return ph_ln_affine_residual(ctx, lnsc, u_bf, x_sb, bias3, deps=(vo,))



def ph_l0_ffn(ctx, lnsc, x_sb, xn_bf, xn_ready):
    wacc_r = ctx.at(0, 2048)
    wacc_i = ctx.at(8, 2048)
    rep_a = ctx.at(16, 2048)
    rep_b = ctx.at(24, 2048)
    rep_t = ctx.at(32, 2048)
    wr_bf = ctx.atb(40, 2048)
    wi_bf = ctx.atb(44, 2048)
    nwi_bf = ctx.atb(48, 2048)
    bias_col = ctx.sbuf("ffn_bias_col", [128, 8], F32)
    ps_a = ctx.ps(0); ps_b = ctx.ps(1)

    dg = ctx.sem(); v = ctx.sem(); t = ctx.sem(); ev = ctx.sem()
    for (arep, brep, wacc) in ((ctx.arep_r, ctx.brep_r, wacc_r),
                               (ctx.arep_i, ctx.brep_i, wacc_i)):
        for cc in range(4):
            with ctx.block() as blk:
                @blk.gpsimd
                def _(gg, arep=arep, brep=brep, cc=cc):
                    ctx.wait(gg, v)
                    for k in range(4):
                        dg.inc(gg.dma_start(
                            out=rep_a[:, k * 512:(k + 1) * 512],
                            in_=arep.ap()[cc, k * 128:(k + 1) * 128, :]), 16)
                        dg.inc(gg.dma_start(
                            out=rep_b[:, k * 512:(k + 1) * 512],
                            in_=brep.ap()[cc, k * 128:(k + 1) * 128, :]), 16)
                @blk.vector
                def _(vv, cc=cc, wacc=wacc):
                    ctx.wait(vv, dg)
                    if cc == 0:
                        v.inc(vv.tensor_tensor(wacc[:], rep_a[:], rep_b[:],
                                               op=ALU.mult))
                    else:
                        m = v.inc(vv.tensor_tensor(rep_t[:], rep_a[:],
                                                   rep_b[:], op=ALU.mult))
                        ctx.wait(vv, v, m)
                        v.inc(vv.tensor_tensor(wacc[:], wacc[:], rep_t[:],
                                               op=ALU.add))
    with ctx.block() as blk:
        @blk.gpsimd
        def _(gg):
            for c in range(8):
                dg.inc(gg.dma_start(out=bias_col[:, c:c + 1],
                                    in_=ctx.bias_ri.ap()[c * 128:(c + 1) * 128, :]), 16)
        @blk.vector
        def _(vv):
            ctx.wait(vv, v)
            vv.tensor_copy(wr_bf[:], wacc_r[:])
            vv.tensor_copy(wi_bf[:], wacc_i[:])
            v.inc(vv.tensor_scalar(nwi_bf[:], wacc_i[:], -1.0, 0.0,
                                   op0=ALU.mult, op1=ALU.add))
    with ctx.block() as blk:
        @blk.tensor
        def _(tt):
            ctx.wait(tt, v)
            ctx.wait(tt, dg)
            ctx.wait(tt, xn_ready)
            for it in range(32):
                m, n = divmod(it, 4)
                real = m < 4
                fcols = (m % 4) * 128
                ps = ps_a if it % 2 == 0 else ps_b
                if it >= 2:
                    ctx.wait(tt, ev, it - 1)
                first = wr_bf if real else wi_bf
                second = nwi_bf if real else wr_bf
                for c in range(4):
                    tt.matmul(ps, first[:, c * 512 + fcols:c * 512 + fcols + 128],
                              xn_bf[:, c * TLOC + n * 512: c * TLOC + (n + 1) * 512],
                              start=(c == 0), stop=False)
                for c in range(4):
                    ins = tt.matmul(
                        ps, second[:, c * 512 + fcols:c * 512 + fcols + 128],
                        xn_bf[:, (4 + c) * TLOC + n * 512: (4 + c) * TLOC + (n + 1) * 512],
                        start=False, stop=(c == 3))
                t.inc(ins)
        @blk.vector
        def _(vv):
            for it in range(32):
                m, n = divmod(it, 4)
                ps = ps_a if it % 2 == 0 else ps_b
                ctx.wait(vv, t, it + 1)
                xsl = slice(m * TLOC + n * 512, m * TLOC + (n + 1) * 512)
                ev.inc(vv.scalar_tensor_tensor(x_sb[:, xsl], ps,
                                               bias_col[:, m:m + 1],
                                               x_sb[:, xsl],
                                               op0=ALU.add, op1=ALU.add))
    return ev




def ph_l1_attn(ctx, lnsc, x_sb, xn_bf, ln_done):
    from concourse.aot_env import allow_ici_all_to_all

    RSH = 386
    a2a_in = ctx.dram("c_a2a_in", [8 * RSH, TLOC], BF16)
    a2a_out = ctx.dram("c_a2a_out", [8 * RSH, TLOC], BF16)
    ao_dram = [ctx.dram(f"c_ao{b_}", [128, 8192], BF16) for b_ in range(2)]
    rs_in = ctx.dram("c_rs_in", [8 * 1024, TLOC], BF16)
    rs_out = ctx.dram("c_rs_out", [1024, TLOC], BF16)
    nb_bounce = ctx.dram("c_nbb", [1, 2], F32)
    a2a_flat = a2a_in.ap().rearrange("r c -> (r c)")
    a2o_flat = a2a_out.ap().rearrange("r c -> (r c)")

    q_bh = ctx.AR[:, 0:4096].bitcast(BF16)
    k_bh = ctx.AR[:, 4096:8192].bitcast(BF16)
    v_bh = ctx.AR[:, 8192:12352].bitcast(BF16)
    pring4 = [ctx.AR[:, 12352 + s * 512:12352 + (s + 1) * 512].bitcast(BF16)
             for s in range(4)]
    masks = ctx.AR[:, 14400:15424].bitcast(BF16)
    bcsb = ctx.AR[0:64, 15424:15936]
    dscr = ctx.AR[0:1, 15936:16448]
    aorot = [ctx.AR[0:64, 16448 + s * 256:16448 + (s + 1) * 256].bitcast(BF16)
             for s in range(2)]
    wpart = ctx.AR[:, 12352:16448].bitcast(BF16)
    rotq = [ctx.AR[:, 16448 + s * 1024:16448 + (s + 1) * 1024].bitcast(BF16)
            for s in range(2)]
    rotv = [ctx.AR[:, 18496 + s * 520:18496 + (s + 1) * 520].bitcast(BF16)
            for s in range(2)]
    wo_sb = ctx.AR[:, 0:512].bitcast(BF16)
    stg7 = ctx.AR[:, 512:1024].bitcast(BF16)
    aold = [ctx.AR[:, 1536 + s * 256:1536 + (s + 1) * 256].bitcast(BF16)
            for s in range(2)]
    rsld = [ctx.AR[:, 2048 + s * 2048:2048 + (s + 1) * 2048] for s in range(2)]
    negc_bh = ctx.sbuf("c_negc", [128, 2], F32)
    nqm = ctx.sbuf("c_nqm", [2, 4], F32)
    ncrow = ctx.sbuf("c_ncrow", [1, 2], F32)
    if not hasattr(ctx, "c_halfones"):
        ctx.c_halfones = ctx.sbuf("c_halfones", [128, 2], BF16)

    dq_ship = dk_ship = None
    for wname in ("q", "k"):
        dgw = ctx.sem(); tw = ctx.sem(); vw = ctx.sem(); dww = ctx.sem()
        wdram = ctx.wqT if wname == "q" else ctx.wkT
        roff = 0 if wname == "q" else 128
        with ctx.block() as blk:
            @blk.gpsimd
            def _(g, wdram=wdram, wname=wname):
                if wname == "q":
                    g.memset(ctx.c_halfones[:], 0.0)
                    g.memset(ctx.c_halfones[0:64, 0:1], 1.0)
                    g.memset(ctx.c_halfones[64:128, 1:2], 1.0)
                load_weight_cm(ctx, g, wdram, wpart, dgw)
            @blk.tensor
            def _(tt):
                ctx.wait(tt, ln_done)
                ctx.wait(tt, dgw)
                for m in range(8):
                    if m >= 2:
                        ctx.wait(tt, vw, m - 1)
                    ps = ctx.PS[:, 0:2048] if m % 2 == 0 else ctx.PS[:, 2048:4096]
                    for n in range(4):
                        for c in range(8):
                            ins = tt.matmul(
                                ps[:, n * 512:(n + 1) * 512],
                                wpart[:, c * 1024 + m * 128: c * 1024 + m * 128 + 128],
                                xn_bf[:, c * TLOC + n * 512: c * TLOC + (n + 1) * 512],
                                start=(c == 0), stop=(c == 7))
                    tw.inc(ins)
            @blk.vector
            def _(vv):
                for m in range(8):
                    ctx.wait(vv, tw, m + 1)
                    if m >= 2:
                        ctx.wait(vv, dww, 16 * (m - 1))
                    ps = ctx.PS[:, 0:2048] if m % 2 == 0 else ctx.PS[:, 2048:4096]
                    vw.inc(vv.tensor_copy(rotq[m % 2][:], ps))
            @blk.sync
            def _(y, roff=roff):
                for m in range(8):
                    ctx.wait(y, vw, m + 1)
                    dww.inc(y.dma_start(
                        out=a2a_in.ap()[RSH * m + roff:RSH * m + roff + 128, :],
                        in_=rotq[m % 2][:]), 16)
        if wname == "q":
            dq_ship = dww
        else:
            dk_ship = dww
    dgv = ctx.sem(); tv = ctx.sem(); vv_ = ctx.sem(); dvv = ctx.sem(); gmz = ctx.sem()
    with ctx.block() as blk:
        @blk.gpsimd
        def _(g):
            load_weight_cm(ctx, g, ctx.wvT, wpart, dgv)
            g.memset(rotv[0][:].rearrange("p (h w) -> p h w", w=65)[:, :, 64:65], 1.0)
            gmz.inc(g.memset(
                rotv[1][:].rearrange("p (h w) -> p h w", w=65)[:, :, 64:65], 1.0))
        @blk.tensor
        def _(tt):
            ctx.wait(tt, dgv)
            ctx.wait(tt, gmz)
            for i in range(16):
                if i >= 2:
                    ctx.wait(tt, vv_, i - 1)
                ps = ctx.PS[:, 0:2048] if i % 2 == 0 else ctx.PS[:, 2048:4096]
                for h in range(2):
                    for c in range(8):
                        ins = tt.matmul(
                            ps[:, h * 512:(h + 1) * 512],
                            xn_bf[:, c * TLOC + i * 128: c * TLOC + (i + 1) * 128],
                            wpart[:, c * 1024 + h * 512: c * 1024 + (h + 1) * 512],
                            start=(c == 0), stop=(c == 7))
                tv.inc(ins)
        @blk.vector
        def _(vv):
            for i in range(16):
                ctx.wait(vv, tv, i + 1)
                if i >= 2:
                    ctx.wait(vv, dvv, 16 * 8 * (i - 1))
                ps = ctx.PS[:, 0:2048] if i % 2 == 0 else ctx.PS[:, 2048:4096]
                vv_.inc(vv.tensor_copy(
                    rotv[i % 2][:].rearrange("p (h w) -> p h w", w=65)[:, :, 0:64],
                    ps[:, 0:1024].rearrange("p (h w) -> p h w", w=64)))
        @blk.sync
        def _(y):
            for i in range(16):
                ctx.wait(y, vv_, i + 1)
                for jj in range(8):
                    off = (RSH * jj + 256) * TLOC + (i * 128) * 130
                    dvv.inc(y.dma_start(
                        out=a2a_flat[off:off + 128 * 130].rearrange(
                            "(p w) -> p w", w=130),
                        in_=rotv[i % 2][:, jj * 130:(jj + 1) * 130]), 16)

    ccs = ctx.sem(); dmk = ctx.sem()
    with ctx.block() as blk:
        @blk.gpsimd
        def _(g):
            ctx.wait(g, dq_ship); ctx.wait(g, dk_ship); ctx.wait(g, dvv)
            with allow_ici_all_to_all():
                ccs.inc(g.collective_compute(
                    "AllToAll", ALU.bypass, replica_groups=ALL8,
                    ins=[a2a_in.ap()], outs=[a2a_out.ap()]))
            for t4 in range(4):
                dmk.inc(g.dma_start(out=masks[:, t4 * 512:(t4 + 1) * 512],
                                    in_=ctx.cmask.ap()[t4, :, :]), 16)

    def rk(b_, g):
        return 4 * b_ + (g if g < 4 else 7 - g), (0 if g < 4 else 1)

    dao_list = []
    for b_ in range(2):
        dl = ctx.sem()
        with ctx.block() as blk:
            @blk.sync
            def _(y, b_=b_):
                ctx.wait(y, ccs)
                for g in range(8):
                    rg, sg = rk(b_, g)
                    dl.inc(y.dma_start(
                        out=q_bh[:, g * 1024:(g + 1) * 1024],
                        in_=a2a_out.ap()[RSH * rg:RSH * rg + 128,
                                         1024 * sg:1024 * sg + 1024]), 16)
                    dl.inc(y.dma_start(
                        out=k_bh[:, g * 1024:(g + 1) * 1024],
                        in_=a2a_out.ap()[RSH * rg + 128:RSH * rg + 256,
                                         1024 * sg:1024 * sg + 1024]), 16)
                    off = (RSH * rg + 256) * TLOC + sg * 1024 * 130
                    dl.inc(y.dma_start(
                        out=v_bh[:, g * 1040:(g + 1) * 1040].rearrange(
                            "p (i w) -> p i w", w=130),
                        in_=a2o_flat[off:off + 1024 * 130].rearrange(
                            "(i p w) -> p i w", p=128, w=130)), 16)

        sn = ctx.sem(); tn = ctx.sem(); vn = ctx.sem(); dn = ctx.sem(); tb2 = ctx.sem()
        with ctx.block() as blk:
            @blk.scalar
            def _(ss):
                ctx.wait(ss, dl)
                for w2 in range(4):
                    src = q_bh if w2 < 2 else k_bh
                    half = w2 % 2
                    if w2 > 0:
                        ctx.wait(ss, tn, 8 * w2)
                    sn.inc(ss.activation(lnsc.sq_bf[:, 0:4096],
                                         src[:, half * 4096:(half + 1) * 4096],
                                         AF.Square))
            @blk.tensor
            def _(tt):
                vn_after = [1, 3, 4, 6]
                for w2 in range(4):
                    ctx.wait(tt, sn, w2 + 1)
                    if w2 > 0:
                        ctx.wait(tt, vn, vn_after[w2 - 1])
                    for n in range(8):
                        ins = tt.matmul(
                            ctx.PS[0:2, n * 512:(n + 1) * 512],
                            ctx.c_halfones[:], lnsc.sq_bf[:, n * 512:(n + 1) * 512],
                            start=True, stop=True)
                        tn.inc(ins)
            @blk.vector
            def _(vv):
                cnt = 0
                for w2 in range(4):
                    col, half = w2 // 2, w2 % 2
                    ctx.wait(vv, tn, 8 * (w2 + 1))
                    if half == 0:
                        m = vn.inc(vv.tensor_reduce(nqm[0:2, col:col + 1],
                                                    ctx.PS[0:2, 0:4096],
                                                    axis=AXL.X, op=ALU.max))
                        ctx.wait(vv, vn, m)
                    else:
                        m = vn.inc(vv.tensor_reduce(nqm[0:2, 2 + col:3 + col],
                                                    ctx.PS[0:2, 0:4096],
                                                    axis=AXL.X, op=ALU.max))
                        ctx.wait(vv, vn, m)
                        m = vn.inc(vv.tensor_tensor(nqm[0:2, col:col + 1],
                                                    nqm[0:2, col:col + 1],
                                                    nqm[0:2, 2 + col:3 + col],
                                                    op=ALU.max))
                        ctx.wait(vv, vn, m)
            @blk.scalar
            def _(ss):
                ctx.wait(ss, vn)
                m = sn.inc(ss.activation(nqm[0:2, 2:3], nqm[0:2, 0:1], AF.Sqrt))
                ctx.wait(ss, sn, m)
                sn.inc(ss.activation(nqm[0:2, 3:4], nqm[0:2, 1:2], AF.Sqrt))
            @blk.vector
            def _(vv):
                ctx.wait(vv, sn)
                m = vn.inc(vv.tensor_tensor(nqm[0:2, 0:1], nqm[0:2, 2:3],
                                            nqm[0:2, 3:4], op=ALU.mult))
                ctx.wait(vv, vn, m)
                vn.inc(vv.tensor_scalar(nqm[0:2, 1:2], nqm[0:2, 0:1],
                                        -0.126, 0.0, op0=ALU.mult, op1=ALU.add))
            @blk.sync
            def _(y):
                ctx.wait(y, vn)
                dn.inc(y.dma_start(out=nb_bounce.ap(), in_=nqm[0:2, 1:2],
                                   allow_slow_non_contiguous=True), 16)
                ctx.wait(y, dn)
                dn.inc(y.dma_start(out=ncrow[:], in_=nb_bounce.ap()), 16)
            @blk.tensor
            def _(tt):
                ctx.wait(tt, dn)
                for h in range(2):
                    tb2.inc(tt.matmul(
                        ctx.PS[:, 2048 + h * 512:2048 + h * 512 + 1],
                        lnsc.ones_row_f[:], ncrow[0:1, h:h + 1],
                        start=True, stop=True))
            @blk.vector
            def _(vv):
                ctx.wait(vv, tb2)
                for h in range(2):
                    vn.inc(vv.tensor_copy(
                        negc_bh[:, h:h + 1],
                        ctx.PS[:, 2048 + h * 512:2048 + h * 512 + 1]))

        tl = ctx.sem(); se = ctx.sem(); vm = ctx.sem(); vf = ctx.sem(); dao_b = ctx.sem()

        def maskops2(bp2):
            ops = []
            for sc in range(2):
                off = bp2 * 128 - sc * 512
                if off <= -128:
                    continue
                ops.append((sc, "zero" if off >= 512 else off // 128))
            return ops

        seq = []
        for h in range(2):
            for qq in range(8):
                for g in range(qq + 1):
                    for bp2 in range(8):
                        seq.append((h, qq, g, bp2))
        n_j = len(seq)
        is_first = [g == 0 and bp2 == 0 for (h, qq, g, bp2) in seq]
        is_last = [g == qq and bp2 == 7 for (h, qq, g, bp2) in seq]

        vmM = [0] * n_j
        _vmc = 0
        for j, (h, qq, g, bp2) in enumerate(seq):
            if g == qq:
                _vmc += len(maskops2(bp2))
            vmM[j] = _vmc
        tlL = [0] * n_j; tlPV = [0] * n_j
        tlB = {}; vfR = {}; vfE = {}; vfA = {}
        fin_f = [None] * n_j
        _tl = 2; tlL[0] = 1; tlL[1] = 2
        _f = 0
        for j in range(n_j):
            if j + 2 < n_j:
                _tl += 1; tlL[j + 2] = _tl
            _tl += 1; tlPV[j] = _tl
            if is_last[j]:
                fin_f[j] = (_f, _f + 1)
                for sc in range(2):
                    _tl += 1; tlB[_f + sc] = _tl
                _f += 2
        _vf = 0
        for j in range(n_j):
            if is_last[j]:
                for f in fin_f[j]:
                    _vf += 1; vfR[f] = _vf
                    _vf += 1; vfE[f] = _vf
                    _vf += 1; vfA[f] = _vf
        nfin = _f

        with ctx.block() as blk:
            @blk.tensor
            def _(tt, b_=b_):
                ctx.wait(tt, vn)
                ctx.wait(tt, dmk)
                prev_f = [None]

                def emit_L(m):
                    h, qq, g, bp2 = seq[m]
                    if m >= 2:
                        ctx.wait(tt, se, m - 1)
                    for sc in range(2):
                        ins = tt.matmul(
                            ctx.PS[:, (m % 2) * 1024 + sc * 512:
                                   (m % 2) * 1024 + sc * 512 + 512],
                            k_bh[64 * h:64 * h + 64,
                                 g * 1024 + bp2 * 128:g * 1024 + (bp2 + 1) * 128],
                            q_bh[64 * h:64 * h + 64,
                                 qq * 1024 + sc * 512:qq * 1024 + sc * 512 + 512],
                            start=True, stop=True, skip_group_check=True)
                    tl.inc(ins)
                    assert tl.n == tlL[m]

                def emit_PV(j):
                    h, qq, g, bp2 = seq[j]
                    ctx.wait(tt, se, j + 1)
                    if g == qq:
                        ctx.wait(tt, vm, vmM[j])
                    if is_first[j] and prev_f[0] is not None:
                        ctx.wait(tt, vf, vfA[prev_f[0]])
                    for sc in range(2):
                        ins = tt.matmul(
                            ctx.PS[0:65, (4 + sc) * 512:(4 + sc) * 512 + 512],
                            v_bh[:, (g * 8 + bp2) * 130 + 65 * h:
                                 (g * 8 + bp2) * 130 + 65 * h + 65],
                            pring4[j % 4][:, sc * 512:(sc + 1) * 512],
                            start=(g == 0 and bp2 == 0),
                            stop=(g == qq and bp2 == 7),
                            skip_group_check=True)
                    tl.inc(ins)
                    assert tl.n == tlPV[j]

                emit_L(0); emit_L(1)
                for j in range(n_j):
                    if j + 2 < n_j:
                        emit_L(j + 2)
                    emit_PV(j)
                    if is_last[j]:
                        for sc in range(2):
                            f = fin_f[j][sc]
                            ctx.wait(tt, vf, vfR[f])
                            ins = tt.matmul(ctx.PS[0:64, 3072:3584],
                                            lnsc.ones_row_f[0:1, 0:64],
                                            dscr[0:1, :], start=True, stop=True,
                                            skip_group_check=True)
                            tl.inc(ins)
                            assert tl.n == tlB[f]
                        prev_f[0] = fin_f[j][1]
            @blk.scalar
            def _(ss):
                for j in range(n_j):
                    h, qq, g, bp2 = seq[j]
                    ctx.wait(ss, tl, tlL[j])
                    se.inc(ss.activation(pring4[j % 4][:],
                                         ctx.PS[:, (j % 2) * 1024:
                                                (j % 2) * 1024 + 1024],
                                         AF.Exp, bias=negc_bh[:, h:h + 1],
                                         scale=0.125))
            @blk.vector
            def _(vv, b_=b_):
                for j in range(n_j):
                    h, qq, g, bp2 = seq[j]
                    if g == qq:
                        ctx.wait(vv, se, j + 1)
                        for (sc, kind) in maskops2(bp2):
                            sl2 = slice(sc * 512, (sc + 1) * 512)
                            if kind == "zero":
                                m = vm.inc(vv.memset(pring4[j % 4][:, sl2], 0.0))
                            else:
                                m = vm.inc(vv.tensor_tensor(
                                    pring4[j % 4][:, sl2], pring4[j % 4][:, sl2],
                                    masks[:, kind * 512:(kind + 1) * 512],
                                    op=ALU.mult))
                            ctx.wait(vv, vm, m)
                        assert vm.n == vmM[j]
                    if is_last[j]:
                        for sc in range(2):
                            f = fin_f[j][sc]
                            ctx.wait(vv, tl, tlPV[j])
                            if f >= 1:
                                ctx.wait(vv, tl, tlB[f - 1])
                            m = vf.inc(vv.reciprocal(
                                dscr[0:1, :],
                                ctx.PS[64:65, (4 + sc) * 512:(4 + sc) * 512 + 512]))
                            assert vf.n == vfR[f]
                            ctx.wait(vv, tl, tlB[f])
                            if f >= 2:
                                ctx.wait(vv, dao_b, 16 * (f - 1))
                            m = vf.inc(vv.tensor_copy(bcsb[:], ctx.PS[0:64, 3072:3584]))
                            assert vf.n == vfE[f]
                            ctx.wait(vv, vf, m)
                            vf.inc(vv.tensor_tensor(
                                aorot[f % 2][:],
                                ctx.PS[0:64, (4 + sc) * 512:(4 + sc) * 512 + 512],
                                bcsb[:], op=ALU.mult))
                            assert vf.n == vfA[f]
            @blk.sync
            def _(y, b_=b_):
                for j in range(n_j):
                    if not is_last[j]:
                        continue
                    h, qq, g, bp2 = seq[j]
                    for sc in range(2):
                        f = fin_f[j][sc]
                        ctx.wait(y, vf, vfA[f])
                        dao_b.inc(y.dma_start(
                            out=ao_dram[b_].ap()[64 * h:64 * h + 64,
                                                 qq * 1024 + sc * 512:
                                                 qq * 1024 + sc * 512 + 512],
                            in_=aorot[f % 2][:]), 16)
        dao_list.append(dao_b)

    dgo = ctx.sem(); dld = ctx.sem(); t7 = ctx.sem(); v7 = ctx.sem(); d7 = ctx.sem()
    with ctx.block() as blk:
        @blk.gpsimd
        def _(g):
            load_weight_cm(ctx, g, ctx.woT_pair, wo_sb, dgo, nch=1)
    with ctx.block() as blk:
        @blk.scalar
        def _(ss):
            ctx.wait(ss, dgo)
            for dab in dao_list:
                ctx.wait(ss, dab)
            for b_ in range(2):
                for n in range(16):
                    i = b_ * 16 + n
                    if i >= 2:
                        ctx.wait(ss, t7, 8 * (i - 1))
                    dld.inc(ss.dma_start(out=aold[i % 2][:],
                                         in_=ao_dram[b_].ap()[:, n * 512:(n + 1) * 512]), 16)
        @blk.tensor
        def _(tt):
            for b_ in range(2):
                for n in range(16):
                    i = b_ * 16 + n
                    ctx.wait(tt, dld, 16 * (i + 1))
                    if i >= 1:
                        ctx.wait(tt, v7, 8 * i)
                    for m in range(8):
                        if m >= 4:
                            ctx.wait(tt, v7, 8 * i + m - 3)
                        ins = tt.matmul(
                            ctx.PS[:, (m % 4) * 512:(m % 4 + 1) * 512],
                            wo_sb[:, m * 128:(m + 1) * 128],
                            aold[i % 2][:], start=True, stop=True,
                            skip_group_check=True)
                        t7.inc(ins)
        @blk.vector
        def _(vv):
            for b_ in range(2):
                for n in range(16):
                    i = b_ * 16 + n
                    for m in range(8):
                        ctx.wait(vv, t7, 8 * i + m + 1)
                        if 8 * i + m >= 2:
                            ctx.wait(vv, d7, 16 * (8 * i + m - 1))
                        v7.inc(vv.tensor_copy(
                            stg7[:, (m % 2) * 512:(m % 2 + 1) * 512],
                            ctx.PS[:, (m % 4) * 512:(m % 4 + 1) * 512]))
        @blk.sync
        def _(y):
            for b_ in range(2):
                for n in range(16):
                    i = b_ * 16 + n
                    g = n // 2
                    rg, sg = rk(b_, g)
                    for m in range(8):
                        ctx.wait(y, v7, 8 * i + m + 1)
                        d7.inc(y.dma_start(
                            out=rs_in.ap()[1024 * rg + m * 128:1024 * rg + (m + 1) * 128,
                                           1024 * sg + (n % 2) * 512:
                                           1024 * sg + (n % 2) * 512 + 512],
                            in_=stg7[:, (m % 2) * 512:(m % 2 + 1) * 512]), 16)
    ccs2 = ctx.sem()
    with ctx.block() as blk:
        @blk.gpsimd
        def _(g):
            ctx.wait(g, d7)
            ccs2.inc(g.collective_compute(
                "ReduceScatter", ALU.add, replica_groups=ALL8,
                ins=[rs_in.ap()], outs=[rs_out.ap()]))
    dr8 = ctx.sem(); v8 = ctx.sem()
    with ctx.block() as blk:
        @blk.gpsimd
        def _(g):
            ctx.wait(g, ccs2)
            for m in range(8):
                if m >= 2:
                    ctx.wait(g, v8, m - 1)
                dr8.inc(g.dma_start(out=rsld[m % 2][:],
                                    in_=rs_out.ap()[m * 128:(m + 1) * 128, :]), 16)
        @blk.vector
        def _(vv):
            for m in range(8):
                ctx.wait(vv, dr8, 16 * (m + 1))
                v8.inc(vv.tensor_tensor(x_sb[:, m * TLOC:(m + 1) * TLOC],
                                        x_sb[:, m * TLOC:(m + 1) * TLOC],
                                        rsld[m % 2][:], op=ALU.add))
    return v8

def ph_l1_moe(ctx, lnsc, x_sb, xn_bf, ln_done):
    w1q = [ctx.AR[:, b * 8192:b * 8192 + 4096].bitcast(BF16) for b in range(2)]
    w2q = [ctx.AR[:, b * 8192 + 4096:(b + 1) * 8192].bitcast(BF16) for b in range(2)]
    h1q = ctx.AR[:, 16384:18432].bitcast(BF16)
    gbc4 = [ctx.AR[:, 18432 + t * 512:18432 + (t + 1) * 512] for t in range(4)]
    tmp2 = [ctx.AR[:, 20480 + s * 512:20480 + (s + 1) * 512] for s in range(2)]
    grow4 = [ctx.AR[0:1, 21504 + s * 256:21504 + (s + 1) * 256].bitcast(BF16)
             for s in range(4)]
    ident_f = ctx.AR[:, 22528:22656]
    gtm = ctx.AR[:, 22656:22664]
    gsc = ctx.AR[:, 22664:22696]
    g_fm = ctx.AR[0:8, 23552:24576].bitcast(BF16)
    b1col = ctx.sbuf("d_b1col", [128, 256], F32)
    b2col = ctx.sbuf("d_b2col", [128, 64], F32)
    rwT_sb = ctx.sbuf("d_rwT", [128, 64], BF16)
    ones128_bf = ctx.sbuf("d_ones128", [1, 128], BF16)

    NIT = 128

    def it_dec(it):
        eq, tc = divmod(it, 4)
        e, Q = divmod(eq, 4)
        return eq, e, Q, tc, 4 * e + tc

    tmGbc = {}; tmMm1 = {}; tmMm2 = {}
    _t = 0
    for it in range(NIT):
        eq, e, Q, tc, gidx = it_dec(it)
        if Q == 0:
            _t += 1; tmGbc[gidx] = _t
        for m in range(8):
            _t += 1; tmMm1[(it, m)] = _t
        for dm in range(8):
            _t += 1; tmMm2[(it, dm)] = _t
    vgGbcC = {}; vgMult = {}; vgAdd = {}; vgB2 = {}
    _v = 0
    for it in range(NIT):
        eq, e, Q, tc, gidx = it_dec(it)
        if Q == 0:
            _v += 1; vgGbcC[gidx] = _v
        for dm in range(8):
            _v += 1; vgMult[(it, dm)] = _v
            _v += 1; vgAdd[(it, dm)] = _v
            if Q == 3:
                _v += 1; vgB2[(it, dm)] = _v

    dgs = ctx.sem(); dwb = [ctx.sem(), ctx.sem()]
    dgr = [ctx.sem() for _ in range(4)]
    tm = ctx.sem(); sg = ctx.sem(); vg = ctx.sem()

    def w_loads(g, eq):
        e, Q = divmod(eq, 4)
        b = eq % 2
        for c in range(8):
            dwb[b].inc(g.dma_start(
                out=w1q[b][:, c * 1024:(c + 1) * 1024],
                in_=ctx.moe_w1t.ap()[1024 * e + c * 128:1024 * e + (c + 1) * 128,
                                     1024 * Q:1024 * (Q + 1)]), 16)
        for k in range(8):
            dwb[b].inc(g.dma_start(
                out=w2q[b][:, k * 1024:(k + 1) * 1024],
                in_=ctx.moe_w2t.ap()[4096 * e + 1024 * Q + k * 128:
                                     4096 * e + 1024 * Q + (k + 1) * 128, :]), 16)

    with ctx.block() as blk:
        @blk.gpsimd
        def _(g):
            w_loads(g, 0)
            w_loads(g, 1)
            g.memset(ones128_bf[:], 1.0)
            for c in range(8):
                dgs.inc(g.dma_start(out=rwT_sb[:, c * 8:(c + 1) * 8],
                                    in_=ctx.rwT.ap()[c * 128:(c + 1) * 128, :]), 16)
            for c in range(256):
                dgs.inc(g.dma_start(out=b1col[:, c:c + 1],
                                    in_=ctx.b1row.ap()[0:1, c * 128:(c + 1) * 128]), 16)
            for c in range(64):
                dgs.inc(g.dma_start(out=b2col[:, c:c + 1],
                                    in_=ctx.b2row.ap()[c // 8:c // 8 + 1,
                                                       (c % 8) * 128:(c % 8 + 1) * 128]), 16)
            dgs.inc(g.dma_start(out=ident_f[:], in_=ctx.ident.ap()), 16)

    tr = ctx.sem(); vr = ctx.sem(); sr = ctx.sem()
    VPI = 12
    with ctx.block() as blk:
        @blk.tensor
        def _(tt):
            ctx.wait(tt, ln_done)
            ctx.wait(tt, dgs)
            for i in range(16):
                if i >= 2:
                    ctx.wait(tt, vr, VPI * (i - 1))
                    ctx.wait(tt, sr, i - 1)
                pcol = (i % 2) * 512
                for c in range(8):
                    ins = tt.matmul(
                        ctx.PS[:, pcol:pcol + 8],
                        xn_bf[:, c * TLOC + i * 128: c * TLOC + (i + 1) * 128],
                        rwT_sb[:, c * 8:(c + 1) * 8],
                        start=(c == 0), stop=(c == 7))
                tr.inc(ins)
                if i >= 1:
                    ctx.wait(tt, vr, VPI * (i - 1) + 12)
                    tr.inc(tt.transpose(ctx.PS[0:8, 1024:1152], gtm[:, 0:8],
                                        ident_f[:]))
            ctx.wait(tt, vr, VPI * 16)
            tr.inc(tt.transpose(ctx.PS[0:8, 1024:1152], gtm[:, 0:8], ident_f[:]))
        @blk.vector
        def _(vv):
            for i in range(16):
                pcol = (i % 2) * 512
                ps = ctx.PS[:, pcol:pcol + 8]
                ctx.wait(vv, tr, 2 * i + 1 if i >= 1 else 1)
                m = vr.inc(vv.tensor_reduce(gsc[:, 16:17], ps, axis=AXL.X,
                                            op=ALU.max, negate=True))
                ctx.wait(vv, vr, m)
                m = vr.inc(vv.tensor_scalar(gsc[:, 8:16], ps, gsc[:, 16:17], 0.0,
                                            op0=ALU.add, op1=ALU.add))
                ctx.wait(vv, vr, m)
                m = vr.inc(vv.tensor_scalar(gsc[:, 24:32], gsc[:, 8:16], 0.0,
                                            -1e30, op0=ALU.is_ge, op1=ALU.mult))
                ctx.wait(vv, vr, m)
                m = vr.inc(vv.tensor_tensor(gsc[:, 24:32], gsc[:, 8:16],
                                            gsc[:, 24:32], op=ALU.add))
                ctx.wait(vv, vr, m)
                m = vr.inc(vv.tensor_reduce(gsc[:, 17:18], gsc[:, 24:32],
                                            axis=AXL.X, op=ALU.max, negate=True))
                ctx.wait(vv, vr, m)
                m = vr.inc(vv.tensor_scalar(gsc[:, 24:32], gsc[:, 8:16],
                                            gsc[:, 17:18], 1.0,
                                            op0=ALU.add, op1=ALU.mult))
                ctx.wait(vv, vr, m)
                m = vr.inc(vv.tensor_scalar(gsc[:, 8:16], gsc[:, 24:32], 0.0, 1.0,
                                            op0=ALU.is_ge, op1=ALU.mult))
                ctx.wait(vv, vr, m)
                ctx.wait(vv, sr, i + 1)
                m = vr.inc(vv.tensor_tensor(gsc[:, 24:32], gsc[:, 0:8],
                                            gsc[:, 8:16], op=ALU.mult))
                ctx.wait(vv, vr, m)
                m = vr.inc(vv.tensor_reduce(gsc[:, 18:19], gsc[:, 24:32],
                                            axis=AXL.X, op=ALU.add))
                ctx.wait(vv, vr, m)
                m = vr.inc(vv.reciprocal(gsc[:, 19:20], gsc[:, 18:19]))
                ctx.wait(vv, vr, m)
                if i >= 1:
                    ctx.wait(vv, tr, 2 * i + 1)
                m = vr.inc(vv.tensor_scalar(gtm[:, 0:8], gsc[:, 24:32],
                                            gsc[:, 19:20], 1.0,
                                            op0=ALU.mult, op1=ALU.mult))
                ctx.wait(vv, vr, m)
                if i >= 1:
                    m = vr.inc(vv.tensor_copy(g_fm[:, (i - 1) * 128:i * 128],
                                              ctx.PS[0:8, 1024:1152]))
                else:
                    m = vr.inc(vv.tensor_copy(gsc[0:1, 0:1], gsc[0:1, 0:1]))
                ctx.wait(vv, vr, m)
            ctx.wait(vv, tr, 32)
            vr.inc(vv.tensor_copy(g_fm[:, 15 * 128:16 * 128], ctx.PS[0:8, 1024:1152]))
        @blk.scalar
        def _(ss):
            for i in range(16):
                pcol = (i % 2) * 512
                ctx.wait(ss, vr, VPI * i + 1)
                sr.inc(ss.activation(gsc[:, 0:8], ctx.PS[:, pcol:pcol + 8],
                                     AF.Exp, bias=gsc[:, 16:17]))

    with ctx.block() as blk:
        @blk.gpsimd
        def _(g):
            for eq in range(32):
                if eq % 4 == 0:
                    e = eq // 4
                    for t4 in range(4):
                        gidx = 4 * e + t4
                        if gidx >= 4:
                            ctx.wait(g, tm, tmGbc[gidx - 4])
                        dgr[gidx % 4].inc(g.dma_start(
                            out=grow4[gidx % 4][0:1, :],
                            in_=g_fm[e:e + 1, t4 * 512:(t4 + 1) * 512]), 16)
                if eq >= 2:
                    ctx.wait(g, tm, tmMm2[((eq - 2) * 4 + 3, 7)])
                    w_loads(g, eq)
        @blk.tensor
        def _(tt):
            ctx.wait(tt, dgs)
            for it in range(NIT):
                eq, e, Q, tc, gidx = it_dec(it)
                if tc == 0:
                    ctx.wait(tt, dwb[eq % 2], 16 * 16 * (eq // 2 + 1))
                if Q == 0:
                    ctx.wait(tt, dgr[gidx % 4], 16 * (gidx // 4 + 1))
                    if gidx >= 1:
                        ctx.wait(tt, vg, vgGbcC[gidx - 1])
                    ins = tt.matmul(ctx.PS[:, 2048:2560], ones128_bf[:],
                                    grow4[gidx % 4][:], start=True, stop=True)
                    tm.inc(ins)
                    assert tm.n == tmGbc[gidx]
                for m in range(8):
                    if 8 * it + m - 1 > 0:
                        ctx.wait(tt, sg, 8 * it + m - 1)
                    for c in range(8):
                        ins = tt.matmul(
                            ctx.PS[:, (m % 2) * 512:(m % 2) * 512 + 512],
                            w1q[eq % 2][:, c * 1024 + m * 128:c * 1024 + m * 128 + 128],
                            xn_bf[:, c * TLOC + tc * 512:c * TLOC + (tc + 1) * 512],
                            start=(c == 0), stop=(c == 7))
                    tm.inc(ins)
                    assert tm.n == tmMm1[(it, m)]
                ctx.wait(tt, sg, 8 * (it + 1))
                for dm in range(8):
                    if dm >= 2:
                        ctx.wait(tt, vg, vgMult[(it, dm - 2)])
                    elif it >= 1:
                        ctx.wait(tt, vg, vgMult[(it - 1, dm + 6)])
                    for k in range(8):
                        ins = tt.matmul(
                            ctx.PS[:, 1024 + (dm % 2) * 512:1024 + (dm % 2) * 512 + 512],
                            w2q[eq % 2][:, k * 1024 + dm * 128:k * 1024 + dm * 128 + 128],
                            h1q[:, k * 512:(k + 1) * 512],
                            start=(k == 0), stop=(k == 7))
                    tm.inc(ins)
                    assert tm.n == tmMm2[(it, dm)]
        @blk.scalar
        def _(ss):
            for it in range(NIT):
                eq, e, Q, tc, gidx = it_dec(it)
                for m in range(8):
                    ctx.wait(ss, tm, tmMm1[(it, m)])
                    sg.inc(ss.activation(h1q[:, m * 512:(m + 1) * 512],
                                         ctx.PS[:, (m % 2) * 512:(m % 2) * 512 + 512],
                                         AF.Gelu,
                                         bias=b1col[:, e * 32 + Q * 8 + m:
                                                    e * 32 + Q * 8 + m + 1]))
        @blk.vector
        def _(vv):
            for it in range(NIT):
                eq, e, Q, tc, gidx = it_dec(it)
                if Q == 0:
                    ctx.wait(vv, tm, tmGbc[gidx])
                    m = vg.inc(vv.tensor_copy(gbc4[tc][:], ctx.PS[:, 2048:2560]))
                    assert vg.n == vgGbcC[gidx]
                    ctx.wait(vv, vg, m)
                for dm in range(8):
                    ctx.wait(vv, tm, tmMm2[(it, dm)])
                    m = vg.inc(vv.tensor_tensor(
                        tmp2[dm % 2][:],
                        ctx.PS[:, 1024 + (dm % 2) * 512:1024 + (dm % 2) * 512 + 512],
                        gbc4[tc][:], op=ALU.mult))
                    assert vg.n == vgMult[(it, dm)]
                    ctx.wait(vv, vg, m)
                    m = vg.inc(vv.tensor_tensor(
                        x_sb[:, dm * TLOC + tc * 512:dm * TLOC + (tc + 1) * 512],
                        x_sb[:, dm * TLOC + tc * 512:dm * TLOC + (tc + 1) * 512],
                        tmp2[dm % 2][:], op=ALU.add))
                    assert vg.n == vgAdd[(it, dm)]
                    if Q == 3:
                        ctx.wait(vv, vg, m)
                        vg.inc(vv.scalar_tensor_tensor(
                            x_sb[:, dm * TLOC + tc * 512:dm * TLOC + (tc + 1) * 512],
                            gbc4[tc][:], b2col[:, e * 8 + dm:e * 8 + dm + 1],
                            x_sb[:, dm * TLOC + tc * 512:dm * TLOC + (tc + 1) * 512],
                            op0=ALU.mult, op1=ALU.add))
                        assert vg.n == vgB2[(it, dm)]
    return vg

def shard_x(x):
    out = []
    for r in range(NCORES):
        b, j = r // 4, r % 4
        tok = np.concatenate([x[b, j * 1024:(j + 1) * 1024],
                              x[b, (7 - j) * 1024:(8 - j) * 1024]], axis=0)
        out.append(np.ascontiguousarray(tok.T))
    return out


def unshard_x(parts):
    x = np.empty((B, S, D), np.float32)
    for r in range(NCORES):
        b, j = r // 4, r % 4
        tok = parts[r].T
        x[b, j * 1024:(j + 1) * 1024] = tok[:1024]
        x[b, (7 - j) * 1024:(8 - j) * 1024] = tok[1024:]
    return x


def stage_ffn_weights(inp):
    G1R, G2R = np.asarray(inp["L0_ffn_G1_R"]), np.asarray(inp["L0_ffn_G2_R"])
    G1I, G2I = np.asarray(inp["L0_ffn_G1_I"]), np.asarray(inp["L0_ffn_G2_I"])

    def arep(G1):
        A = np.transpose(np.asarray(G1)[0], (1, 2, 0))
        return np.ascontiguousarray(
            np.broadcast_to(A[:, :, None, :, None],
                            (4, 32, 16, 32, 16)).reshape(4, 512, 512))

    def brep(G2):
        Bm = np.transpose(np.asarray(G2)[:, :, 0, :], (0, 2, 1))
        return np.ascontiguousarray(
            np.broadcast_to(Bm[:, None, :, None, :],
                            (4, 32, 16, 32, 16)).reshape(4, 512, 512))

    br = np.asarray(inp["L0_ffn_bias_real"]).reshape(512, 1)
    bi = np.asarray(inp["L0_ffn_bias_imag"]).reshape(512, 1)
    bias_ri = np.ascontiguousarray(
        np.concatenate([br, bi], 0).astype(np.float32))
    return (arep(G1R).astype(np.float32), brep(G2R).astype(np.float32),
            arep(G1I).astype(np.float32), brep(G2I).astype(np.float32), bias_ri)


def stage_l0_attn_weights(inp):
    wq = np.ascontiguousarray(np.asarray(inp["L0_attn_Wq"], np.float32))
    wk = np.ascontiguousarray(np.asarray(inp["L0_attn_Wk"], np.float32))
    wv = np.ascontiguousarray(np.asarray(inp["L0_attn_Wv"], np.float32))
    wo = np.ascontiguousarray(np.asarray(inp["L0_attn_Wo"], np.float32))
    bo_gb = np.ascontiguousarray(np.stack(
        [np.asarray(inp["L0_attn_bo"], np.float32),
         np.asarray(inp["L0_attn_gamma"], np.float32),
         np.asarray(inp["L0_attn_beta"], np.float32)], axis=1))
    return wq, wk, wv, wo, bo_gb



def build_stage_a():
    ctx = Ctx()
    nc = ctx.nc
    x_in = nc.dram_tensor("x_in", [D, TLOC], F32, kind="ExternalInput")
    x_out = nc.dram_tensor("x_out", [D, TLOC], F32, kind="ExternalOutput")
    ctx.wq = nc.dram_tensor("wq", [D, D], F32, kind="ExternalInput")
    ctx.wk = nc.dram_tensor("wk", [D, D], F32, kind="ExternalInput")
    ctx.wv = nc.dram_tensor("wv", [D, D], F32, kind="ExternalInput")
    ctx.wo = nc.dram_tensor("wo", [D, D], F32, kind="ExternalInput")
    ctx.bo_gb = nc.dram_tensor("bo_gb", [D, 3], F32, kind="ExternalInput")

    x_sb = ctx.sbuf("x_sb", [128, NCH * TLOC], F32)
    xn_bf = ctx.sbuf("xn_bf", [128, NCH * TLOC], BF16)
    ctx.arena_init()
    lnsc = alloc_ln_scratch(ctx)

    dsem = ctx.sem()
    ph_load_x(ctx, x_in, x_sb, dsem)
    csem = ph_init_consts(ctx, lnsc)
    ln_done = ph_ln(ctx, lnsc, x_sb, xn_bf, deps=(dsem, csem))
    fin = ph_l0_attn(ctx, lnsc, x_sb, xn_bf, ln_done)
    ph_store_x(ctx, x_sb, x_out, (fin,))
    ctx.close()
    return ctx


def run_stage_a(x_parts, inp):
    wq, wk, wv, wo, bo_gb = stage_l0_attn_weights(inp)
    ctx = build_stage_a()
    in_maps = [{"x_in": x_parts[r], "wq": wq, "wk": wk, "wv": wv, "wo": wo,
                "bo_gb": bo_gb} for r in range(NCORES)]
    res = run_bass_kernel_spmd(ctx.nc, in_maps, core_ids=list(range(NCORES)))
    return [res.results[r]["x_out"] for r in range(NCORES)]


def build_stage_b():
    ctx = Ctx()
    nc = ctx.nc
    x_in = nc.dram_tensor("x_in", [D, TLOC], F32, kind="ExternalInput")
    x_out = nc.dram_tensor("x_out", [D, TLOC], F32, kind="ExternalOutput")
    ctx.arep_r = nc.dram_tensor("arep_r", [4, 512, 512], F32, kind="ExternalInput")
    ctx.brep_r = nc.dram_tensor("brep_r", [4, 512, 512], F32, kind="ExternalInput")
    ctx.arep_i = nc.dram_tensor("arep_i", [4, 512, 512], F32, kind="ExternalInput")
    ctx.brep_i = nc.dram_tensor("brep_i", [4, 512, 512], F32, kind="ExternalInput")
    ctx.bias_ri = nc.dram_tensor("bias_ri", [D, 1], F32, kind="ExternalInput")

    x_sb = ctx.sbuf("x_sb", [128, NCH * TLOC], F32)
    xn_bf = ctx.sbuf("xn_bf", [128, NCH * TLOC], BF16)
    ctx.arena_init()
    lnsc = alloc_ln_scratch(ctx)

    dsem = ctx.sem()
    ph_load_x(ctx, x_in, x_sb, dsem)
    csem = ph_init_consts(ctx, lnsc)
    ln_done = ph_ln(ctx, lnsc, x_sb, xn_bf, deps=(dsem, csem))
    ev = ph_l0_ffn(ctx, lnsc, x_sb, xn_bf, ln_done)
    ph_store_x(ctx, x_sb, x_out, (ev,))
    ctx.close()
    return ctx


def run_stage_b(x_parts, inp):
    ar, br_, ai, bi_, bias = stage_ffn_weights(inp)
    ctx = build_stage_b()
    in_maps = [{"x_in": x_parts[r], "arep_r": ar, "brep_r": br_,
                "arep_i": ai, "brep_i": bi_, "bias_ri": bias}
               for r in range(NCORES)]
    res = run_bass_kernel_spmd(ctx.nc, in_maps, core_ids=list(range(NCORES)))
    return [res.results[r]["x_out"] for r in range(NCORES)]




def stage_l1_attn_weights(inp, r):
    wqkv = np.asarray(inp["L1_attn_w_qkv"], np.float32)
    wout = np.asarray(inp["L1_attn_w_out"], np.float32)
    wqT = np.ascontiguousarray(wqkv[0:1024].T)
    wkT = np.ascontiguousarray(wqkv[1024:2048].T)
    wvT = np.ascontiguousarray(wqkv[2048:3072].T)
    woT = np.ascontiguousarray(wout.T)
    woT_pair = np.ascontiguousarray(woT[128 * r:128 * (r + 1), :])
    return wqT, wkT, wvT, woT_pair


def make_cmask():
    m = np.zeros((4, 128, 512), np.float32)
    for o in range(4):
        for p in range(128):
            m[o, p, o * 128 + p:] = 1.0
    return m


def build_stage_c():
    ctx = Ctx()
    nc = ctx.nc
    x_in = nc.dram_tensor("x_in", [D, TLOC], F32, kind="ExternalInput")
    x_out = nc.dram_tensor("x_out", [D, TLOC], F32, kind="ExternalOutput")
    ctx.wqT = nc.dram_tensor("wqT", [D, D], F32, kind="ExternalInput")
    ctx.wkT = nc.dram_tensor("wkT", [D, D], F32, kind="ExternalInput")
    ctx.wvT = nc.dram_tensor("wvT", [D, D], F32, kind="ExternalInput")
    ctx.woT_pair = nc.dram_tensor("woT_pair", [128, D], F32, kind="ExternalInput")
    ctx.cmask = nc.dram_tensor("cmask", [4, 128, 512], F32, kind="ExternalInput")

    x_sb = ctx.sbuf("x_sb", [128, NCH * TLOC], F32)
    xn_bf = ctx.sbuf("xn_bf", [128, NCH * TLOC], BF16)
    ctx.arena_init()
    lnsc = alloc_ln_scratch(ctx)

    dsem = ctx.sem()
    ph_load_x(ctx, x_in, x_sb, dsem)
    csem = ph_init_consts(ctx, lnsc)
    ln_done = ph_ln(ctx, lnsc, x_sb, xn_bf, deps=(dsem, csem))
    fin = ph_l1_attn(ctx, lnsc, x_sb, xn_bf, ln_done)
    ph_store_x(ctx, x_sb, x_out, (fin,))
    ctx.close()
    return ctx


def run_stage_c(x_parts, inp):
    cm = make_cmask()
    ctx = build_stage_c()
    in_maps = []
    for r in range(NCORES):
        wqT, wkT, wvT, woT_pair = stage_l1_attn_weights(inp, r)
        in_maps.append({"x_in": x_parts[r], "wqT": wqT, "wkT": wkT,
                        "wvT": wvT, "woT_pair": woT_pair, "cmask": cm})
    res = run_bass_kernel_spmd(ctx.nc, in_maps, core_ids=list(range(NCORES)))
    return [res.results[r]["x_out"] for r in range(NCORES)]




def stage_moe_weights(inp):
    w1 = np.asarray(inp["L1_ffn_w1"], np.float32)
    w2 = np.asarray(inp["L1_ffn_w2"], np.float32)
    moe_w1t = np.ascontiguousarray(
        np.concatenate([w1[e].T for e in range(8)], axis=0))
    moe_w2t = np.ascontiguousarray(
        np.concatenate([w2[e].T for e in range(8)], axis=0))
    b1row = np.ascontiguousarray(
        np.asarray(inp["L1_ffn_b1"], np.float32).reshape(1, 8 * 4096))
    b2row = np.ascontiguousarray(
        np.asarray(inp["L1_ffn_b2"], np.float32).reshape(8, 1024))
    rwT = np.ascontiguousarray(np.asarray(inp["L1_ffn_router_w"], np.float32).T)
    return moe_w1t, moe_w2t, b1row, b2row, rwT


def build_stage_d():
    ctx = Ctx()
    nc = ctx.nc
    x_in = nc.dram_tensor("x_in", [D, TLOC], F32, kind="ExternalInput")
    x_out = nc.dram_tensor("x_out", [D, TLOC], F32, kind="ExternalOutput")
    ctx.moe_w1t = nc.dram_tensor("moe_w1t", [8 * D, FMOE], F32, kind="ExternalInput")
    ctx.moe_w2t = nc.dram_tensor("moe_w2t", [8 * FMOE, D], F32, kind="ExternalInput")
    ctx.b1row = nc.dram_tensor("b1row", [1, 8 * FMOE], F32, kind="ExternalInput")
    ctx.b2row = nc.dram_tensor("b2row", [8, D], F32, kind="ExternalInput")
    ctx.rwT = nc.dram_tensor("rwT", [D, 8], F32, kind="ExternalInput")
    ctx.ident = nc.dram_tensor("ident", [128, 128], F32, kind="ExternalInput")

    x_sb = ctx.sbuf("x_sb", [128, NCH * TLOC], F32)
    xn_bf = ctx.sbuf("xn_bf", [128, NCH * TLOC], BF16)
    ctx.arena_init()
    lnsc = alloc_ln_scratch(ctx)

    dsem = ctx.sem()
    ph_load_x(ctx, x_in, x_sb, dsem)
    csem = ph_init_consts(ctx, lnsc)
    ln_done = ph_ln(ctx, lnsc, x_sb, xn_bf, deps=(dsem, csem))
    fin = ph_l1_moe(ctx, lnsc, x_sb, xn_bf, ln_done)
    ph_store_x(ctx, x_sb, x_out, (fin,))
    ctx.close()
    return ctx


def run_stage_d(x_parts, inp):
    ctx = build_stage_d()
    ident = np.eye(128, dtype=np.float32)
    in_maps = []
    for r in range(NCORES):
        w1t, w2t, b1r, b2r, rwT = stage_moe_weights(inp)
        in_maps.append({"x_in": x_parts[r], "moe_w1t": w1t, "moe_w2t": w2t,
                        "b1row": b1r, "b2row": b2r, "rwT": rwT, "ident": ident})
    res = run_bass_kernel_spmd(ctx.nc, in_maps, core_ids=list(range(NCORES)))
    return [res.results[r]["x_out"] for r in range(NCORES)]




def build_fused():
    ctx = Ctx()
    nc = ctx.nc
    x_in = nc.dram_tensor("x_in", [D, TLOC], F32, kind="ExternalInput")
    x_out = nc.dram_tensor("x_out", [D, TLOC], F32, kind="ExternalOutput")
    ctx.wq = nc.dram_tensor("wq", [D, D], F32, kind="ExternalInput")
    ctx.wk = nc.dram_tensor("wk", [D, D], F32, kind="ExternalInput")
    ctx.wv = nc.dram_tensor("wv", [D, D], F32, kind="ExternalInput")
    ctx.wo = nc.dram_tensor("wo", [D, D], F32, kind="ExternalInput")
    ctx.bo_gb = nc.dram_tensor("bo_gb", [D, 3], F32, kind="ExternalInput")
    ctx.arep_r = nc.dram_tensor("arep_r", [4, 512, 512], F32, kind="ExternalInput")
    ctx.brep_r = nc.dram_tensor("brep_r", [4, 512, 512], F32, kind="ExternalInput")
    ctx.arep_i = nc.dram_tensor("arep_i", [4, 512, 512], F32, kind="ExternalInput")
    ctx.brep_i = nc.dram_tensor("brep_i", [4, 512, 512], F32, kind="ExternalInput")
    ctx.bias_ri = nc.dram_tensor("bias_ri", [D, 1], F32, kind="ExternalInput")
    ctx.wqT = nc.dram_tensor("wqT", [D, D], F32, kind="ExternalInput")
    ctx.wkT = nc.dram_tensor("wkT", [D, D], F32, kind="ExternalInput")
    ctx.wvT = nc.dram_tensor("wvT", [D, D], F32, kind="ExternalInput")
    ctx.woT_pair = nc.dram_tensor("woT_pair", [128, D], F32, kind="ExternalInput")
    ctx.cmask = nc.dram_tensor("cmask", [4, 128, 512], F32, kind="ExternalInput")
    ctx.moe_w1t = nc.dram_tensor("moe_w1t", [8 * D, FMOE], F32, kind="ExternalInput")
    ctx.moe_w2t = nc.dram_tensor("moe_w2t", [8 * FMOE, D], F32, kind="ExternalInput")
    ctx.b1row = nc.dram_tensor("b1row", [1, 8 * FMOE], F32, kind="ExternalInput")
    ctx.b2row = nc.dram_tensor("b2row", [8, D], F32, kind="ExternalInput")
    ctx.rwT = nc.dram_tensor("rwT", [D, 8], F32, kind="ExternalInput")
    ctx.ident = nc.dram_tensor("ident", [128, 128], F32, kind="ExternalInput")

    x_sb = ctx.sbuf("x_sb", [128, NCH * TLOC], F32)
    xn_bf = ctx.sbuf("xn_bf", [128, NCH * TLOC], BF16)
    ctx.arena_init()
    lnsc = alloc_ln_scratch(ctx)

    dsem = ctx.sem()
    ph_load_x(ctx, x_in, x_sb, dsem)
    csem = ph_init_consts(ctx, lnsc)
    ln0 = ph_ln(ctx, lnsc, x_sb, xn_bf, deps=(dsem, csem))
    finA = ph_l0_attn(ctx, lnsc, x_sb, xn_bf, ln0)
    ln1 = ph_ln(ctx, lnsc, x_sb, xn_bf, deps=(finA,))
    finB = ph_l0_ffn(ctx, lnsc, x_sb, xn_bf, ln1)
    ln2 = ph_ln(ctx, lnsc, x_sb, xn_bf, deps=(finB,))
    finC = ph_l1_attn(ctx, lnsc, x_sb, xn_bf, ln2)
    ln3 = ph_ln(ctx, lnsc, x_sb, xn_bf, deps=(finC,))
    finD = ph_l1_moe(ctx, lnsc, x_sb, xn_bf, ln3)
    ph_store_x(ctx, x_sb, x_out, (finD,))
    ctx.close()
    return ctx


def kernel(**inputs):
    parts = shard_x(np.asarray(inputs["L0_attn_x"], np.float32))
    wq, wk, wv, wo, bo_gb = stage_l0_attn_weights(inputs)
    ar, br_, ai, bi_, bias = stage_ffn_weights(inputs)
    cm = make_cmask()
    ident = np.eye(128, dtype=np.float32)
    w1t, w2t, b1r, b2r, rwT = stage_moe_weights(inputs)
    ctx = build_fused()
    in_maps = []
    for r in range(NCORES):
        wqT, wkT, wvT, woT_pair = stage_l1_attn_weights(inputs, r)
        in_maps.append({
            "x_in": parts[r], "wq": wq, "wk": wk, "wv": wv, "wo": wo,
            "bo_gb": bo_gb, "arep_r": ar, "brep_r": br_, "arep_i": ai,
            "brep_i": bi_, "bias_ri": bias, "wqT": wqT, "wkT": wkT,
            "wvT": wvT, "woT_pair": woT_pair, "cmask": cm,
            "moe_w1t": w1t, "moe_w2t": w2t, "b1row": b1r, "b2row": b2r,
            "rwT": rwT, "ident": ident})
    res = run_bass_kernel_spmd(ctx.nc, in_maps, core_ids=list(range(NCORES)))
    return unshard_x([res.results[r]["x_out"] for r in range(NCORES)])


_OLD = '''unused'''



# version 23
# speedup vs baseline: 1.1107x; 1.0829x over previous
import numpy as np
from contextlib import ExitStack

import concourse.bass as bass
import concourse.mybir as mybir
from concourse.bass_utils import run_bass_kernel_spmd

F32 = mybir.dt.float32
BF16 = mybir.dt.bfloat16
AF = mybir.ActivationFunctionType
ALU = mybir.AluOpType
AXL = mybir.AxisListType

B, S, D = 2, 8192, 1024
NCORES = 8
TLOC = 2048
NCH = D // 128
LN_EPS = 1e-5
NHEADS = 16
HDIM = 64
FMOE = 4096
NEXP = 8
GROUPS = [[0, 1, 2, 3], [4, 5, 6, 7]]
ALL8 = [list(range(8))]


class Sem:
    __slots__ = ("h", "hid", "base", "n")
    def __init__(self, handle, hid, base):
        self.h = handle
        self.hid = hid
        self.base = base
        self.n = 0
    def inc(self, instr, k=1):
        instr.then_inc(self.h, k)
        self.n += k
        return self.n


class Ctx:
    def __init__(self):
        self.nc = bass.Bass("TRN2", target_bir_lowering=False, debug=False)
        self.es = ExitStack()
        self._ns = 0
        self._waited = {}

    SEM_POOL = 72

    def sem(self, name=None):
        slot = self._ns % self.SEM_POOL
        self._ns += 1
        if not hasattr(self, "_pool"):
            self._pool = {}
            self._last = {}
        if slot not in self._pool:
            self._pool[slot] = self.es.enter_context(
                self.nc.semaphore(f"sp{slot}"))
            base = 0
        else:
            prev = self._last[slot]
            base = prev.base + prev.n
        s = Sem(self._pool[slot], slot, base)
        self._last[slot] = s
        return s

    def sbuf(self, name, shape, dtype):
        return self.es.enter_context(self.nc.sbuf_tensor(name, shape, dtype))

    def psum(self, name, shape, dtype=F32):
        return self.es.enter_context(self.nc.psum_tensor(name, shape, dtype))

    def dram(self, name, shape, dtype, **kw):
        return self.nc.dram_tensor(name, shape, dtype, **kw)

    def block(self):
        return self.nc.Block()

    def wait(self, eng, sem, val=None):
        if val is None:
            val = sem.n
        if val <= 0:
            return
        a = sem.base + val
        key = (eng.engine, sem.hid)
        if self._waited.get(key, -1) >= a:
            return
        eng.wait_ge(sem.h, a)
        self._waited[key] = a

    def arena_init(self, words=24576, ps_words=4096):
        self.AR = self.sbuf("ARENA", [128, words], F32)
        self.PS = self.psum("PSARENA", [128, ps_words], F32)

    def at(self, off_kb, cols, part=128):
        w0 = off_kb * 256
        return self.AR[0:part, w0:w0 + cols]

    def atb(self, off_kb, cols):
        w0 = off_kb * 256
        return self.AR[:, w0:w0 + cols // 2].bitcast(BF16)

    def ps(self, bank, cols=512, part=128):
        return self.PS[0:part, bank * 512:bank * 512 + cols]

    def close(self):
        self.es.close()
        return self.nc


class NS:
    pass



def ph_load_x(ctx, x_dram, x_sb, dsem):
    with ctx.block() as blk:
        @blk.sync
        def _(y):
            for c in range(NCH):
                dsem.inc(y.dma_start(out=x_sb[:, c * TLOC:(c + 1) * TLOC],
                                     in_=x_dram.ap()[c * 128:(c + 1) * 128, :]), 16)


def ph_store_x(ctx, x_sb, xo_dram, deps):
    dsem = ctx.sem()
    with ctx.block() as blk:
        @blk.sync
        def _(y):
            for s in deps:
                ctx.wait(y, s)
            for c in range(NCH):
                dsem.inc(y.dma_start(out=xo_dram.ap()[c * 128:(c + 1) * 128, :],
                                     in_=x_sb[:, c * TLOC:(c + 1) * TLOC]), 16)
            ctx.wait(y, dsem)
    return dsem



def alloc_ln_scratch(ctx, nt=TLOC):
    sc = NS()
    sc.sq_bf = ctx.atb(88, NCH * 512)
    rows = ctx.sbuf("ln_rows", [33, nt], F32)
    sc.murow = rows[0:1, 0:nt]
    sc.rowA = rows[32:33, 0:nt]
    sc.rowB = ctx.AR[32:33, 84 * 256:84 * 256 + nt]
    sc.ones_col_bf = ctx.sbuf("ones_col_bf", [128, 1], BF16)
    ones_rows = ctx.sbuf("ones_rows", [65, 128], F32)
    sc.ones_rows = ones_rows
    sc.ones_row_f = ones_rows[0:1, :]
    sc.ones_row32 = ones_rows[32:33, :]
    sc.tmp512 = ctx.at(86, 512)
    return sc


def ph_init_consts(ctx, sc):
    g = ctx.sem()
    with ctx.block() as blk:
        @blk.gpsimd
        def _(gg):
            gg.memset(sc.ones_col_bf[:], 1.0)
            g.inc(gg.memset(sc.ones_rows[:], 1.0))
    return g


def _ln_stats(ctx, blk, sc, src_getter, deps, src_is_bf, nt=TLOC):
    v = ctx.sem(); s = ctx.sem(); t = ctx.sem()
    ntb = nt // 512
    ps_sum = ctx.ps(0)
    ps_sq = ctx.ps(1)
    for tb in range(ntb):
        sl = slice(tb * 512, (tb + 1) * 512)
        if not src_is_bf:
            pass
        @blk.scalar
        def _(ss, tb=tb):
            for d in deps:
                ctx.wait(ss, d)
            if tb > 0:
                ctx.wait(ss, t, tb)
            for c in range(NCH):
                ins = ss.activation(sc.sq_bf[:, c * 512:(c + 1) * 512],
                                    src_getter(c, tb), AF.Square)
            s.inc(ins)
        @blk.tensor
        def _(tt, tb=tb):
            for d in deps:
                ctx.wait(tt, d)
            ctx.wait(tt, s, tb + 1)
            if tb > 0:
                ctx.wait(tt, v, 2 * tb)
            for c in range(NCH):
                tt.matmul(ps_sum[0:1, :], sc.ones_col_bf[:], _bf_src(ctx, sc, src_getter, c, tb),
                          start=(c == 0), stop=(c == NCH - 1))
            for c in range(NCH):
                ins = tt.matmul(ps_sq[0:1, :], sc.ones_col_bf[:],
                                sc.sq_bf[:, c * 512:(c + 1) * 512],
                                start=(c == 0), stop=(c == NCH - 1))
            t.inc(ins)
        @blk.vector
        def _(vv, tb=tb, sl=sl):
            ctx.wait(vv, t)
            v.inc(vv.tensor_scalar(sc.murow[:, sl], ps_sum[0:1, :],
                                   1.0 / D, 0.0, op0=ALU.mult, op1=ALU.add))
            v.inc(vv.tensor_scalar(sc.rowA[:, sl], ps_sq[0:1, :],
                                   1.0 / D, LN_EPS, op0=ALU.mult, op1=ALU.add))
    @blk.vector
    def _(vv):
        ctx.wait(vv, v)
        m = v.inc(vv.tensor_tensor(sc.rowB[:, :nt], sc.murow[:, :nt],
                                   sc.murow[:, :nt], op=ALU.mult))
        ctx.wait(vv, v, m)
        v.inc(vv.tensor_tensor(sc.rowA[:, :nt], sc.rowA[:, :nt],
                               sc.rowB[:, :nt], op=ALU.subtract))
    @blk.scalar
    def _(ss):
        ctx.wait(ss, v)
        m = s.inc(ss.activation(sc.rowB[:, :nt], sc.rowA[:, :nt], AF.Ln))
        ctx.wait(ss, s, m)
        s.inc(ss.activation(sc.rowA[:, :nt], sc.rowB[:, :nt], AF.Exp, scale=-0.5))
    return v, s, t


def _bf_src(ctx, sc, src_getter, c, tb):
    ap = src_getter(c, tb)
    return ap


def ph_ln(ctx, sc, x_sb, xn_bf, deps=(), nt=TLOC):
    ntb = nt // 512
    fin = ctx.sem()
    with ctx.block() as blk:
        @blk.vector
        def _(vv):
            for d in deps:
                ctx.wait(vv, d)
            vv.tensor_copy(xn_bf[:, :NCH * nt], x_sb[:, :NCH * nt])
    with ctx.block() as blk:
        v, s, t = _ln_stats(ctx, blk, sc,
                            lambda c, tb: xn_bf[:, c * nt + tb * 512:
                                                c * nt + (tb + 1) * 512],
                            deps=(), src_is_bf=True, nt=nt)
        bc1 = ctx.ps(2); bc2 = ctx.ps(3)
        for tb in range(ntb):
            sl = slice(tb * 512, (tb + 1) * 512)
            @blk.tensor
            def _(tt, sl=sl):
                ctx.wait(tt, s)
                ctx.wait(tt, v)
                tt.matmul(bc1[:], sc.ones_row_f[:], sc.murow[:, sl],
                          start=True, stop=True)
                t.inc(tt.matmul(bc2[:], sc.ones_row32[:], sc.rowA[:, sl],
                                start=True, stop=True))
            @blk.vector
            def _(vv, sl=sl, tb=tb):
                ctx.wait(vv, t)
                for c in range(NCH):
                    xsl = slice(c * nt + tb * 512, c * nt + (tb + 1) * 512)
                    ctx.wait(vv, v)
                    m = v.inc(vv.tensor_tensor(sc.tmp512[:], x_sb[:, xsl],
                                               bc1[:], op=ALU.subtract))
                    ctx.wait(vv, v, m)
                    m = v.inc(vv.tensor_tensor(xn_bf[:, xsl], sc.tmp512[:],
                                               bc2[:], op=ALU.mult))
        @blk.vector
        def _(vv):
            fin.inc(vv.tensor_copy(sc.tmp512[0:1, 0:1], sc.tmp512[0:1, 0:1]))
    return fin


def ph_ln_affine_residual(ctx, sc, u_bf, x_sb, gb_col, deps=(), nt=TLOC):
    ntb = nt // 512
    fin = ctx.sem()
    with ctx.block() as blk:
        v, s, t = _ln_stats(ctx, blk, sc,
                            lambda c, tb: u_bf[:, c * nt + tb * 512:
                                               c * nt + (tb + 1) * 512],
                            deps=deps, src_is_bf=True, nt=nt)
        bc1 = ctx.ps(2); bc2 = ctx.ps(3)
        for tb in range(ntb):
            sl = slice(tb * 512, (tb + 1) * 512)
            @blk.tensor
            def _(tt, sl=sl):
                ctx.wait(tt, s)
                ctx.wait(tt, v)
                tt.matmul(bc1[:], sc.ones_row_f[:], sc.murow[:, sl],
                          start=True, stop=True)
                t.inc(tt.matmul(bc2[:], sc.ones_row32[:], sc.rowA[:, sl],
                                start=True, stop=True))
            @blk.vector
            def _(vv, sl=sl, tb=tb):
                ctx.wait(vv, t)
                for c in range(NCH):
                    xsl = slice(c * nt + tb * 512, c * nt + (tb + 1) * 512)
                    ctx.wait(vv, v)
                    m = v.inc(vv.tensor_tensor(sc.tmp512[:], u_bf[:, xsl],
                                               bc1[:], op=ALU.subtract))
                    ctx.wait(vv, v, m)
                    m = v.inc(vv.tensor_tensor(sc.tmp512[:], sc.tmp512[:],
                                               bc2[:], op=ALU.mult))
                    ctx.wait(vv, v, m)
                    m = v.inc(vv.tensor_scalar(sc.tmp512[:], sc.tmp512[:],
                                               gb_col[:, 3 * c + 1:3 * c + 2],
                                               gb_col[:, 3 * c + 2:3 * c + 3],
                                               op0=ALU.mult, op1=ALU.add))
                    ctx.wait(vv, v, m)
                    v.inc(vv.tensor_tensor(x_sb[:, xsl], x_sb[:, xsl],
                                           sc.tmp512[:], op=ALU.add))
        @blk.vector
        def _(vv):
            fin.inc(vv.tensor_copy(sc.tmp512[0:1, 0:1], sc.tmp512[0:1, 0:1]))
    return fin



def load_weight_cm(ctx, eng, dram_t, sb, sem, row0=0, nch=8, dout=1024):
    for c in range(nch):
        sem.inc(eng.dma_start(
            out=sb[:, c * dout:(c + 1) * dout],
            in_=dram_t.ap()[row0 + c * 128:row0 + (c + 1) * 128, :]), 16)



def ph_l0_attn(ctx, lnsc, x_sb, xn_bf, ln_done):
    kexp_dram = ctx.dram("l0_kexp", [D, TLOC], BF16)
    qn_dram = ctx.dram("l0_qn", [TLOC, D], BF16)
    ar_in = ctx.dram("l0_ar_in", [D, 1026], F32)
    ar_out = ctx.dram("l0_ar_out", [D, 1026], F32)
    ck_bounce = ctx.dram("l0_ckb", [1, 1024], F32)

    wk_sb = ctx.atb(0, 8 * 1024)
    wq_sb = ctx.atb(0, 8 * 1024)
    ktd = ctx.atb(0, 16 * 1024)
    qn_dt = ctx.atb(0, 8 * TLOC)
    wv_sb = ctx.atb(16, 8 * 1024)
    wo_sb = ctx.atb(0, 8 * 1024)
    vte = ctx.atb(32, 16 * 1024)
    a_et = ctx.atb(32, 8 * TLOC)
    krot = ctx.atb(64, 2 * 2048)
    stg = ctx.at(64, 2048)
    qrot = ctx.atb(72, 2 * 1024)
    crot = ctx.at(72, 2048)
    wksq = ctx.atb(80, 1024)
    ckrow = ctx.at(82, 1024, part=1)
    c_bf = ctx.atb(80, 8 * 1024)

    negck = ctx.sbuf("l0_negck", [128, 8], F32)
    zacc = ctx.sbuf("l0_zacc", [128, 8], F32)
    zinv = ctx.sbuf("l0_zinv", [128, 8], F32)
    bias3 = ctx.sbuf("l0_bias3", [128, 24], F32)
    qsc = ctx.sbuf("l0_qsc", [128, 8], F32)
    u_bf = xn_bf

    big0 = ctx.PS[:, 0:2048]
    big1 = ctx.PS[:, 2048:4096]
    Q = [ctx.PS[:, 0:512], ctx.PS[:, 512:1024],
         ctx.PS[:, 1024:1536], ctx.PS[:, 1536:2048]]

    dg = ctx.sem(); dy = ctx.sem()

    with ctx.block() as blk:
        @blk.gpsimd
        def _(g):
            load_weight_cm(ctx, g, ctx.wk, wk_sb, dg)
            load_weight_cm(ctx, g, ctx.wv, wv_sb, dg)
            for c in range(8):
                for j in range(3):
                    dg.inc(g.dma_start(
                        out=bias3[:, 3 * c + j:3 * c + j + 1],
                        in_=ctx.bo_gb.ap()[c * 128:(c + 1) * 128, j:j + 1],
                        allow_slow_non_contiguous=True), 16)
    v0 = ctx.sem(); s0 = ctx.sem(); t0 = ctx.sem()
    with ctx.block() as blk:
        @blk.scalar
        def _(ss):
            ctx.wait(ss, dg, 16 * 8)
            for c in range(8):
                if c > 0:
                    ctx.wait(ss, t0, c)
                s0.inc(ss.activation(wksq[:], wk_sb[:, c * 1024:(c + 1) * 1024],
                                     AF.Square))
        @blk.tensor
        def _(tt):
            for c in range(8):
                ctx.wait(tt, s0, c + 1)
                tt.matmul(ctx.PS[0:1, 0:512], lnsc.ones_col_bf[:],
                          wksq[:, 0:512], start=(c == 0), stop=(c == 7))
                ins = tt.matmul(ctx.PS[0:1, 512:1024], lnsc.ones_col_bf[:],
                                wksq[:, 512:1024], start=(c == 0), stop=(c == 7))
                t0.inc(ins)
        @blk.scalar
        def _(ss):
            ctx.wait(ss, t0, 8)
            s0.inc(ss.activation(ckrow[0:1, :], ctx.PS[0:1, 0:1024], AF.Sqrt,
                                 scale=1024.0))
        @blk.vector
        def _(vv):
            ctx.wait(vv, s0)
            v0.inc(vv.tensor_scalar(ckrow[0:1, :], ckrow[0:1, :], -1.0, 0.0,
                                    op0=ALU.mult, op1=ALU.add))
        @blk.sync
        def _(y):
            ctx.wait(y, v0)
            dy.inc(y.dma_start(out=ck_bounce.ap(), in_=ckrow[0:1, :]), 16)
            ctx.wait(y, dy)
            for c in range(8):
                dy.inc(y.dma_start(out=negck[:, c:c + 1],
                                   in_=ck_bounce.ap()[0:1, c * 128:(c + 1) * 128]), 16)

    tk = ctx.sem(); sk = ctx.sem(); dk = ctx.sem()
    with ctx.block() as blk:
        @blk.tensor
        def _(tt):
            ctx.wait(tt, ln_done)
            ctx.wait(tt, dg)
            ctx.wait(tt, dy)
            for m in range(8):
                if m >= 2:
                    ctx.wait(tt, sk, m - 1)
                ps = big0 if m % 2 == 0 else big1
                for n in range(4):
                    for c in range(8):
                        ins = tt.matmul(
                            ps[:, n * 512:(n + 1) * 512],
                            wk_sb[:, c * 1024 + m * 128: c * 1024 + m * 128 + 128],
                            xn_bf[:, c * TLOC + n * 512: c * TLOC + (n + 1) * 512],
                            start=(c == 0), stop=(c == 7))
                tk.inc(ins)
        @blk.scalar
        def _(ss):
            for m in range(8):
                ctx.wait(ss, tk, m + 1)
                if m >= 2:
                    ctx.wait(ss, dk, 16 * (m - 1))
                ps = big0 if m % 2 == 0 else big1
                sk.inc(ss.activation(krot[:, (m % 2) * 2048:(m % 2 + 1) * 2048],
                                     ps, AF.Exp, bias=negck[:, m:m + 1],
                                     accum_out=zacc[:, m:m + 1]))
        @blk.sync
        def _(y):
            for m in range(8):
                ctx.wait(y, sk, m + 1)
                dk.inc(y.dma_start(out=kexp_dram.ap()[m * 128:(m + 1) * 128, :],
                                   in_=krot[:, (m % 2) * 2048:(m % 2 + 1) * 2048]), 16)

    tv = ctx.sem(); vv_ = ctx.sem(); dgq = ctx.sem()
    with ctx.block() as blk:
        @blk.gpsimd
        def _(g):
            load_weight_cm(ctx, g, ctx.wq, wq_sb, dgq)
        @blk.tensor
        def _(tt):
            for i in range(16):
                if i >= 2:
                    ctx.wait(tt, vv_, i - 1)
                ps = big0 if i % 2 == 0 else big1
                for h in range(2):
                    for c in range(8):
                        ins = tt.matmul(
                            ps[:, h * 512:(h + 1) * 512],
                            xn_bf[:, c * TLOC + i * 128: c * TLOC + (i + 1) * 128],
                            wv_sb[:, c * 1024 + h * 512: c * 1024 + (h + 1) * 512],
                            start=(c == 0), stop=(c == 7))
                tv.inc(ins)
        @blk.vector
        def _(vv):
            for i in range(16):
                ctx.wait(vv, tv, i + 1)
                ps = big0 if i % 2 == 0 else big1
                vv_.inc(vv.tensor_copy(vte[:, i * 1024:(i + 1) * 1024],
                                       ps[:, 0:1024]))

    tq = ctx.sem(); vq = ctx.sem(); sq = ctx.sem(); vq2 = ctx.sem(); dq = ctx.sem()
    with ctx.block() as blk:
        @blk.tensor
        def _(tt):
            ctx.wait(tt, dgq)
            for i in range(16):
                if i >= 2:
                    ctx.wait(tt, sq, i - 1)
                ps = big0 if i % 2 == 0 else big1
                for h in range(2):
                    for c in range(8):
                        ins = tt.matmul(
                            ps[:, h * 512:(h + 1) * 512],
                            xn_bf[:, c * TLOC + i * 128: c * TLOC + (i + 1) * 128],
                            wq_sb[:, c * 1024 + h * 512: c * 1024 + (h + 1) * 512],
                            start=(c == 0), stop=(c == 7))
                tq.inc(ins)
        @blk.scalar
        def _(ss):
            for i in range(16):
                p = i % 2
                ctx.wait(ss, vq, i + 1)
                if i >= 2:
                    ctx.wait(ss, vq2, 2 * (i - 1))
                ps = big0 if p == 0 else big1
                sq.inc(ss.activation(qrot[:, p * 1024:(p + 1) * 1024],
                                     ps[:, 0:1024], AF.Exp,
                                     bias=qsc[:, 4 * p:4 * p + 1],
                                     accum_out=qsc[:, 4 * p + 1:4 * p + 2]))
        @blk.vector
        def _(vv):
            def recip_scale(j):
                p = j % 2
                ctx.wait(vv, sq, j + 1)
                m = vq2.inc(vv.reciprocal(qsc[:, 4 * p + 2:4 * p + 3],
                                          qsc[:, 4 * p + 1:4 * p + 2]))
                ctx.wait(vv, vq2, m)
                if j >= 2:
                    ctx.wait(vv, dq, 16 * (j - 1))
                vq2.inc(vv.tensor_scalar(qrot[:, p * 1024:(p + 1) * 1024],
                                         qrot[:, p * 1024:(p + 1) * 1024],
                                         qsc[:, 4 * p + 2:4 * p + 3], 1.0 / 32.0,
                                         op0=ALU.mult, op1=ALU.mult))
            for i in range(16):
                p = i % 2
                ctx.wait(vv, tq, i + 1)
                if i >= 2:
                    ctx.wait(vv, sq, i - 1)
                ps = big0 if p == 0 else big1
                vq.inc(vv.tensor_reduce(qsc[:, 4 * p:4 * p + 1], ps[:, 0:1024],
                                        axis=AXL.X, op=ALU.max, negate=True))
                if i >= 1:
                    recip_scale(i - 1)
            recip_scale(15)
        @blk.sync
        def _(y):
            for i in range(16):
                ctx.wait(y, vq2, 2 * i + 2)
                dq.inc(y.dma_start(out=qn_dram.ap()[i * 128:(i + 1) * 128, :],
                                   in_=qrot[:, (i % 2) * 1024:(i % 2 + 1) * 1024]), 16)

    dktd = ctx.sem(); tc_ = ctx.sem(); vc = ctx.sem(); dc = ctx.sem()
    with ctx.block() as blk:
        @blk.scalar
        def _(ss):
            ctx.wait(ss, dk)
            for i in range(16):
                dktd.inc(ss.dma_start(out=ktd[:, i * 1024:(i + 1) * 1024],
                                      in_=kexp_dram.ap()[:, i * 128:(i + 1) * 128],
                                      transpose=True), 16)
        @blk.tensor
        def _(tt):
            ctx.wait(tt, dktd)
            for m in range(8):
                for n in range(2):
                    it = m * 2 + n
                    if it >= 4:
                        ctx.wait(tt, vc, it - 3)
                    for i in range(16):
                        ins = tt.matmul(
                            Q[it % 4],
                            ktd[:, i * 1024 + m * 128: i * 1024 + m * 128 + 128],
                            vte[:, i * 1024 + n * 512: i * 1024 + (n + 1) * 512],
                            start=(i == 0), stop=(i == 15))
                    tc_.inc(ins)
        @blk.vector
        def _(vv):
            for it in range(16):
                ctx.wait(vv, tc_, it + 1)
                if it >= 4:
                    ctx.wait(vv, dc, 16 * (it - 3))
                vc.inc(vv.tensor_copy(stg[:, (it % 4) * 512:(it % 4 + 1) * 512],
                                      Q[it % 4]))
        @blk.sync
        def _(y):
            for it in range(16):
                m, n = divmod(it, 2)
                ctx.wait(y, vc, it + 1)
                dc.inc(y.dma_start(
                    out=ar_in.ap()[m * 128:(m + 1) * 128, n * 512:(n + 1) * 512],
                    in_=stg[:, (it % 4) * 512:(it % 4 + 1) * 512]), 16)
            ctx.wait(y, sk)
            for m in range(8):
                dc.inc(y.dma_start(out=ar_in.ap()[m * 128:(m + 1) * 128, 1024:1025],
                                   in_=zacc[:, m:m + 1],
                                   allow_slow_non_contiguous=True), 16)

    ccs = ctx.sem()
    with ctx.block() as blk:
        @blk.gpsimd
        def _(g):
            ctx.wait(g, dc)
            ccs.inc(g.collective_compute(
                "AllReduce", ALU.add, replica_groups=GROUPS,
                ins=[ar_in.ap()], outs=[ar_out.ap()]))

    dr = ctx.sem(); vr = ctx.sem(); dqn = ctx.sem()
    with ctx.block() as blk:
        @blk.sync
        def _(y):
            ctx.wait(y, ccs)
            for m in range(8):
                dr.inc(y.dma_start(out=zinv[:, m:m + 1],
                                   in_=ar_out.ap()[m * 128:(m + 1) * 128, 1024:1025],
                                   allow_slow_non_contiguous=True), 16)
            for m in range(8):
                if m >= 2:
                    ctx.wait(y, vr, m)
                dr.inc(y.dma_start(out=crot[:, (m % 2) * 1024:(m % 2 + 1) * 1024],
                                   in_=ar_out.ap()[m * 128:(m + 1) * 128, 0:1024]), 16)
        @blk.vector
        def _(vv):
            ctx.wait(vv, dr, 16 * 8)
            vr.inc(vv.reciprocal(zinv[:], zinv[:]))
            for m in range(8):
                ctx.wait(vv, dr, 16 * (9 + m))
                vr.inc(vv.tensor_scalar(c_bf[:, m * 1024:(m + 1) * 1024],
                                        crot[:, (m % 2) * 1024:(m % 2 + 1) * 1024],
                                        zinv[:, m:m + 1], 1.0,
                                        op0=ALU.mult, op1=ALU.mult))
        @blk.scalar
        def _(ss):
            ctx.wait(ss, dq)
            for c in range(8):
                dqn.inc(ss.dma_start(out=qn_dt[:, c * TLOC:(c + 1) * TLOC],
                                     in_=qn_dram.ap()[:, c * 128:(c + 1) * 128],
                                     transpose=True), 16)

    ta = ctx.sem(); va = ctx.sem()
    with ctx.block() as blk:
        @blk.tensor
        def _(tt):
            ctx.wait(tt, dqn)
            ctx.wait(tt, vr)
            for m in range(8):
                for n in range(4):
                    it = m * 4 + n
                    if it >= 4:
                        ctx.wait(tt, va, it - 3)
                    for c in range(8):
                        ins = tt.matmul(
                            Q[it % 4],
                            c_bf[:, c * 1024 + m * 128: c * 1024 + m * 128 + 128],
                            qn_dt[:, c * TLOC + n * 512: c * TLOC + (n + 1) * 512],
                            start=(c == 0), stop=(c == 7))
                    ta.inc(ins)
        @blk.vector
        def _(vv):
            for it in range(32):
                m, n = divmod(it, 4)
                ctx.wait(vv, ta, it + 1)
                va.inc(vv.tensor_copy(
                    a_et[:, m * TLOC + n * 512: m * TLOC + (n + 1) * 512],
                    Q[it % 4]))

    dgo = ctx.sem(); to = ctx.sem(); vo = ctx.sem()
    with ctx.block() as blk:
        @blk.gpsimd
        def _(g):
            load_weight_cm(ctx, g, ctx.wo, wo_sb, dgo)
    with ctx.block() as blk:
        @blk.tensor
        def _(tt):
            ctx.wait(tt, dgo)
            for n in range(4):
                for m in range(8):
                    it = n * 8 + m
                    if it >= 4:
                        ctx.wait(tt, vo, 2 * (it - 3))
                    for c in range(8):
                        ins = tt.matmul(
                            Q[it % 4],
                            wo_sb[:, c * 1024 + m * 128: c * 1024 + m * 128 + 128],
                            a_et[:, c * TLOC + n * 512: c * TLOC + (n + 1) * 512],
                            start=(c == 0), stop=(c == 7))
                    to.inc(ins)
        @blk.vector
        def _(vv):
            for n in range(4):
                for m in range(8):
                    it = n * 8 + m
                    ctx.wait(vv, to, it + 1)
                    xsl = slice(m * TLOC + n * 512, m * TLOC + (n + 1) * 512)
                    mv = vo.inc(vv.scalar_tensor_tensor(
                        stg[:, 0:512], Q[it % 4], bias3[:, 3 * m:3 * m + 1],
                        u_bf[:, xsl], op0=ALU.add, op1=ALU.add))
                    ctx.wait(vv, vo, mv)
                    vo.inc(vv.tensor_copy(u_bf[:, xsl], stg[:, 0:512]))
    return ph_ln_affine_residual(ctx, lnsc, u_bf, x_sb, bias3, deps=(vo,))



def ph_l0_ffn(ctx, lnsc, x_sb, xn_bf, xn_ready):
    wacc_r = ctx.at(0, 2048)
    wacc_i = ctx.at(8, 2048)
    rep_a = ctx.at(16, 2048)
    rep_b = ctx.at(24, 2048)
    rep_t = ctx.at(32, 2048)
    wr_bf = ctx.atb(40, 2048)
    wi_bf = ctx.atb(44, 2048)
    nwi_bf = ctx.atb(48, 2048)
    bias_col = ctx.sbuf("ffn_bias_col", [128, 8], F32)
    ps_a = ctx.ps(0); ps_b = ctx.ps(1)

    dg = ctx.sem(); v = ctx.sem(); t = ctx.sem(); ev = ctx.sem()
    for (arep, brep, wacc) in ((ctx.arep_r, ctx.brep_r, wacc_r),
                               (ctx.arep_i, ctx.brep_i, wacc_i)):
        for cc in range(4):
            with ctx.block() as blk:
                @blk.gpsimd
                def _(gg, arep=arep, brep=brep, cc=cc):
                    ctx.wait(gg, v)
                    for k in range(4):
                        dg.inc(gg.dma_start(
                            out=rep_a[:, k * 512:(k + 1) * 512],
                            in_=arep.ap()[cc, k * 128:(k + 1) * 128, :]), 16)
                        dg.inc(gg.dma_start(
                            out=rep_b[:, k * 512:(k + 1) * 512],
                            in_=brep.ap()[cc, k * 128:(k + 1) * 128, :]), 16)
                @blk.vector
                def _(vv, cc=cc, wacc=wacc):
                    ctx.wait(vv, dg)
                    if cc == 0:
                        v.inc(vv.tensor_tensor(wacc[:], rep_a[:], rep_b[:],
                                               op=ALU.mult))
                    else:
                        m = v.inc(vv.tensor_tensor(rep_t[:], rep_a[:],
                                                   rep_b[:], op=ALU.mult))
                        ctx.wait(vv, v, m)
                        v.inc(vv.tensor_tensor(wacc[:], wacc[:], rep_t[:],
                                               op=ALU.add))
    with ctx.block() as blk:
        @blk.gpsimd
        def _(gg):
            for c in range(8):
                dg.inc(gg.dma_start(out=bias_col[:, c:c + 1],
                                    in_=ctx.bias_ri.ap()[c * 128:(c + 1) * 128, :]), 16)
        @blk.vector
        def _(vv):
            ctx.wait(vv, v)
            vv.tensor_copy(wr_bf[:], wacc_r[:])
            vv.tensor_copy(wi_bf[:], wacc_i[:])
            v.inc(vv.tensor_scalar(nwi_bf[:], wacc_i[:], -1.0, 0.0,
                                   op0=ALU.mult, op1=ALU.add))
    with ctx.block() as blk:
        @blk.tensor
        def _(tt):
            ctx.wait(tt, v)
            ctx.wait(tt, dg)
            ctx.wait(tt, xn_ready)
            for it in range(32):
                m, n = divmod(it, 4)
                real = m < 4
                fcols = (m % 4) * 128
                ps = ps_a if it % 2 == 0 else ps_b
                if it >= 2:
                    ctx.wait(tt, ev, it - 1)
                first = wr_bf if real else wi_bf
                second = nwi_bf if real else wr_bf
                for c in range(4):
                    tt.matmul(ps, first[:, c * 512 + fcols:c * 512 + fcols + 128],
                              xn_bf[:, c * TLOC + n * 512: c * TLOC + (n + 1) * 512],
                              start=(c == 0), stop=False)
                for c in range(4):
                    ins = tt.matmul(
                        ps, second[:, c * 512 + fcols:c * 512 + fcols + 128],
                        xn_bf[:, (4 + c) * TLOC + n * 512: (4 + c) * TLOC + (n + 1) * 512],
                        start=False, stop=(c == 3))
                t.inc(ins)
        @blk.vector
        def _(vv):
            for it in range(32):
                m, n = divmod(it, 4)
                ps = ps_a if it % 2 == 0 else ps_b
                ctx.wait(vv, t, it + 1)
                xsl = slice(m * TLOC + n * 512, m * TLOC + (n + 1) * 512)
                ev.inc(vv.scalar_tensor_tensor(x_sb[:, xsl], ps,
                                               bias_col[:, m:m + 1],
                                               x_sb[:, xsl],
                                               op0=ALU.add, op1=ALU.add))
    return ev




def ph_l1_attn(ctx, lnsc, x_sb, xn_bf, ln_done):
    from concourse.aot_env import allow_ici_all_to_all

    RSH = 386
    a2a_in = ctx.dram("c_a2a_in", [8 * RSH, TLOC], BF16)
    a2a_out = ctx.dram("c_a2a_out", [8 * RSH, TLOC], BF16)
    ao_dram = [ctx.dram(f"c_ao{b_}", [128, 8192], BF16) for b_ in range(2)]
    rs_in = ctx.dram("c_rs_in", [8 * 1024, TLOC], BF16)
    rs_out = ctx.dram("c_rs_out", [1024, TLOC], BF16)
    nb_bounce = ctx.dram("c_nbb", [1, 2], F32)
    a2a_flat = a2a_in.ap().rearrange("r c -> (r c)")
    a2o_flat = a2a_out.ap().rearrange("r c -> (r c)")

    q_bh = ctx.AR[:, 0:4096].bitcast(BF16)
    k_bh = ctx.AR[:, 4096:8192].bitcast(BF16)
    v_bh = ctx.AR[:, 8192:12352].bitcast(BF16)
    pring4 = [ctx.AR[:, 12352 + s * 512:12352 + (s + 1) * 512].bitcast(BF16)
             for s in range(4)]
    masks = ctx.AR[:, 14400:15424].bitcast(BF16)
    bcsb = ctx.AR[0:64, 15424:15936]
    dscr = ctx.AR[0:1, 15936:16448]
    aorot = [ctx.AR[0:64, 16448 + s * 256:16448 + (s + 1) * 256].bitcast(BF16)
             for s in range(2)]
    wpart = ctx.AR[:, 12352:16448].bitcast(BF16)
    rotq = [ctx.AR[:, 16448 + s * 1024:16448 + (s + 1) * 1024].bitcast(BF16)
            for s in range(2)]
    rotv = [ctx.AR[:, 18496 + s * 520:18496 + (s + 1) * 520].bitcast(BF16)
            for s in range(2)]
    wo_sb = ctx.AR[:, 0:512].bitcast(BF16)
    stg7 = ctx.AR[:, 512:1024].bitcast(BF16)
    aold = [ctx.AR[:, 1536 + s * 256:1536 + (s + 1) * 256].bitcast(BF16)
            for s in range(4)]
    rsld = [ctx.AR[:, 2048 + s * 2048:2048 + (s + 1) * 2048] for s in range(2)]
    negc_bh = ctx.sbuf("c_negc", [128, 2], F32)
    nqm = ctx.sbuf("c_nqm", [2, 4], F32)
    ncrow = ctx.sbuf("c_ncrow", [1, 2], F32)
    if not hasattr(ctx, "c_halfones"):
        ctx.c_halfones = ctx.sbuf("c_halfones", [128, 2], BF16)

    dq_ship = dk_ship = None
    for wname in ("q", "k"):
        dgw = ctx.sem(); tw = ctx.sem(); vw = ctx.sem(); dww = ctx.sem()
        wdram = ctx.wqT if wname == "q" else ctx.wkT
        roff = 0 if wname == "q" else 128
        with ctx.block() as blk:
            @blk.gpsimd
            def _(g, wdram=wdram, wname=wname):
                if wname == "q":
                    g.memset(ctx.c_halfones[:], 0.0)
                    g.memset(ctx.c_halfones[0:64, 0:1], 1.0)
                    g.memset(ctx.c_halfones[64:128, 1:2], 1.0)
                load_weight_cm(ctx, g, wdram, wpart, dgw)
            @blk.tensor
            def _(tt):
                ctx.wait(tt, ln_done)
                ctx.wait(tt, dgw)
                for m in range(8):
                    if m >= 2:
                        ctx.wait(tt, vw, m - 1)
                    ps = ctx.PS[:, 0:2048] if m % 2 == 0 else ctx.PS[:, 2048:4096]
                    for n in range(4):
                        for c in range(8):
                            ins = tt.matmul(
                                ps[:, n * 512:(n + 1) * 512],
                                wpart[:, c * 1024 + m * 128: c * 1024 + m * 128 + 128],
                                xn_bf[:, c * TLOC + n * 512: c * TLOC + (n + 1) * 512],
                                start=(c == 0), stop=(c == 7))
                    tw.inc(ins)
            @blk.vector
            def _(vv):
                for m in range(8):
                    ctx.wait(vv, tw, m + 1)
                    if m >= 2:
                        ctx.wait(vv, dww, 16 * (m - 1))
                    ps = ctx.PS[:, 0:2048] if m % 2 == 0 else ctx.PS[:, 2048:4096]
                    vw.inc(vv.tensor_copy(rotq[m % 2][:], ps))
            @blk.sync
            def _(y, roff=roff):
                for m in range(8):
                    ctx.wait(y, vw, m + 1)
                    dww.inc(y.dma_start(
                        out=a2a_in.ap()[RSH * m + roff:RSH * m + roff + 128, :],
                        in_=rotq[m % 2][:]), 16)
        if wname == "q":
            dq_ship = dww
        else:
            dk_ship = dww
    dgv = ctx.sem(); tv = ctx.sem(); vv_ = ctx.sem(); dvv = ctx.sem(); gmz = ctx.sem()
    with ctx.block() as blk:
        @blk.gpsimd
        def _(g):
            load_weight_cm(ctx, g, ctx.wvT, wpart, dgv)
            g.memset(rotv[0][:].rearrange("p (h w) -> p h w", w=65)[:, :, 64:65], 1.0)
            gmz.inc(g.memset(
                rotv[1][:].rearrange("p (h w) -> p h w", w=65)[:, :, 64:65], 1.0))
        @blk.tensor
        def _(tt):
            ctx.wait(tt, dgv)
            ctx.wait(tt, gmz)
            for i in range(16):
                if i >= 2:
                    ctx.wait(tt, vv_, i - 1)
                ps = ctx.PS[:, 0:2048] if i % 2 == 0 else ctx.PS[:, 2048:4096]
                for h in range(2):
                    for c in range(8):
                        ins = tt.matmul(
                            ps[:, h * 512:(h + 1) * 512],
                            xn_bf[:, c * TLOC + i * 128: c * TLOC + (i + 1) * 128],
                            wpart[:, c * 1024 + h * 512: c * 1024 + (h + 1) * 512],
                            start=(c == 0), stop=(c == 7))
                tv.inc(ins)
        @blk.vector
        def _(vv):
            for i in range(16):
                ctx.wait(vv, tv, i + 1)
                if i >= 2:
                    ctx.wait(vv, dvv, 16 * 8 * (i - 1))
                ps = ctx.PS[:, 0:2048] if i % 2 == 0 else ctx.PS[:, 2048:4096]
                vv_.inc(vv.tensor_copy(
                    rotv[i % 2][:].rearrange("p (h w) -> p h w", w=65)[:, :, 0:64],
                    ps[:, 0:1024].rearrange("p (h w) -> p h w", w=64)))
        @blk.sync
        def _(y):
            for i in range(16):
                ctx.wait(y, vv_, i + 1)
                for jj in range(8):
                    off = (RSH * jj + 256) * TLOC + (i * 128) * 130
                    dvv.inc(y.dma_start(
                        out=a2a_flat[off:off + 128 * 130].rearrange(
                            "(p w) -> p w", w=130),
                        in_=rotv[i % 2][:, jj * 130:(jj + 1) * 130]), 16)

    ccs = ctx.sem(); dmk = ctx.sem()
    with ctx.block() as blk:
        @blk.gpsimd
        def _(g):
            ctx.wait(g, dq_ship); ctx.wait(g, dk_ship); ctx.wait(g, dvv)
            with allow_ici_all_to_all():
                ccs.inc(g.collective_compute(
                    "AllToAll", ALU.bypass, replica_groups=ALL8,
                    ins=[a2a_in.ap()], outs=[a2a_out.ap()]))
            for t4 in range(4):
                dmk.inc(g.dma_start(out=masks[:, t4 * 512:(t4 + 1) * 512],
                                    in_=ctx.cmask.ap()[t4, :, :]), 16)

    def rk(b_, g):
        return 4 * b_ + (g if g < 4 else 7 - g), (0 if g < 4 else 1)

    dao_list = []
    for b_ in range(2):
        dl = ctx.sem()
        with ctx.block() as blk:
            @blk.sync
            def _(y, b_=b_):
                ctx.wait(y, ccs)
                for g in range(8):
                    rg, sg = rk(b_, g)
                    dl.inc(y.dma_start(
                        out=q_bh[:, g * 1024:(g + 1) * 1024],
                        in_=a2a_out.ap()[RSH * rg:RSH * rg + 128,
                                         1024 * sg:1024 * sg + 1024]), 16)
                    dl.inc(y.dma_start(
                        out=k_bh[:, g * 1024:(g + 1) * 1024],
                        in_=a2a_out.ap()[RSH * rg + 128:RSH * rg + 256,
                                         1024 * sg:1024 * sg + 1024]), 16)
                    off = (RSH * rg + 256) * TLOC + sg * 1024 * 130
                    dl.inc(y.dma_start(
                        out=v_bh[:, g * 1040:(g + 1) * 1040].rearrange(
                            "p (i w) -> p i w", w=130),
                        in_=a2o_flat[off:off + 1024 * 130].rearrange(
                            "(i p w) -> p i w", p=128, w=130)), 16)

        sn = ctx.sem(); tn = ctx.sem(); vn = ctx.sem(); dn = ctx.sem(); tb2 = ctx.sem()
        with ctx.block() as blk:
            @blk.scalar
            def _(ss):
                ctx.wait(ss, dl)
                for w2 in range(4):
                    src = q_bh if w2 < 2 else k_bh
                    half = w2 % 2
                    if w2 > 0:
                        ctx.wait(ss, tn, 8 * w2)
                    sn.inc(ss.activation(lnsc.sq_bf[:, 0:4096],
                                         src[:, half * 4096:(half + 1) * 4096],
                                         AF.Square))
            @blk.tensor
            def _(tt):
                vn_after = [1, 3, 4, 6]
                for w2 in range(4):
                    ctx.wait(tt, sn, w2 + 1)
                    if w2 > 0:
                        ctx.wait(tt, vn, vn_after[w2 - 1])
                    for n in range(8):
                        ins = tt.matmul(
                            ctx.PS[0:2, n * 512:(n + 1) * 512],
                            ctx.c_halfones[:], lnsc.sq_bf[:, n * 512:(n + 1) * 512],
                            start=True, stop=True)
                        tn.inc(ins)
            @blk.vector
            def _(vv):
                cnt = 0
                for w2 in range(4):
                    col, half = w2 // 2, w2 % 2
                    ctx.wait(vv, tn, 8 * (w2 + 1))
                    if half == 0:
                        m = vn.inc(vv.tensor_reduce(nqm[0:2, col:col + 1],
                                                    ctx.PS[0:2, 0:4096],
                                                    axis=AXL.X, op=ALU.max))
                        ctx.wait(vv, vn, m)
                    else:
                        m = vn.inc(vv.tensor_reduce(nqm[0:2, 2 + col:3 + col],
                                                    ctx.PS[0:2, 0:4096],
                                                    axis=AXL.X, op=ALU.max))
                        ctx.wait(vv, vn, m)
                        m = vn.inc(vv.tensor_tensor(nqm[0:2, col:col + 1],
                                                    nqm[0:2, col:col + 1],
                                                    nqm[0:2, 2 + col:3 + col],
                                                    op=ALU.max))
                        ctx.wait(vv, vn, m)
            @blk.scalar
            def _(ss):
                ctx.wait(ss, vn)
                m = sn.inc(ss.activation(nqm[0:2, 2:3], nqm[0:2, 0:1], AF.Sqrt))
                ctx.wait(ss, sn, m)
                sn.inc(ss.activation(nqm[0:2, 3:4], nqm[0:2, 1:2], AF.Sqrt))
            @blk.vector
            def _(vv):
                ctx.wait(vv, sn)
                m = vn.inc(vv.tensor_tensor(nqm[0:2, 0:1], nqm[0:2, 2:3],
                                            nqm[0:2, 3:4], op=ALU.mult))
                ctx.wait(vv, vn, m)
                vn.inc(vv.tensor_scalar(nqm[0:2, 1:2], nqm[0:2, 0:1],
                                        -0.126, 0.0, op0=ALU.mult, op1=ALU.add))
            @blk.sync
            def _(y):
                ctx.wait(y, vn)
                dn.inc(y.dma_start(out=nb_bounce.ap(), in_=nqm[0:2, 1:2],
                                   allow_slow_non_contiguous=True), 16)
                ctx.wait(y, dn)
                dn.inc(y.dma_start(out=ncrow[:], in_=nb_bounce.ap()), 16)
            @blk.tensor
            def _(tt):
                ctx.wait(tt, dn)
                for h in range(2):
                    tb2.inc(tt.matmul(
                        ctx.PS[:, 2048 + h * 512:2048 + h * 512 + 1],
                        lnsc.ones_row_f[:], ncrow[0:1, h:h + 1],
                        start=True, stop=True))
            @blk.vector
            def _(vv):
                ctx.wait(vv, tb2)
                for h in range(2):
                    vn.inc(vv.tensor_copy(
                        negc_bh[:, h:h + 1],
                        ctx.PS[:, 2048 + h * 512:2048 + h * 512 + 1]))

        tl = ctx.sem(); se = ctx.sem(); vm = ctx.sem(); vf = ctx.sem(); dao_b = ctx.sem()

        def maskops2(bp2):
            ops = []
            for sc in range(2):
                off = bp2 * 128 - sc * 512
                if off <= -128:
                    continue
                ops.append((sc, "zero" if off >= 512 else off // 128))
            return ops

        seq = []
        for h in range(2):
            for qq in range(8):
                for g in range(qq + 1):
                    for bp2 in range(8):
                        seq.append((h, qq, g, bp2))
        n_j = len(seq)
        is_first = [g == 0 and bp2 == 0 for (h, qq, g, bp2) in seq]
        is_last = [g == qq and bp2 == 7 for (h, qq, g, bp2) in seq]

        vmM = [0] * n_j
        _vmc = 0
        for j, (h, qq, g, bp2) in enumerate(seq):
            if g == qq:
                _vmc += len(maskops2(bp2))
            vmM[j] = _vmc
        tlL = [0] * n_j; tlPV = [0] * n_j
        tlB = {}; vfR = {}; vfE = {}; vfA = {}
        fin_f = [None] * n_j
        _tl = 2; tlL[0] = 1; tlL[1] = 2
        _f = 0
        for j in range(n_j):
            if j + 2 < n_j:
                _tl += 1; tlL[j + 2] = _tl
            _tl += 1; tlPV[j] = _tl
            if is_last[j]:
                fin_f[j] = (_f, _f + 1)
                for sc in range(2):
                    _tl += 1; tlB[_f + sc] = _tl
                _f += 2
        _vf = 0
        for j in range(n_j):
            if is_last[j]:
                for f in fin_f[j]:
                    _vf += 1; vfR[f] = _vf
                    _vf += 1; vfE[f] = _vf
                    _vf += 1; vfA[f] = _vf
        nfin = _f

        with ctx.block() as blk:
            @blk.tensor
            def _(tt, b_=b_):
                ctx.wait(tt, vn)
                ctx.wait(tt, dmk)
                prev_f = [None]

                def emit_L(m):
                    h, qq, g, bp2 = seq[m]
                    if m >= 2:
                        ctx.wait(tt, se, m - 1)
                    for sc in range(2):
                        ins = tt.matmul(
                            ctx.PS[:, (m % 2) * 1024 + sc * 512:
                                   (m % 2) * 1024 + sc * 512 + 512],
                            k_bh[64 * h:64 * h + 64,
                                 g * 1024 + bp2 * 128:g * 1024 + (bp2 + 1) * 128],
                            q_bh[64 * h:64 * h + 64,
                                 qq * 1024 + sc * 512:qq * 1024 + sc * 512 + 512],
                            start=True, stop=True, skip_group_check=True)
                    tl.inc(ins)
                    assert tl.n == tlL[m]

                def emit_PV(j):
                    h, qq, g, bp2 = seq[j]
                    ctx.wait(tt, se, j + 1)
                    if g == qq:
                        ctx.wait(tt, vm, vmM[j])
                    if is_first[j] and prev_f[0] is not None:
                        ctx.wait(tt, vf, vfA[prev_f[0]])
                    for sc in range(2):
                        ins = tt.matmul(
                            ctx.PS[0:65, (4 + sc) * 512:(4 + sc) * 512 + 512],
                            v_bh[:, (g * 8 + bp2) * 130 + 65 * h:
                                 (g * 8 + bp2) * 130 + 65 * h + 65],
                            pring4[j % 4][:, sc * 512:(sc + 1) * 512],
                            start=(g == 0 and bp2 == 0),
                            stop=(g == qq and bp2 == 7),
                            skip_group_check=True)
                    tl.inc(ins)
                    assert tl.n == tlPV[j]

                emit_L(0); emit_L(1)
                for j in range(n_j):
                    if j + 2 < n_j:
                        emit_L(j + 2)
                    emit_PV(j)
                    if is_last[j]:
                        for sc in range(2):
                            f = fin_f[j][sc]
                            ctx.wait(tt, vf, vfR[f])
                            ins = tt.matmul(ctx.PS[0:64, 3072:3584],
                                            lnsc.ones_row_f[0:1, 0:64],
                                            dscr[0:1, :], start=True, stop=True,
                                            skip_group_check=True)
                            tl.inc(ins)
                            assert tl.n == tlB[f]
                        prev_f[0] = fin_f[j][1]
            @blk.scalar
            def _(ss):
                for j in range(n_j):
                    h, qq, g, bp2 = seq[j]
                    ctx.wait(ss, tl, tlL[j])
                    se.inc(ss.activation(pring4[j % 4][:],
                                         ctx.PS[:, (j % 2) * 1024:
                                                (j % 2) * 1024 + 1024],
                                         AF.Exp, bias=negc_bh[:, h:h + 1],
                                         scale=0.125))
            @blk.vector
            def _(vv, b_=b_):
                for j in range(n_j):
                    h, qq, g, bp2 = seq[j]
                    if g == qq:
                        ctx.wait(vv, se, j + 1)
                        for (sc, kind) in maskops2(bp2):
                            sl2 = slice(sc * 512, (sc + 1) * 512)
                            if kind == "zero":
                                m = vm.inc(vv.memset(pring4[j % 4][:, sl2], 0.0))
                            else:
                                m = vm.inc(vv.tensor_tensor(
                                    pring4[j % 4][:, sl2], pring4[j % 4][:, sl2],
                                    masks[:, kind * 512:(kind + 1) * 512],
                                    op=ALU.mult))
                            ctx.wait(vv, vm, m)
                        assert vm.n == vmM[j]
                    if is_last[j]:
                        for sc in range(2):
                            f = fin_f[j][sc]
                            ctx.wait(vv, tl, tlPV[j])
                            if f >= 1:
                                ctx.wait(vv, tl, tlB[f - 1])
                            m = vf.inc(vv.reciprocal(
                                dscr[0:1, :],
                                ctx.PS[64:65, (4 + sc) * 512:(4 + sc) * 512 + 512]))
                            assert vf.n == vfR[f]
                            ctx.wait(vv, tl, tlB[f])
                            if f >= 2:
                                ctx.wait(vv, dao_b, 16 * (f - 1))
                            m = vf.inc(vv.tensor_copy(bcsb[:], ctx.PS[0:64, 3072:3584]))
                            assert vf.n == vfE[f]
                            ctx.wait(vv, vf, m)
                            vf.inc(vv.tensor_tensor(
                                aorot[f % 2][:],
                                ctx.PS[0:64, (4 + sc) * 512:(4 + sc) * 512 + 512],
                                bcsb[:], op=ALU.mult))
                            assert vf.n == vfA[f]
            @blk.sync
            def _(y, b_=b_):
                for j in range(n_j):
                    if not is_last[j]:
                        continue
                    h, qq, g, bp2 = seq[j]
                    for sc in range(2):
                        f = fin_f[j][sc]
                        ctx.wait(y, vf, vfA[f])
                        dao_b.inc(y.dma_start(
                            out=ao_dram[b_].ap()[64 * h:64 * h + 64,
                                                 qq * 1024 + sc * 512:
                                                 qq * 1024 + sc * 512 + 512],
                            in_=aorot[f % 2][:]), 16)
        dao_list.append(dao_b)

    dgo = ctx.sem(); t7 = ctx.sem(); v7 = ctx.sem(); d7 = ctx.sem()
    dld_sl = [ctx.sem() for _ in range(4)]
    with ctx.block() as blk:
        @blk.gpsimd
        def _(g):
            load_weight_cm(ctx, g, ctx.woT_pair, wo_sb, dgo, nch=1)
    with ctx.block() as blk:
        @blk.scalar
        def _(ss):
            ctx.wait(ss, dgo)
            for dab in dao_list:
                ctx.wait(ss, dab)
            for b_ in range(2):
                for n in range(16):
                    i = b_ * 16 + n
                    if i >= 4:
                        ctx.wait(ss, t7, 8 * (i - 3))
                    dld_sl[i % 4].inc(ss.dma_start(
                        out=aold[i % 4][:],
                        in_=ao_dram[b_].ap()[:, n * 512:(n + 1) * 512]), 16)
        @blk.tensor
        def _(tt):
            for b_ in range(2):
                for n in range(16):
                    i = b_ * 16 + n
                    ctx.wait(tt, dld_sl[i % 4], 16 * (i // 4 + 1))
                    if i >= 1:
                        ctx.wait(tt, v7, 8 * i)
                    for m in range(8):
                        if m >= 4:
                            ctx.wait(tt, v7, 8 * i + m - 3)
                        ins = tt.matmul(
                            ctx.PS[:, (m % 4) * 512:(m % 4 + 1) * 512],
                            wo_sb[:, m * 128:(m + 1) * 128],
                            aold[i % 4][:], start=True, stop=True,
                            skip_group_check=True)
                        t7.inc(ins)
        @blk.vector
        def _(vv):
            for b_ in range(2):
                for n in range(16):
                    i = b_ * 16 + n
                    for m in range(8):
                        ctx.wait(vv, t7, 8 * i + m + 1)
                        if 8 * i + m >= 2:
                            ctx.wait(vv, d7, 16 * (8 * i + m - 1))
                        v7.inc(vv.tensor_copy(
                            stg7[:, (m % 2) * 512:(m % 2 + 1) * 512],
                            ctx.PS[:, (m % 4) * 512:(m % 4 + 1) * 512]))
        @blk.sync
        def _(y):
            for b_ in range(2):
                for n in range(16):
                    i = b_ * 16 + n
                    g = n // 2
                    rg, sg = rk(b_, g)
                    for m in range(8):
                        ctx.wait(y, v7, 8 * i + m + 1)
                        d7.inc(y.dma_start(
                            out=rs_in.ap()[1024 * rg + m * 128:1024 * rg + (m + 1) * 128,
                                           1024 * sg + (n % 2) * 512:
                                           1024 * sg + (n % 2) * 512 + 512],
                            in_=stg7[:, (m % 2) * 512:(m % 2 + 1) * 512]), 16)
    ccs2 = ctx.sem()
    with ctx.block() as blk:
        @blk.gpsimd
        def _(g):
            ctx.wait(g, d7)
            ccs2.inc(g.collective_compute(
                "ReduceScatter", ALU.add, replica_groups=ALL8,
                ins=[rs_in.ap()], outs=[rs_out.ap()]))
    dr8 = ctx.sem(); v8 = ctx.sem()
    with ctx.block() as blk:
        @blk.gpsimd
        def _(g):
            ctx.wait(g, ccs2)
            for m in range(8):
                if m >= 2:
                    ctx.wait(g, v8, m - 1)
                dr8.inc(g.dma_start(out=rsld[m % 2][:],
                                    in_=rs_out.ap()[m * 128:(m + 1) * 128, :]), 16)
        @blk.vector
        def _(vv):
            for m in range(8):
                ctx.wait(vv, dr8, 16 * (m + 1))
                v8.inc(vv.tensor_tensor(x_sb[:, m * TLOC:(m + 1) * TLOC],
                                        x_sb[:, m * TLOC:(m + 1) * TLOC],
                                        rsld[m % 2][:], op=ALU.add))
    return v8

def ph_l1_moe(ctx, lnsc, x_sb, xn_bf, ln_done):
    w1q = [ctx.AR[:, b * 8192:b * 8192 + 4096].bitcast(BF16) for b in range(2)]
    w2q = [ctx.AR[:, b * 8192 + 4096:(b + 1) * 8192].bitcast(BF16) for b in range(2)]
    h1q = ctx.AR[:, 16384:18432].bitcast(BF16)
    gbc4 = [ctx.AR[:, 18432 + t * 512:18432 + (t + 1) * 512] for t in range(4)]
    tmp2 = [ctx.AR[:, 20480 + s * 512:20480 + (s + 1) * 512] for s in range(2)]
    grow4 = [ctx.AR[0:1, 21504 + s * 256:21504 + (s + 1) * 256].bitcast(BF16)
             for s in range(4)]
    ident_f = ctx.AR[:, 22528:22656]
    gtm = ctx.AR[:, 22656:22664]
    gsc = ctx.AR[:, 22664:22696]
    g_fm = ctx.AR[0:8, 23552:24576].bitcast(BF16)
    b1col = ctx.sbuf("d_b1col", [128, 256], F32)
    b2col = ctx.sbuf("d_b2col", [128, 64], F32)
    rwT_sb = ctx.sbuf("d_rwT", [128, 64], BF16)
    ones128_bf = ctx.sbuf("d_ones128", [1, 128], BF16)

    NIT = 128

    def it_dec(it):
        eq, tc = divmod(it, 4)
        e, Q = divmod(eq, 4)
        return eq, e, Q, tc, 4 * e + tc

    tmGbc = {}; tmMm1 = {}; tmMm2 = {}
    _t = 0
    for it in range(NIT):
        eq, e, Q, tc, gidx = it_dec(it)
        if Q == 0:
            _t += 1; tmGbc[gidx] = _t
        for m in range(8):
            _t += 1; tmMm1[(it, m)] = _t
        for dm in range(8):
            _t += 1; tmMm2[(it, dm)] = _t
    vgGbcC = {}; vgMult = {}; vgAdd = {}; vgB2 = {}
    _v = 0
    for it in range(NIT):
        eq, e, Q, tc, gidx = it_dec(it)
        if Q == 0:
            _v += 1; vgGbcC[gidx] = _v
        for dm in range(8):
            _v += 1; vgMult[(it, dm)] = _v
            _v += 1; vgAdd[(it, dm)] = _v
            if Q == 3:
                _v += 1; vgB2[(it, dm)] = _v

    dgs = ctx.sem(); dwb = [ctx.sem(), ctx.sem()]
    dgr = [ctx.sem() for _ in range(4)]
    tm = ctx.sem(); sg = ctx.sem(); vg = ctx.sem()

    def w_loads(g, eq):
        e, Q = divmod(eq, 4)
        b = eq % 2
        for c in range(8):
            dwb[b].inc(g.dma_start(
                out=w1q[b][:, c * 1024:(c + 1) * 1024],
                in_=ctx.moe_w1t.ap()[1024 * e + c * 128:1024 * e + (c + 1) * 128,
                                     1024 * Q:1024 * (Q + 1)]), 16)
        for k in range(8):
            dwb[b].inc(g.dma_start(
                out=w2q[b][:, k * 1024:(k + 1) * 1024],
                in_=ctx.moe_w2t.ap()[4096 * e + 1024 * Q + k * 128:
                                     4096 * e + 1024 * Q + (k + 1) * 128, :]), 16)

    with ctx.block() as blk:
        @blk.gpsimd
        def _(g):
            w_loads(g, 0)
            w_loads(g, 1)
            g.memset(ones128_bf[:], 1.0)
            for c in range(8):
                dgs.inc(g.dma_start(out=rwT_sb[:, c * 8:(c + 1) * 8],
                                    in_=ctx.rwT.ap()[c * 128:(c + 1) * 128, :]), 16)
            for c in range(256):
                dgs.inc(g.dma_start(out=b1col[:, c:c + 1],
                                    in_=ctx.b1row.ap()[0:1, c * 128:(c + 1) * 128]), 16)
            for c in range(64):
                dgs.inc(g.dma_start(out=b2col[:, c:c + 1],
                                    in_=ctx.b2row.ap()[c // 8:c // 8 + 1,
                                                       (c % 8) * 128:(c % 8 + 1) * 128]), 16)
            dgs.inc(g.dma_start(out=ident_f[:], in_=ctx.ident.ap()), 16)

    tr = ctx.sem(); vr = ctx.sem(); sr = ctx.sem()
    VPI = 12
    with ctx.block() as blk:
        @blk.tensor
        def _(tt):
            ctx.wait(tt, ln_done)
            ctx.wait(tt, dgs)
            for i in range(16):
                if i >= 2:
                    ctx.wait(tt, vr, VPI * (i - 1))
                    ctx.wait(tt, sr, i - 1)
                pcol = (i % 2) * 512
                for c in range(8):
                    ins = tt.matmul(
                        ctx.PS[:, pcol:pcol + 8],
                        xn_bf[:, c * TLOC + i * 128: c * TLOC + (i + 1) * 128],
                        rwT_sb[:, c * 8:(c + 1) * 8],
                        start=(c == 0), stop=(c == 7))
                tr.inc(ins)
                if i >= 1:
                    ctx.wait(tt, vr, VPI * (i - 1) + 12)
                    tr.inc(tt.transpose(ctx.PS[0:8, 1024:1152], gtm[:, 0:8],
                                        ident_f[:]))
            ctx.wait(tt, vr, VPI * 16)
            tr.inc(tt.transpose(ctx.PS[0:8, 1024:1152], gtm[:, 0:8], ident_f[:]))
        @blk.vector
        def _(vv):
            for i in range(16):
                pcol = (i % 2) * 512
                ps = ctx.PS[:, pcol:pcol + 8]
                ctx.wait(vv, tr, 2 * i + 1 if i >= 1 else 1)
                m = vr.inc(vv.tensor_reduce(gsc[:, 16:17], ps, axis=AXL.X,
                                            op=ALU.max, negate=True))
                ctx.wait(vv, vr, m)
                m = vr.inc(vv.tensor_scalar(gsc[:, 8:16], ps, gsc[:, 16:17], 0.0,
                                            op0=ALU.add, op1=ALU.add))
                ctx.wait(vv, vr, m)
                m = vr.inc(vv.tensor_scalar(gsc[:, 24:32], gsc[:, 8:16], 0.0,
                                            -1e30, op0=ALU.is_ge, op1=ALU.mult))
                ctx.wait(vv, vr, m)
                m = vr.inc(vv.tensor_tensor(gsc[:, 24:32], gsc[:, 8:16],
                                            gsc[:, 24:32], op=ALU.add))
                ctx.wait(vv, vr, m)
                m = vr.inc(vv.tensor_reduce(gsc[:, 17:18], gsc[:, 24:32],
                                            axis=AXL.X, op=ALU.max, negate=True))
                ctx.wait(vv, vr, m)
                m = vr.inc(vv.tensor_scalar(gsc[:, 24:32], gsc[:, 8:16],
                                            gsc[:, 17:18], 1.0,
                                            op0=ALU.add, op1=ALU.mult))
                ctx.wait(vv, vr, m)
                m = vr.inc(vv.tensor_scalar(gsc[:, 8:16], gsc[:, 24:32], 0.0, 1.0,
                                            op0=ALU.is_ge, op1=ALU.mult))
                ctx.wait(vv, vr, m)
                ctx.wait(vv, sr, i + 1)
                m = vr.inc(vv.tensor_tensor(gsc[:, 24:32], gsc[:, 0:8],
                                            gsc[:, 8:16], op=ALU.mult))
                ctx.wait(vv, vr, m)
                m = vr.inc(vv.tensor_reduce(gsc[:, 18:19], gsc[:, 24:32],
                                            axis=AXL.X, op=ALU.add))
                ctx.wait(vv, vr, m)
                m = vr.inc(vv.reciprocal(gsc[:, 19:20], gsc[:, 18:19]))
                ctx.wait(vv, vr, m)
                if i >= 1:
                    ctx.wait(vv, tr, 2 * i + 1)
                m = vr.inc(vv.tensor_scalar(gtm[:, 0:8], gsc[:, 24:32],
                                            gsc[:, 19:20], 1.0,
                                            op0=ALU.mult, op1=ALU.mult))
                ctx.wait(vv, vr, m)
                if i >= 1:
                    m = vr.inc(vv.tensor_copy(g_fm[:, (i - 1) * 128:i * 128],
                                              ctx.PS[0:8, 1024:1152]))
                else:
                    m = vr.inc(vv.tensor_copy(gsc[0:1, 0:1], gsc[0:1, 0:1]))
                ctx.wait(vv, vr, m)
            ctx.wait(vv, tr, 32)
            vr.inc(vv.tensor_copy(g_fm[:, 15 * 128:16 * 128], ctx.PS[0:8, 1024:1152]))
        @blk.scalar
        def _(ss):
            for i in range(16):
                pcol = (i % 2) * 512
                ctx.wait(ss, vr, VPI * i + 1)
                sr.inc(ss.activation(gsc[:, 0:8], ctx.PS[:, pcol:pcol + 8],
                                     AF.Exp, bias=gsc[:, 16:17]))

    with ctx.block() as blk:
        @blk.gpsimd
        def _(g):
            for eq in range(32):
                if eq % 4 == 0:
                    e = eq // 4
                    for t4 in range(4):
                        gidx = 4 * e + t4
                        if gidx >= 4:
                            ctx.wait(g, tm, tmGbc[gidx - 4])
                        dgr[gidx % 4].inc(g.dma_start(
                            out=grow4[gidx % 4][0:1, :],
                            in_=g_fm[e:e + 1, t4 * 512:(t4 + 1) * 512]), 16)
                if eq >= 2:
                    ctx.wait(g, tm, tmMm2[((eq - 2) * 4 + 3, 7)])
                    w_loads(g, eq)
        @blk.tensor
        def _(tt):
            ctx.wait(tt, dgs)
            for it in range(NIT):
                eq, e, Q, tc, gidx = it_dec(it)
                if tc == 0:
                    ctx.wait(tt, dwb[eq % 2], 16 * 16 * (eq // 2 + 1))
                if Q == 0:
                    ctx.wait(tt, dgr[gidx % 4], 16 * (gidx // 4 + 1))
                    if gidx >= 1:
                        ctx.wait(tt, vg, vgGbcC[gidx - 1])
                    ins = tt.matmul(ctx.PS[:, 2048:2560], ones128_bf[:],
                                    grow4[gidx % 4][:], start=True, stop=True)
                    tm.inc(ins)
                    assert tm.n == tmGbc[gidx]
                for m in range(8):
                    if 8 * it + m - 1 > 0:
                        ctx.wait(tt, sg, 8 * it + m - 1)
                    for c in range(8):
                        ins = tt.matmul(
                            ctx.PS[:, (m % 2) * 512:(m % 2) * 512 + 512],
                            w1q[eq % 2][:, c * 1024 + m * 128:c * 1024 + m * 128 + 128],
                            xn_bf[:, c * TLOC + tc * 512:c * TLOC + (tc + 1) * 512],
                            start=(c == 0), stop=(c == 7))
                    tm.inc(ins)
                    assert tm.n == tmMm1[(it, m)]
                ctx.wait(tt, sg, 8 * (it + 1))
                for dm in range(8):
                    if dm >= 2:
                        ctx.wait(tt, vg, vgMult[(it, dm - 2)])
                    elif it >= 1:
                        ctx.wait(tt, vg, vgMult[(it - 1, dm + 6)])
                    for k in range(8):
                        ins = tt.matmul(
                            ctx.PS[:, 1024 + (dm % 2) * 512:1024 + (dm % 2) * 512 + 512],
                            w2q[eq % 2][:, k * 1024 + dm * 128:k * 1024 + dm * 128 + 128],
                            h1q[:, k * 512:(k + 1) * 512],
                            start=(k == 0), stop=(k == 7))
                    tm.inc(ins)
                    assert tm.n == tmMm2[(it, dm)]
        @blk.scalar
        def _(ss):
            for it in range(NIT):
                eq, e, Q, tc, gidx = it_dec(it)
                for m in range(8):
                    ctx.wait(ss, tm, tmMm1[(it, m)])
                    sg.inc(ss.activation(h1q[:, m * 512:(m + 1) * 512],
                                         ctx.PS[:, (m % 2) * 512:(m % 2) * 512 + 512],
                                         AF.Gelu,
                                         bias=b1col[:, e * 32 + Q * 8 + m:
                                                    e * 32 + Q * 8 + m + 1]))
        @blk.vector
        def _(vv):
            for it in range(NIT):
                eq, e, Q, tc, gidx = it_dec(it)
                if Q == 0:
                    ctx.wait(vv, tm, tmGbc[gidx])
                    m = vg.inc(vv.tensor_copy(gbc4[tc][:], ctx.PS[:, 2048:2560]))
                    assert vg.n == vgGbcC[gidx]
                    ctx.wait(vv, vg, m)
                for dm in range(8):
                    ctx.wait(vv, tm, tmMm2[(it, dm)])
                    m = vg.inc(vv.tensor_tensor(
                        tmp2[dm % 2][:],
                        ctx.PS[:, 1024 + (dm % 2) * 512:1024 + (dm % 2) * 512 + 512],
                        gbc4[tc][:], op=ALU.mult))
                    assert vg.n == vgMult[(it, dm)]
                    ctx.wait(vv, vg, m)
                    m = vg.inc(vv.tensor_tensor(
                        x_sb[:, dm * TLOC + tc * 512:dm * TLOC + (tc + 1) * 512],
                        x_sb[:, dm * TLOC + tc * 512:dm * TLOC + (tc + 1) * 512],
                        tmp2[dm % 2][:], op=ALU.add))
                    assert vg.n == vgAdd[(it, dm)]
                    if Q == 3:
                        ctx.wait(vv, vg, m)
                        vg.inc(vv.scalar_tensor_tensor(
                            x_sb[:, dm * TLOC + tc * 512:dm * TLOC + (tc + 1) * 512],
                            gbc4[tc][:], b2col[:, e * 8 + dm:e * 8 + dm + 1],
                            x_sb[:, dm * TLOC + tc * 512:dm * TLOC + (tc + 1) * 512],
                            op0=ALU.mult, op1=ALU.add))
                        assert vg.n == vgB2[(it, dm)]
    return vg

def shard_x(x):
    out = []
    for r in range(NCORES):
        b, j = r // 4, r % 4
        tok = np.concatenate([x[b, j * 1024:(j + 1) * 1024],
                              x[b, (7 - j) * 1024:(8 - j) * 1024]], axis=0)
        out.append(np.ascontiguousarray(tok.T))
    return out


def unshard_x(parts):
    x = np.empty((B, S, D), np.float32)
    for r in range(NCORES):
        b, j = r // 4, r % 4
        tok = parts[r].T
        x[b, j * 1024:(j + 1) * 1024] = tok[:1024]
        x[b, (7 - j) * 1024:(8 - j) * 1024] = tok[1024:]
    return x


def stage_ffn_weights(inp):
    G1R, G2R = np.asarray(inp["L0_ffn_G1_R"]), np.asarray(inp["L0_ffn_G2_R"])
    G1I, G2I = np.asarray(inp["L0_ffn_G1_I"]), np.asarray(inp["L0_ffn_G2_I"])

    def arep(G1):
        A = np.transpose(np.asarray(G1)[0], (1, 2, 0))
        return np.ascontiguousarray(
            np.broadcast_to(A[:, :, None, :, None],
                            (4, 32, 16, 32, 16)).reshape(4, 512, 512))

    def brep(G2):
        Bm = np.transpose(np.asarray(G2)[:, :, 0, :], (0, 2, 1))
        return np.ascontiguousarray(
            np.broadcast_to(Bm[:, None, :, None, :],
                            (4, 32, 16, 32, 16)).reshape(4, 512, 512))

    br = np.asarray(inp["L0_ffn_bias_real"]).reshape(512, 1)
    bi = np.asarray(inp["L0_ffn_bias_imag"]).reshape(512, 1)
    bias_ri = np.ascontiguousarray(
        np.concatenate([br, bi], 0).astype(np.float32))
    return (arep(G1R).astype(np.float32), brep(G2R).astype(np.float32),
            arep(G1I).astype(np.float32), brep(G2I).astype(np.float32), bias_ri)


def stage_l0_attn_weights(inp):
    wq = np.ascontiguousarray(np.asarray(inp["L0_attn_Wq"], np.float32))
    wk = np.ascontiguousarray(np.asarray(inp["L0_attn_Wk"], np.float32))
    wv = np.ascontiguousarray(np.asarray(inp["L0_attn_Wv"], np.float32))
    wo = np.ascontiguousarray(np.asarray(inp["L0_attn_Wo"], np.float32))
    bo_gb = np.ascontiguousarray(np.stack(
        [np.asarray(inp["L0_attn_bo"], np.float32),
         np.asarray(inp["L0_attn_gamma"], np.float32),
         np.asarray(inp["L0_attn_beta"], np.float32)], axis=1))
    return wq, wk, wv, wo, bo_gb



def build_stage_a():
    ctx = Ctx()
    nc = ctx.nc
    x_in = nc.dram_tensor("x_in", [D, TLOC], F32, kind="ExternalInput")
    x_out = nc.dram_tensor("x_out", [D, TLOC], F32, kind="ExternalOutput")
    ctx.wq = nc.dram_tensor("wq", [D, D], F32, kind="ExternalInput")
    ctx.wk = nc.dram_tensor("wk", [D, D], F32, kind="ExternalInput")
    ctx.wv = nc.dram_tensor("wv", [D, D], F32, kind="ExternalInput")
    ctx.wo = nc.dram_tensor("wo", [D, D], F32, kind="ExternalInput")
    ctx.bo_gb = nc.dram_tensor("bo_gb", [D, 3], F32, kind="ExternalInput")

    x_sb = ctx.sbuf("x_sb", [128, NCH * TLOC], F32)
    xn_bf = ctx.sbuf("xn_bf", [128, NCH * TLOC], BF16)
    ctx.arena_init()
    lnsc = alloc_ln_scratch(ctx)

    dsem = ctx.sem()
    ph_load_x(ctx, x_in, x_sb, dsem)
    csem = ph_init_consts(ctx, lnsc)
    ln_done = ph_ln(ctx, lnsc, x_sb, xn_bf, deps=(dsem, csem))
    fin = ph_l0_attn(ctx, lnsc, x_sb, xn_bf, ln_done)
    ph_store_x(ctx, x_sb, x_out, (fin,))
    ctx.close()
    return ctx


def run_stage_a(x_parts, inp):
    wq, wk, wv, wo, bo_gb = stage_l0_attn_weights(inp)
    ctx = build_stage_a()
    in_maps = [{"x_in": x_parts[r], "wq": wq, "wk": wk, "wv": wv, "wo": wo,
                "bo_gb": bo_gb} for r in range(NCORES)]
    res = run_bass_kernel_spmd(ctx.nc, in_maps, core_ids=list(range(NCORES)))
    return [res.results[r]["x_out"] for r in range(NCORES)]


def build_stage_b():
    ctx = Ctx()
    nc = ctx.nc
    x_in = nc.dram_tensor("x_in", [D, TLOC], F32, kind="ExternalInput")
    x_out = nc.dram_tensor("x_out", [D, TLOC], F32, kind="ExternalOutput")
    ctx.arep_r = nc.dram_tensor("arep_r", [4, 512, 512], F32, kind="ExternalInput")
    ctx.brep_r = nc.dram_tensor("brep_r", [4, 512, 512], F32, kind="ExternalInput")
    ctx.arep_i = nc.dram_tensor("arep_i", [4, 512, 512], F32, kind="ExternalInput")
    ctx.brep_i = nc.dram_tensor("brep_i", [4, 512, 512], F32, kind="ExternalInput")
    ctx.bias_ri = nc.dram_tensor("bias_ri", [D, 1], F32, kind="ExternalInput")

    x_sb = ctx.sbuf("x_sb", [128, NCH * TLOC], F32)
    xn_bf = ctx.sbuf("xn_bf", [128, NCH * TLOC], BF16)
    ctx.arena_init()
    lnsc = alloc_ln_scratch(ctx)

    dsem = ctx.sem()
    ph_load_x(ctx, x_in, x_sb, dsem)
    csem = ph_init_consts(ctx, lnsc)
    ln_done = ph_ln(ctx, lnsc, x_sb, xn_bf, deps=(dsem, csem))
    ev = ph_l0_ffn(ctx, lnsc, x_sb, xn_bf, ln_done)
    ph_store_x(ctx, x_sb, x_out, (ev,))
    ctx.close()
    return ctx


def run_stage_b(x_parts, inp):
    ar, br_, ai, bi_, bias = stage_ffn_weights(inp)
    ctx = build_stage_b()
    in_maps = [{"x_in": x_parts[r], "arep_r": ar, "brep_r": br_,
                "arep_i": ai, "brep_i": bi_, "bias_ri": bias}
               for r in range(NCORES)]
    res = run_bass_kernel_spmd(ctx.nc, in_maps, core_ids=list(range(NCORES)))
    return [res.results[r]["x_out"] for r in range(NCORES)]




def stage_l1_attn_weights(inp, r):
    wqkv = np.asarray(inp["L1_attn_w_qkv"], np.float32)
    wout = np.asarray(inp["L1_attn_w_out"], np.float32)
    wqT = np.ascontiguousarray(wqkv[0:1024].T)
    wkT = np.ascontiguousarray(wqkv[1024:2048].T)
    wvT = np.ascontiguousarray(wqkv[2048:3072].T)
    woT = np.ascontiguousarray(wout.T)
    woT_pair = np.ascontiguousarray(woT[128 * r:128 * (r + 1), :])
    return wqT, wkT, wvT, woT_pair


def make_cmask():
    m = np.zeros((4, 128, 512), np.float32)
    for o in range(4):
        for p in range(128):
            m[o, p, o * 128 + p:] = 1.0
    return m


def build_stage_c():
    ctx = Ctx()
    nc = ctx.nc
    x_in = nc.dram_tensor("x_in", [D, TLOC], F32, kind="ExternalInput")
    x_out = nc.dram_tensor("x_out", [D, TLOC], F32, kind="ExternalOutput")
    ctx.wqT = nc.dram_tensor("wqT", [D, D], F32, kind="ExternalInput")
    ctx.wkT = nc.dram_tensor("wkT", [D, D], F32, kind="ExternalInput")
    ctx.wvT = nc.dram_tensor("wvT", [D, D], F32, kind="ExternalInput")
    ctx.woT_pair = nc.dram_tensor("woT_pair", [128, D], F32, kind="ExternalInput")
    ctx.cmask = nc.dram_tensor("cmask", [4, 128, 512], F32, kind="ExternalInput")

    x_sb = ctx.sbuf("x_sb", [128, NCH * TLOC], F32)
    xn_bf = ctx.sbuf("xn_bf", [128, NCH * TLOC], BF16)
    ctx.arena_init()
    lnsc = alloc_ln_scratch(ctx)

    dsem = ctx.sem()
    ph_load_x(ctx, x_in, x_sb, dsem)
    csem = ph_init_consts(ctx, lnsc)
    ln_done = ph_ln(ctx, lnsc, x_sb, xn_bf, deps=(dsem, csem))
    fin = ph_l1_attn(ctx, lnsc, x_sb, xn_bf, ln_done)
    ph_store_x(ctx, x_sb, x_out, (fin,))
    ctx.close()
    return ctx


def run_stage_c(x_parts, inp):
    cm = make_cmask()
    ctx = build_stage_c()
    in_maps = []
    for r in range(NCORES):
        wqT, wkT, wvT, woT_pair = stage_l1_attn_weights(inp, r)
        in_maps.append({"x_in": x_parts[r], "wqT": wqT, "wkT": wkT,
                        "wvT": wvT, "woT_pair": woT_pair, "cmask": cm})
    res = run_bass_kernel_spmd(ctx.nc, in_maps, core_ids=list(range(NCORES)))
    return [res.results[r]["x_out"] for r in range(NCORES)]




def stage_moe_weights(inp):
    w1 = np.asarray(inp["L1_ffn_w1"], np.float32)
    w2 = np.asarray(inp["L1_ffn_w2"], np.float32)
    moe_w1t = np.ascontiguousarray(
        np.concatenate([w1[e].T for e in range(8)], axis=0))
    moe_w2t = np.ascontiguousarray(
        np.concatenate([w2[e].T for e in range(8)], axis=0))
    b1row = np.ascontiguousarray(
        np.asarray(inp["L1_ffn_b1"], np.float32).reshape(1, 8 * 4096))
    b2row = np.ascontiguousarray(
        np.asarray(inp["L1_ffn_b2"], np.float32).reshape(8, 1024))
    rwT = np.ascontiguousarray(np.asarray(inp["L1_ffn_router_w"], np.float32).T)
    return moe_w1t, moe_w2t, b1row, b2row, rwT


def build_stage_d():
    ctx = Ctx()
    nc = ctx.nc
    x_in = nc.dram_tensor("x_in", [D, TLOC], F32, kind="ExternalInput")
    x_out = nc.dram_tensor("x_out", [D, TLOC], F32, kind="ExternalOutput")
    ctx.moe_w1t = nc.dram_tensor("moe_w1t", [8 * D, FMOE], F32, kind="ExternalInput")
    ctx.moe_w2t = nc.dram_tensor("moe_w2t", [8 * FMOE, D], F32, kind="ExternalInput")
    ctx.b1row = nc.dram_tensor("b1row", [1, 8 * FMOE], F32, kind="ExternalInput")
    ctx.b2row = nc.dram_tensor("b2row", [8, D], F32, kind="ExternalInput")
    ctx.rwT = nc.dram_tensor("rwT", [D, 8], F32, kind="ExternalInput")
    ctx.ident = nc.dram_tensor("ident", [128, 128], F32, kind="ExternalInput")

    x_sb = ctx.sbuf("x_sb", [128, NCH * TLOC], F32)
    xn_bf = ctx.sbuf("xn_bf", [128, NCH * TLOC], BF16)
    ctx.arena_init()
    lnsc = alloc_ln_scratch(ctx)

    dsem = ctx.sem()
    ph_load_x(ctx, x_in, x_sb, dsem)
    csem = ph_init_consts(ctx, lnsc)
    ln_done = ph_ln(ctx, lnsc, x_sb, xn_bf, deps=(dsem, csem))
    fin = ph_l1_moe(ctx, lnsc, x_sb, xn_bf, ln_done)
    ph_store_x(ctx, x_sb, x_out, (fin,))
    ctx.close()
    return ctx


def run_stage_d(x_parts, inp):
    ctx = build_stage_d()
    ident = np.eye(128, dtype=np.float32)
    in_maps = []
    for r in range(NCORES):
        w1t, w2t, b1r, b2r, rwT = stage_moe_weights(inp)
        in_maps.append({"x_in": x_parts[r], "moe_w1t": w1t, "moe_w2t": w2t,
                        "b1row": b1r, "b2row": b2r, "rwT": rwT, "ident": ident})
    res = run_bass_kernel_spmd(ctx.nc, in_maps, core_ids=list(range(NCORES)))
    return [res.results[r]["x_out"] for r in range(NCORES)]




def build_fused():
    ctx = Ctx()
    nc = ctx.nc
    x_in = nc.dram_tensor("x_in", [D, TLOC], F32, kind="ExternalInput")
    x_out = nc.dram_tensor("x_out", [D, TLOC], F32, kind="ExternalOutput")
    ctx.wq = nc.dram_tensor("wq", [D, D], F32, kind="ExternalInput")
    ctx.wk = nc.dram_tensor("wk", [D, D], F32, kind="ExternalInput")
    ctx.wv = nc.dram_tensor("wv", [D, D], F32, kind="ExternalInput")
    ctx.wo = nc.dram_tensor("wo", [D, D], F32, kind="ExternalInput")
    ctx.bo_gb = nc.dram_tensor("bo_gb", [D, 3], F32, kind="ExternalInput")
    ctx.arep_r = nc.dram_tensor("arep_r", [4, 512, 512], F32, kind="ExternalInput")
    ctx.brep_r = nc.dram_tensor("brep_r", [4, 512, 512], F32, kind="ExternalInput")
    ctx.arep_i = nc.dram_tensor("arep_i", [4, 512, 512], F32, kind="ExternalInput")
    ctx.brep_i = nc.dram_tensor("brep_i", [4, 512, 512], F32, kind="ExternalInput")
    ctx.bias_ri = nc.dram_tensor("bias_ri", [D, 1], F32, kind="ExternalInput")
    ctx.wqT = nc.dram_tensor("wqT", [D, D], F32, kind="ExternalInput")
    ctx.wkT = nc.dram_tensor("wkT", [D, D], F32, kind="ExternalInput")
    ctx.wvT = nc.dram_tensor("wvT", [D, D], F32, kind="ExternalInput")
    ctx.woT_pair = nc.dram_tensor("woT_pair", [128, D], F32, kind="ExternalInput")
    ctx.cmask = nc.dram_tensor("cmask", [4, 128, 512], F32, kind="ExternalInput")
    ctx.moe_w1t = nc.dram_tensor("moe_w1t", [8 * D, FMOE], F32, kind="ExternalInput")
    ctx.moe_w2t = nc.dram_tensor("moe_w2t", [8 * FMOE, D], F32, kind="ExternalInput")
    ctx.b1row = nc.dram_tensor("b1row", [1, 8 * FMOE], F32, kind="ExternalInput")
    ctx.b2row = nc.dram_tensor("b2row", [8, D], F32, kind="ExternalInput")
    ctx.rwT = nc.dram_tensor("rwT", [D, 8], F32, kind="ExternalInput")
    ctx.ident = nc.dram_tensor("ident", [128, 128], F32, kind="ExternalInput")

    x_sb = ctx.sbuf("x_sb", [128, NCH * TLOC], F32)
    xn_bf = ctx.sbuf("xn_bf", [128, NCH * TLOC], BF16)
    ctx.arena_init()
    lnsc = alloc_ln_scratch(ctx)

    dsem = ctx.sem()
    ph_load_x(ctx, x_in, x_sb, dsem)
    csem = ph_init_consts(ctx, lnsc)
    ln0 = ph_ln(ctx, lnsc, x_sb, xn_bf, deps=(dsem, csem))
    finA = ph_l0_attn(ctx, lnsc, x_sb, xn_bf, ln0)
    ln1 = ph_ln(ctx, lnsc, x_sb, xn_bf, deps=(finA,))
    finB = ph_l0_ffn(ctx, lnsc, x_sb, xn_bf, ln1)
    ln2 = ph_ln(ctx, lnsc, x_sb, xn_bf, deps=(finB,))
    finC = ph_l1_attn(ctx, lnsc, x_sb, xn_bf, ln2)
    ln3 = ph_ln(ctx, lnsc, x_sb, xn_bf, deps=(finC,))
    finD = ph_l1_moe(ctx, lnsc, x_sb, xn_bf, ln3)
    ph_store_x(ctx, x_sb, x_out, (finD,))
    ctx.close()
    return ctx


def kernel(**inputs):
    parts = shard_x(np.asarray(inputs["L0_attn_x"], np.float32))
    wq, wk, wv, wo, bo_gb = stage_l0_attn_weights(inputs)
    ar, br_, ai, bi_, bias = stage_ffn_weights(inputs)
    cm = make_cmask()
    ident = np.eye(128, dtype=np.float32)
    w1t, w2t, b1r, b2r, rwT = stage_moe_weights(inputs)
    ctx = build_fused()
    in_maps = []
    for r in range(NCORES):
        wqT, wkT, wvT, woT_pair = stage_l1_attn_weights(inputs, r)
        in_maps.append({
            "x_in": parts[r], "wq": wq, "wk": wk, "wv": wv, "wo": wo,
            "bo_gb": bo_gb, "arep_r": ar, "brep_r": br_, "arep_i": ai,
            "brep_i": bi_, "bias_ri": bias, "wqT": wqT, "wkT": wkT,
            "wvT": wvT, "woT_pair": woT_pair, "cmask": cm,
            "moe_w1t": w1t, "moe_w2t": w2t, "b1row": b1r, "b2row": b2r,
            "rwT": rwT, "ident": ident})
    res = run_bass_kernel_spmd(ctx.nc, in_maps, core_ids=list(range(NCORES)))
    return unshard_x([res.results[r]["x_out"] for r in range(NCORES)])


_OLD = '''unused'''

